# version 02
# baseline (speedup 1.0000x reference)
import numpy as np
import ml_dtypes
import concourse.bass as bass
import concourse.mybir as mybir
from concourse.bass_utils import run_bass_kernel_spmd

F32 = mybir.dt.float32
BF16 = mybir.dt.bfloat16
I32 = mybir.dt.int32
AF = mybir.ActivationFunctionType
ALU = mybir.AluOpType
AX = mybir.AxisListType

C0 = 16.0
C1 = 56.0
LN_EPS = 1e-5
ATT_EPS = 1e-6

NCORE = 8
T = 2048
D = 1024
I0 = 2816
S = 8192
NH = 16
HD = 64


def _nsplit(total, w):
    out = []
    off = 0
    while off < total:
        out.append((off, min(w, total - off)))
        off += w
    return out


class _P:

    def __init__(self):
        self.nc = bass.Bass("TRN2", target_bir_lowering=False, debug=False)
        self.steps = []
        self.out_names = []
        self._ctx = {}
        self._n = 0

    def dram_in(self, name, shape, dt):
        return self.nc.dram_tensor(name, list(shape), dt, kind="ExternalInput").ap()

    def dram_out(self, name, shape, dt):
        self.out_names.append(name)
        return self.nc.dram_tensor(name, list(shape), dt, kind="ExternalOutput").ap()

    def dram_tmp(self, name, shape, dt):
        return self.nc.dram_tensor(name, list(shape), dt, kind="Internal").ap()

    def sbuf(self, shape, dt, name=None):
        self._n += 1
        g = self.nc.sbuf_tensor(name or f"sb{self._n}", list(shape), dt)
        h = g.__enter__()
        self._ctx[id(h)] = g
        return h

    def psum(self, shape, dt, name=None):
        self._n += 1
        g = self.nc.psum_tensor(name or f"ps{self._n}", list(shape), dt)
        h = g.__enter__()
        self._ctx[id(h)] = g
        return h

    def free(self, *handles):
        for h in handles:
            g = self._ctx.pop(id(h))
            g.__exit__(None, None, None)

    def step(self, engine, fn, inc=1, nodrain=False):
        self.steps.append((engine, fn, inc, nodrain))

    DMA_REMAP = False

    def dma(self, engine, out, in_, **kw):
        if self.DMA_REMAP and engine == "gpsimd":
            engine = "sync"
        self.step(engine, lambda e: e.dma_start(out, in_, **kw), inc=16)

    SEG_CAP = 30000

    def finalize(self):
        nc = self.nc
        seg_of = []
        seg_base = [0]
        tot = 0
        for st in self.steps:
            inc = st[2]
            if tot - seg_base[-1] + inc > self.SEG_CAP:
                seg_base.append(tot)
            seg_of.append(len(seg_base) - 1)
            tot += inc
        seg_base.append(tot)
        nseg = len(seg_base) - 1
        sems = [nc.semaphore(f"stepsem{k}").__enter__() for k in range(nseg)]
        waits = []
        t = 0
        for st in self.steps:
            waits.append(t)
            t += st[2]
        blk = nc.Block().__enter__()
        by_eng = {}
        for i, st in enumerate(self.steps):
            by_eng.setdefault(st[0], []).append((i, st[1], st[2], st[3]))
        last_eng = self.steps[-1][0]
        for name in ("sync", "scalar", "vector", "tensor", "gpsimd"):
            if name not in by_eng and name != last_eng:
                continue
            items = by_eng.get(name, [])

            def make(items=items, name=name):
                def body(e):
                    cur = 0
                    for (i, fn, inc, nodrain) in items:
                        sg = seg_of[i]
                        while cur < sg:
                            e.wait_ge(sems[cur], seg_base[cur + 1] - seg_base[cur])
                            cur += 1
                        w = waits[i] - seg_base[sg]
                        if w > 0:
                            e.wait_ge(sems[sg], w)
                        if inc == 16 or name == "sync" or nodrain:
                            fn(e).then_inc(sems[sg], inc)
                        else:
                            fn(e)
                            e.drain().then_inc(sems[sg], inc)
                    if name == last_eng:
                        while cur < nseg - 1:
                            e.wait_ge(sems[cur], seg_base[cur + 1] - seg_base[cur])
                            cur += 1
                        e.wait_ge(sems[nseg - 1],
                                  seg_base[nseg] - seg_base[nseg - 1])
                return body
            getattr(blk, name)(make())
        blk.__exit__(None, None, None)
        for g in reversed(list(self._ctx.values())):
            g.__exit__(None, None, None)
        self._ctx.clear()
        return nc


def _ident(p, dt=BF16):
    p._n += 1
    t = p.sbuf([128, 128], dt, f"ident{p._n}")
    p.step("gpsimd", lambda e: e.memset(t[:], 0.0))
    p.step("gpsimd", lambda e: e.affine_select(
        out=t[:], in_=t[:], compare_op=ALU.not_equal, fill=1.0, base=0,
        pattern=[[-1, 128]], channel_multiplier=1))
    return t


def _transpose_tile(p, ident, dst, src_ap_fn, tp_ps, n_kchunk):
    for kc in range(n_kchunk):
        p.step("tensor", lambda e, kc=kc: e.transpose(tp_ps[:], src_ap_fn(kc), ident[:]))
        p.step("vector", lambda e, kc=kc: e.tensor_copy(dst(kc), tp_ps[:]))


class _Quant:

    def __init__(self, p, maxN):
        self.p = p
        self.W = [p.sbuf([128, maxN], F32, f"qW{i}") for i in range(5)]
        nb = maxN // 16
        self.bm = p.sbuf([128, nb], F32, "q_bm")
        self.scale = p.sbuf([128, nb], F32, "q_scale")
        self.lns = p.sbuf([128, nb], F32, "q_lns")
        self.invs = p.sbuf([128, nb], F32, "q_invs")

    def free(self):
        self.p.free(self.invs, self.lns, self.scale, self.bm,
                    *reversed(self.W))

    def emit(self, out_bf, x_f32, N):
        p = self.p
        nb = N // 16
        W1, W2, W3, W4, W5 = self.W
        bm, scale, lns, invs = self.bm, self.scale, self.lns, self.invs
        a3 = lambda t: t[:, 0:N].rearrange("p (b k) -> p b k", k=16)
        bcol = lambda t: t[:, 0:nb].rearrange("p (b k) -> p b k", k=1).broadcast_to(
            [128, nb, 16])

        def v1(e):
            return e.tensor_reduce(
                bm[:, 0:nb].rearrange("p (b k) -> p b k", k=1), a3(x_f32),
                AX.X, ALU.max, apply_absolute_value=True)
        p.step("vector", v1)
        p.step("vector", lambda e: e.tensor_scalar(
            scale[:, 0:nb], bm[:, 0:nb], 1.0 / 6.0, 1e-12, ALU.mult, ALU.max))

        def s1(e):
            e.activation(lns[:, 0:nb], scale[:, 0:nb], AF.Ln, bias=0.0, scale=1.0)
            e.activation(invs[:, 0:nb], lns[:, 0:nb], AF.Exp, bias=0.0, scale=-1.0)
            e.activation(W1[:, 0:N], x_f32, AF.Abs, bias=0.0, scale=1.0)
            return e.activation(W3[:, 0:N], x_f32, AF.Sign, bias=0.0, scale=1.0)
        p.step("scalar", s1)

        def v2(e):
            e.tensor_tensor(a3(W2), a3(W1), bcol(invs), ALU.mult)
            W4i = W4[:, 0:N].bitcast(I32)
            e.tensor_scalar(W4i, W2[:, 0:N], 2.0, None, ALU.mult)
            e.tensor_scalar(W1[:, 0:N], W4i, 0.5, 2.0, ALU.mult, ALU.min)
            e.tensor_scalar(W5[:, 0:N], W2[:, 0:N], 5.0, 2.0, ALU.is_gt, ALU.mult)
            e.tensor_scalar(W4i, W2[:, 0:N], 1.0, None, ALU.mult)
            e.tensor_scalar(W2[:, 0:N], W4i, 2.0, 0.0, ALU.subtract, ALU.max)
            e.scalar_tensor_tensor(W4[:, 0:N], W2[:, 0:N], 2.0, W1[:, 0:N],
                                   ALU.min, ALU.add)
            e.tensor_tensor(W1[:, 0:N], W4[:, 0:N], W5[:, 0:N], ALU.add)
            e.tensor_tensor(W2[:, 0:N], W1[:, 0:N], W3[:, 0:N], ALU.mult)
            return e.tensor_tensor(a3(out_bf), a3(W2), bcol(scale), ALU.mult)
        p.step("vector", v2)


def _ln_tile(p, xn_out, x_f32, cols, eps_col, scr_f32):
    ssum, nmean, sumsq, msq, var, lnv, rstd, nmr = cols
    p.step("vector", lambda e: e.tensor_reduce(ssum[:], x_f32, AX.X, ALU.add))
    p.step("scalar", lambda e: e.activation(
        scr_f32, x_f32, AF.Square, bias=0.0, scale=1.0))
    p.step("vector", lambda e: e.tensor_reduce(sumsq[:], scr_f32, AX.X, ALU.add))

    p.step("vector", lambda e: e.tensor_scalar(nmean[:], ssum[:], -1.0 / D,
                                               None, ALU.mult))
    p.step("scalar", lambda e: e.activation(msq[:], nmean[:], AF.Square,
                                            bias=0.0, scale=1.0))
    p.step("vector", lambda e: e.scalar_tensor_tensor(
        var[:], sumsq[:], 1.0 / D, msq[:], ALU.mult, ALU.subtract))

    def s2(e):
        e.activation(lnv[:], var[:], AF.Ln, bias=eps_col[:], scale=1.0)
        return e.activation(rstd[:], lnv[:], AF.Exp, bias=0.0, scale=-0.5)
    p.step("scalar", s2)
    p.step("vector", lambda e: e.tensor_tensor(nmr[:], nmean[:], rstd[:], ALU.mult))
    p.step("scalar", lambda e: e.activation(xn_out, x_f32, AF.Identity,
                                            bias=nmr[:], scale=rstd[:]))


def _mk_ln_cols(p):
    return tuple(p.sbuf([128, 1], F32, f"ln{i}") for i in range(8))


def build_l0_head():
    p = _P()
    x_d = p.dram_in("x", (T, D), F32)
    wq_d = p.dram_in("wq", (D, D), F32)
    wk_d = p.dram_in("wk", (D, D), F32)
    wv_d = p.dram_in("wv", (D, D), F32)
    wg_d = p.dram_in("wg", (I0, D), F32)
    wu_d = p.dram_in("wu", (I0, D), F32)
    wd_d = p.dram_in("wd", (D, I0), F32)
    qt_d = p.dram_out("QT", (D, T), BF16)
    kt_d = p.dram_out("KT", (D, T), BF16)
    v_d = p.dram_out("Vtok", (T, D), BF16)
    g_d = p.dram_out("g", (T, D), F32)
    hq_d = p.dram_tmp("hqT", (I0, T), BF16)

    ident = _ident(p)
    eps_col = p.sbuf([128, 1], F32, "lneps")
    p.step("vector", lambda e: e.memset(eps_col[:], LN_EPS))
    ln_cols = _mk_ln_cols(p)
    q = _Quant(p, I0)
    tp_ps = p.psum([128, 128], BF16, "tp")
    mm_ps = p.psum([128, 512], F32, "mm")

    ntile = T // 128
    xnqT = p.sbuf([128, 8 * T], BF16, "xnqT")
    xnT = p.sbuf([128, 8 * T], BF16, "xnT")
    TC = lambda buf, c: buf[:, c * T:(c + 1) * T]

    xf = p.sbuf([128, D], F32, "xf")
    xn = p.sbuf([128, D], F32, "xn")
    xnb = p.sbuf([128, D], BF16, "xnb")
    xnqb = p.sbuf([128, D], BF16, "xnqb")
    for t in range(ntile):
        p.dma("sync", xf[:], x_d[128 * t:128 * (t + 1), :])
        _ln_tile(p, xn[:], xf[:], ln_cols, eps_col, q.W[0][:, 0:D])
        p.step("vector", lambda e: e.tensor_copy(xnb[:], xn[:]))
        q.emit(xnqb[:], xn[:], D)
        _transpose_tile(p, ident,
                        lambda kc, t=t: TC(xnT, kc)[:, 128 * t:128 * (t + 1)],
                        lambda kc: xnb[:, 128 * kc:128 * (kc + 1)], tp_ps, 8)
        _transpose_tile(p, ident,
                        lambda kc, t=t: TC(xnqT, kc)[:, 128 * t:128 * (t + 1)],
                        lambda kc: xnqb[:, 128 * kc:128 * (kc + 1)], tp_ps, 8)
    p.free(xnqb, xnb, xn, xf)

    wrow = p.sbuf([128, D], F32, "wrow")
    wrb = p.sbuf([128, D], BF16, "wrb")
    wT = p.sbuf([128, 8 * D], BF16, "wT")
    WC = lambda c: wT[:, c * D:(c + 1) * D]
    outb = p.sbuf([128, T], BF16, "outb")
    for wsrc, wdst, scalef in ((wq_d, qt_d, 0.125), (wk_d, kt_d, 1.0)):
        for r in range(8):
            p.dma("sync", wrow[:], wsrc[128 * r:128 * (r + 1), :])
            p.step("vector", lambda e, s=scalef: e.tensor_scalar(
                wrb[:], wrow[:], s, None, ALU.mult))
            _transpose_tile(p, ident,
                            lambda kc, r=r: WC(kc)[:, 128 * r:128 * (r + 1)],
                            lambda kc: wrb[:, 128 * kc:128 * (kc + 1)], tp_ps, 8)
        for ec in range(8):
            for tg in range(T // 512):
                def mms(e, ec=ec, tg=tg):
                    for dc in range(8):
                        inst = e.matmul(mm_ps[:],
                                        WC(dc)[:, 128 * ec:128 * (ec + 1)],
                                        TC(xnT, dc)[:, 512 * tg:512 * (tg + 1)],
                                        start=(dc == 0), stop=(dc == 7))
                    return inst
                p.step("tensor", mms)
                p.step("vector", lambda e, tg=tg: e.tensor_copy(
                    outb[:, 512 * tg:512 * (tg + 1)], mm_ps[:]))
            p.dma("gpsimd", wdst[128 * ec:128 * (ec + 1), :], outb[:])
    for r in range(8):
        p.dma("sync", wrow[:], wv_d[128 * r:128 * (r + 1), :])
        p.step("vector", lambda e: e.tensor_copy(wrb[:], wrow[:]))
        _transpose_tile(p, ident,
                        lambda kc, r=r: WC(kc)[:, 128 * r:128 * (r + 1)],
                        lambda kc: wrb[:, 128 * kc:128 * (kc + 1)], tp_ps, 8)
    for t in range(ntile):
        for eg in range(2):
            def mmv(e, t=t, eg=eg):
                for dc in range(8):
                    inst = e.matmul(mm_ps[:],
                                    TC(xnT, dc)[:, 128 * t:128 * (t + 1)],
                                    WC(dc)[:, 512 * eg:512 * (eg + 1)],
                                    start=(dc == 0), stop=(dc == 7))
                return inst
            p.step("tensor", mmv)
            p.step("vector", lambda e, eg=eg: e.tensor_copy(
                outb[:, 512 * eg:512 * (eg + 1)], mm_ps[:]))
        p.dma("gpsimd", v_d[128 * t:128 * (t + 1), :], outb[:, 0:D])
    p.free(outb, wT, wrb, wrow, xnT)

    wgqT = p.sbuf([128, 8 * I0], BF16, "wgqT")
    wuqT = p.sbuf([128, 8 * I0], BF16, "wuqT")
    WGC = lambda buf, c: buf[:, c * I0:(c + 1) * I0]
    wrow2 = p.sbuf([128, D], F32, "wrow2")
    wqb = p.sbuf([128, D], BF16, "wqb")
    for wsrc, wbuf in ((wg_d, wgqT), (wu_d, wuqT)):
        for r in range(I0 // 128):
            p.dma("sync", wrow2[:], wsrc[128 * r:128 * (r + 1), :])
            q.emit(wqb[:], wrow2[:], D)
            _transpose_tile(p, ident,
                            lambda kc, r=r, wbuf=wbuf:
                            WGC(wbuf, kc)[:, 128 * r:128 * (r + 1)],
                            lambda kc: wqb[:, 128 * kc:128 * (kc + 1)], tp_ps, 8)
    gate_ps = p.psum([128, 512], F32, "gate")
    up_ps = p.psum([128, 512], F32, "up")
    sgate = q.W[1]
    hbuf = p.sbuf([128, I0], F32, "hbuf")
    hqb = p.sbuf([128, I0], BF16, "hqb")
    hpc = p.sbuf([128, 128], BF16, "hpc")
    for t in range(ntile):
        for off, w in _nsplit(I0, 512):
            def mmg(e, t=t, off=off, w=w):
                for dc in range(8):
                    e.matmul(gate_ps[:, 0:w],
                             TC(xnqT, dc)[:, 128 * t:128 * (t + 1)],
                             WGC(wgqT, dc)[:, off:off + w],
                             start=(dc == 0), stop=(dc == 7))
                for dc in range(8):
                    inst = e.matmul(up_ps[:, 0:w],
                                    TC(xnqT, dc)[:, 128 * t:128 * (t + 1)],
                                    WGC(wuqT, dc)[:, off:off + w],
                                    start=(dc == 0), stop=(dc == 7))
                return inst
            p.step("tensor", mmg)
            p.step("scalar", lambda e, off=off, w=w: e.activation(
                sgate[:, off:off + w], gate_ps[:, 0:w], AF.Silu, bias=0.0, scale=1.0))
            p.step("vector", lambda e, off=off, w=w: e.tensor_tensor(
                hbuf[:, off:off + w], sgate[:, off:off + w], up_ps[:, 0:w], ALU.mult))
        q.emit(hqb[:], hbuf[:], I0)
        for kc in range(I0 // 128):
            p.step("tensor", lambda e, kc=kc: e.transpose(
                tp_ps[:], hqb[:, 128 * kc:128 * (kc + 1)], ident[:]))
            p.step("vector", lambda e: e.tensor_copy(hpc[:], tp_ps[:]))
            p.dma("gpsimd",
                  hq_d[128 * kc:128 * (kc + 1), 128 * t:128 * (t + 1)], hpc[:])
    p.free(hpc, hqb, hbuf, wqb, wrow2, wuqT, wgqT, xnqT)

    wdqT = p.sbuf([128, 22 * D], BF16, "wdqT")
    wdrow = p.sbuf([128, I0], F32, "wdrow")
    wdqb = p.sbuf([128, I0], BF16, "wdqb")
    for r in range(8):
        p.dma("sync", wdrow[:], wd_d[128 * r:128 * (r + 1), :])
        q.emit(wdqb[:], wdrow[:], I0)
        for kc in range(I0 // 128):
            p.step("tensor", lambda e, kc=kc: e.transpose(
                tp_ps[:], wdqb[:, 128 * kc:128 * (kc + 1)], ident[:]))
            p.step("vector", lambda e, kc=kc, r=r: e.tensor_copy(
                wdqT[:, kc * D + 128 * r: kc * D + 128 * (r + 1)], tp_ps[:]))
    p.free(wdqb, wdrow)
    hqT_s = p.sbuf([128, 22 * T], BF16, "hqTs")
    for kc in range(I0 // 128):
        p.dma("sync", hqT_s[:, kc * T:(kc + 1) * T], hq_d[128 * kc:128 * (kc + 1), :])
    gout = p.sbuf([128, D], F32, "gout")
    for t in range(ntile):
        for eg in range(2):
            def mmd(e, t=t, eg=eg):
                for kc in range(22):
                    inst = e.matmul(mm_ps[:],
                                    hqT_s[:, kc * T + 128 * t: kc * T + 128 * (t + 1)],
                                    wdqT[:, kc * D + 512 * eg: kc * D + 512 * (eg + 1)],
                                    start=(kc == 0), stop=(kc == 21))
                return inst
            p.step("tensor", mmd)
            p.step("scalar", lambda e, eg=eg: e.activation(
                gout[:, 512 * eg:512 * (eg + 1)], mm_ps[:], AF.Silu,
                bias=0.0, scale=1.0))
        p.dma("gpsimd", g_d[128 * t:128 * (t + 1), :], gout[:])
    return p.finalize(), p.out_names


def build_l0_attn():
    p = _P()
    qt_d = p.dram_in("QT", (D, T), BF16)
    kt_d = p.dram_in("KTb", (D, S), BF16)
    v_d = p.dram_in("Vb", (S, D), BF16)
    ao_d = p.dram_out("AOT", (D, T), BF16)

    KSUP = 1024
    QG = 512
    nks = S // KSUP
    nqg = T // QG

    qt_s = p.sbuf([128, 8 * T], BF16, "qts")
    TC = lambda buf, c, w=T: buf[:, c * w:(c + 1) * w]
    for c in range(8):
        p.dma("sync", TC(qt_s, c), qt_d[128 * c:128 * (c + 1), :])
    oacc = p.sbuf([128, 8 * T], F32, "oacc")
    p.step("vector", lambda e: e.memset(oacc[:], 0.0))
    lacc = p.sbuf([1, T], F32, "lacc")
    p.step("vector", lambda e: e.memset(lacc[:], 0.0))
    ones_col = p.sbuf([128, 1], BF16, "ones")
    p.step("vector", lambda e: e.memset(ones_col[:], 1.0))
    nc0 = p.sbuf([128, 1], F32, "nc0")
    p.step("vector", lambda e: e.memset(nc0[:], -C0))

    kt_s = p.sbuf([128, 8 * KSUP], BF16, "kts")
    v_s = p.sbuf([128, 8 * D], BF16, "vs")
    pt_s = p.sbuf([128, 8 * QG], BF16, "pts")
    s_ps = p.psum([128, QG], F32, "sps")
    o_ps = p.psum([128, QG], F32, "ops")
    l_ps = p.psum([1, QG], F32, "lps")

    for ks in range(nks):
        for c in range(8):
            p.dma("sync", TC(kt_s, c, KSUP),
                  kt_d[128 * c:128 * (c + 1), KSUP * ks:KSUP * (ks + 1)])
        for kb in range(8):
            p.dma("sync", TC(v_s, kb, D),
                  v_d[KSUP * ks + 128 * kb:KSUP * ks + 128 * (kb + 1), :])
        for qg in range(nqg):
            for kb in range(8):
                def mmS(e, kb=kb, qg=qg):
                    for dc in range(8):
                        inst = e.matmul(
                            s_ps[:],
                            TC(kt_s, dc, KSUP)[:, 128 * kb:128 * (kb + 1)],
                            TC(qt_s, dc)[:, QG * qg:QG * (qg + 1)],
                            start=(dc == 0), stop=(dc == 7))
                    return inst
                p.step("tensor", mmS)
                p.step("scalar", lambda e, kb=kb: e.activation(
                    TC(pt_s, kb, QG), s_ps[:], AF.Exp, bias=nc0[:], scale=1.0))
            def mmL(e):
                for kb in range(8):
                    inst = e.matmul(l_ps[:], ones_col[:],
                                    TC(pt_s, kb, QG),
                                    start=(kb == 0), stop=(kb == 7))
                return inst
            p.step("tensor", mmL)
            p.step("vector", lambda e, qg=qg: e.tensor_tensor(
                lacc[:, QG * qg:QG * (qg + 1)], lacc[:, QG * qg:QG * (qg + 1)],
                l_ps[:], ALU.add))
            for ec in range(8):
                def mmPV(e, ec=ec):
                    for kb in range(8):
                        inst = e.matmul(o_ps[:],
                                        TC(v_s, kb, D)[:, 128 * ec:128 * (ec + 1)],
                                        TC(pt_s, kb, QG),
                                        start=(kb == 0), stop=(kb == 7))
                    return inst
                p.step("tensor", mmPV)
                p.step("vector", lambda e, ec=ec, qg=qg: e.tensor_tensor(
                    TC(oacc, ec)[:, QG * qg:QG * (qg + 1)],
                    TC(oacc, ec)[:, QG * qg:QG * (qg + 1)], o_ps[:], ALU.add))

    rl = p.sbuf([1, T], F32, "rl")
    p64c = p.sbuf([1, 1], F32, "p64c")
    p.step("vector", lambda e: e.memset(p64c[:], 44.3614195558365))
    def srl(e):
        e.activation(rl[:], lacc[:], AF.Ln, bias=0.0,
                     scale=18446744073709551616.0)
        return e.activation(rl[:], rl[:], AF.Exp, bias=p64c[0:1, :], scale=-1.0)
    p.step("scalar", srl)
    rlb_ps = p.psum([128, QG], F32, "rlb")
    rlb = p.sbuf([128, T], F32, "rlbs")
    ones1 = p.sbuf([1, 128], F32, "ones1")
    p.step("vector", lambda e: e.memset(ones1[:], 1.0))
    for qg in range(nqg):
        p.step("tensor", lambda e, qg=qg: e.matmul(
            rlb_ps[:], ones1[:], rl[:, QG * qg:QG * (qg + 1)],
            start=True, stop=True))
        p.step("vector", lambda e, qg=qg: e.tensor_copy(
            rlb[:, QG * qg:QG * (qg + 1)], rlb_ps[:]))
    aob = p.sbuf([128, T], BF16, "aob")
    for ec in range(8):
        p.step("vector", lambda e, ec=ec: e.tensor_tensor(
            aob[:], TC(oacc, ec), rlb[:], ALU.mult))
        p.dma("gpsimd", ao_d[128 * ec:128 * (ec + 1), :], aob[:])
    return p.finalize(), p.out_names


def build_l0_tail_l1_head():
    p = _P()
    spt_d = p.dram_in("SPT", (D, T), BF16)
    xt_d = p.dram_in("xT", (D, T), F32)
    gt_d = p.dram_in("gT", (D, T), F32)
    wo_d = p.dram_in("wo", (D, D), F32)
    wqkv_d = p.dram_in("wqkv", (D, 3 * D), F32)
    bqkv_d = p.dram_in("bqkv", (3 * D,), F32)
    wg_d = p.dram_in("Wg", (D, 8), F32)
    bg_d = p.dram_in("bg", (1, 8), F32)
    w1_d = p.dram_in("W1", (8, D, D), F32)
    b1_d = p.dram_in("b1", (8, D), F32)
    w2_d = p.dram_in("W2", (8, D, D), F32)
    b2_d = p.dram_in("b2", (8, D), F32)
    x1_d = p.dram_out("x1", (T, D), F32)
    g1_d = p.dram_out("g1", (T, D), F32)
    q1_d = p.dram_out("Q1T", (D, T), BF16)
    k1_d = p.dram_out("K1T", (D, T), BF16)
    v1_d = p.dram_out("V1tok", (T, D), BF16)

    ident = _ident(p)
    identf = _ident(p, F32)
    ntile = T // 128
    TC = lambda buf, c, w=T: buf[:, c * w:(c + 1) * w]
    tp_ps = p.psum([128, 128], BF16, "tp")
    tpf_ps = p.psum([128, 128], F32, "tpf")
    mm_ps = p.psum([128, 512], F32, "mm")

    eps_col = p.sbuf([128, 1], F32, "lneps")
    p.step("vector", lambda e: e.memset(eps_col[:], LN_EPS))
    ln_cols = _mk_ln_cols(p)
    ntile_ = T // 128
    selT = p.sbuf([128, 8 * ntile_], F32, "selT")
    xn1T = p.sbuf([128, 8 * T], BF16, "xn1T")
    xn1loT = p.sbuf([128, 8 * T], BF16, "xn1loT")

    x1T = p.sbuf([128, 8 * T], F32, "x1T")
    spt_s = p.sbuf([128, 8 * T], BF16, "spts")
    for c in range(8):
        p.dma("sync", TC(spt_s, c), spt_d[128 * c:128 * (c + 1), :])
    woT = p.sbuf([128, 8 * D], BF16, "woT")
    wrow = p.sbuf([128, D], F32, "wrow_a")
    wrb = p.sbuf([128, D], BF16, "wrb_a")
    for r in range(8):
        p.dma("sync", wrow[:], wo_d[128 * r:128 * (r + 1), :])
        p.step("vector", lambda e: e.tensor_copy(wrb[:], wrow[:]))
        _transpose_tile(p, ident,
                        lambda kc, r=r: TC(woT, kc, D)[:, 128 * r:128 * (r + 1)],
                        lambda kc: wrb[:, 128 * kc:128 * (kc + 1)], tp_ps, 8)
    xg_x = p.sbuf([128, T], F32, "xgx")
    xg_g = p.sbuf([128, T], F32, "xgg")
    stmp = p.sbuf([128, 512], F32, "stmp")
    for ec in range(8):
        p.dma("sync", xg_x[:], xt_d[128 * ec:128 * (ec + 1), :])
        p.dma("sync", xg_g[:], gt_d[128 * ec:128 * (ec + 1), :])
        for tg in range(T // 512):
            def mmS(e, ec=ec, tg=tg):
                for jc in range(8):
                    inst = e.matmul(mm_ps[:],
                                    TC(woT, jc, D)[:, 128 * ec:128 * (ec + 1)],
                                    TC(spt_s, jc)[:, 512 * tg:512 * (tg + 1)],
                                    start=(jc == 0), stop=(jc == 7))
                return inst
            p.step("tensor", mmS)
            p.step("vector", lambda e, tg=tg: e.scalar_tensor_tensor(
                stmp[:], mm_ps[:], 1.0, xg_g[:, 512 * tg:512 * (tg + 1)],
                ALU.mult, ALU.mult))
            p.step("vector", lambda e, ec=ec, tg=tg: e.tensor_tensor(
                TC(x1T, ec)[:, 512 * tg:512 * (tg + 1)], stmp[:],
                xg_x[:, 512 * tg:512 * (tg + 1)], ALU.add))
    p.free(stmp, xg_g, xg_x, wrb, wrow, woT, spt_s)

    x1t_f = p.sbuf([128, D], F32, "x1tf")
    xn1_f = p.sbuf([128, D], F32, "xn1f")
    scrf = p.sbuf([128, D], F32, "scrf")
    xn1b = p.sbuf([128, D], BF16, "xn1b")
    xn1lo = p.sbuf([128, D], BF16, "xn1lo")
    for t in range(ntile):
        for kc in range(8):
            p.step("tensor", lambda e, kc=kc, t=t: e.transpose(
                tpf_ps[:], TC(x1T, kc)[:, 128 * t:128 * (t + 1)], identf[:]))
            p.step("vector", lambda e, kc=kc: e.tensor_copy(
                x1t_f[:, 128 * kc:128 * (kc + 1)], tpf_ps[:]))
        p.dma("gpsimd", x1_d[128 * t:128 * (t + 1), :], x1t_f[:])
        _ln_tile(p, xn1_f[:], x1t_f[:], ln_cols, eps_col, scrf[:])
        p.step("vector", lambda e: e.tensor_copy(xn1b[:], xn1_f[:]))
        p.step("vector", lambda e: e.tensor_sub(xn1lo[:], xn1_f[:], xn1b[:]))
        _transpose_tile(p, ident,
                        lambda kc, t=t: TC(xn1T, kc)[:, 128 * t:128 * (t + 1)],
                        lambda kc: xn1b[:, 128 * kc:128 * (kc + 1)], tp_ps, 8)
        _transpose_tile(p, ident,
                        lambda kc, t=t: TC(xn1loT, kc)[:, 128 * t:128 * (t + 1)],
                        lambda kc: xn1lo[:, 128 * kc:128 * (kc + 1)], tp_ps, 8)
    p.free(xn1lo, xn1b, scrf, xn1_f, x1t_f)
    p.free(x1T)

    wqkv_s = p.sbuf([128, 8 * 3 * D], BF16, "wqkvs")
    W3C = lambda c: wqkv_s[:, c * 3 * D:(c + 1) * 3 * D]
    wrow3 = p.sbuf([128, 3 * D], F32, "wrow3")
    for r in range(8):
        p.dma("sync", wrow3[:], wqkv_d[128 * r:128 * (r + 1), :])
        def cv(e, r=r):
            e.tensor_copy(W3C(r)[:, 0:2 * D], wrow3[:, 0:2 * D])
            return e.tensor_scalar(W3C(r)[:, D:2 * D], wrow3[:, D:2 * D],
                                   -2.0, None, ALU.mult)
        def cv2(e, r=r):
            return e.tensor_copy(W3C(r)[:, 2 * D:3 * D], wrow3[:, 2 * D:3 * D])
        p.step("vector", cv)
        p.step("vector", cv2)
    bq_s = p.sbuf([128, 16], F32, "bqs")
    for ec in range(8):
        p.dma("sync", bq_s[:, ec:ec + 1], bqkv_d[128 * ec:128 * (ec + 1)].unsqueeze(1))
    for ec in range(8):
        p.dma("sync", bq_s[:, 8 + ec:9 + ec],
              bqkv_d[D + 128 * ec:D + 128 * (ec + 1)].unsqueeze(1))
    p.step("vector", lambda e: e.tensor_scalar(bq_s[:, 8:16], bq_s[:, 8:16],
                                               -2.0, None, ALU.mult))
    outqb = p.sbuf([128, T], BF16, "outqb")
    for (dst, coff, bcol_off) in ((q1_d, 0, 0), (k1_d, D, 8)):
        for ec in range(8):
            for tg in range(T // 512):
                def mmq(e, ec=ec, tg=tg, coff=coff):
                    for dc in range(8):
                        inst = e.matmul(
                            mm_ps[:],
                            W3C(dc)[:, coff + 128 * ec:coff + 128 * (ec + 1)],
                            TC(xn1T, dc)[:, 512 * tg:512 * (tg + 1)],
                            start=(dc == 0), stop=(dc == 7))
                    return inst
                p.step("tensor", mmq)
                p.step("scalar", lambda e, tg=tg, ec=ec, bo=bcol_off: e.activation(
                    outqb[:, 512 * tg:512 * (tg + 1)], mm_ps[:], AF.Identity,
                    bias=bq_s[:, bo + ec:bo + ec + 1], scale=1.0))
            p.dma("gpsimd", dst[128 * ec:128 * (ec + 1), :], outqb[:])
    onesrow = p.sbuf([1, 128], BF16, "onesr")
    p.step("vector", lambda e: e.memset(onesrow[:], 1.0))
    bvrow = p.sbuf([1, D], F32, "bvrow")
    p.dma("sync", bvrow[:], bqkv_d[2 * D:3 * D].unsqueeze(0))
    bvrow_b = p.sbuf([1, D], BF16, "bvrowb")
    p.step("vector", lambda e: e.tensor_copy(bvrow_b[:], bvrow[:]))
    for t in range(ntile):
        for eg in range(2):
            def mmv(e, t=t, eg=eg):
                for dc in range(8):
                    e.matmul(mm_ps[:],
                             TC(xn1T, dc)[:, 128 * t:128 * (t + 1)],
                             W3C(dc)[:, 2 * D + 512 * eg:2 * D + 512 * (eg + 1)],
                             start=(dc == 0), stop=False)
                return e.matmul(mm_ps[:], onesrow[:],
                                bvrow_b[:, 512 * eg:512 * (eg + 1)],
                                start=False, stop=True)
            p.step("tensor", mmv)
            p.step("vector", lambda e, eg=eg: e.tensor_copy(
                outqb[:, 512 * eg:512 * (eg + 1)], mm_ps[:]))
        p.dma("gpsimd", v1_d[128 * t:128 * (t + 1), :], outqb[:, 0:D])
    p.free(bvrow_b, bvrow, onesrow, outqb, bq_s, wrow3, wqkv_s)

    wg_s = p.sbuf([128, 8 * 8], F32, "wgs")
    for dc in range(8):
        p.dma("sync", wg_s[:, 8 * dc:8 * (dc + 1)], wg_d[128 * dc:128 * (dc + 1), :])
    wg_hi = p.sbuf([128, 8 * 8], BF16, "wghi")
    wg_lo = p.sbuf([128, 8 * 8], BF16, "wglo")
    p.step("vector", lambda e: e.tensor_copy(wg_hi[:], wg_s[:]))
    p.step("vector", lambda e: e.tensor_sub(wg_lo[:], wg_s[:], wg_hi[:]))
    lg_ps = p.psum([8, 512], F32, "lgps")
    lgT = p.sbuf([8, T], F32, "lgT")
    for tg in range(T // 512):
        def mmlg(e, tg=tg):
            combos = []
            for dc in range(8):
                combos.append((wg_hi[:, 8 * dc:8 * (dc + 1)], TC(xn1T, dc)))
                combos.append((wg_lo[:, 8 * dc:8 * (dc + 1)], TC(xn1T, dc)))
                combos.append((wg_hi[:, 8 * dc:8 * (dc + 1)], TC(xn1loT, dc)))
            for i, (wb, xb) in enumerate(combos):
                inst = e.matmul(lg_ps[:], wb, xb[:, 512 * tg:512 * (tg + 1)],
                                start=(i == 0), stop=(i == len(combos) - 1))
            return inst
        p.step("tensor", mmlg)
        p.step("vector", lambda e, tg=tg: e.tensor_copy(
            lgT[:, 512 * tg:512 * (tg + 1)], lg_ps[:]))
    bg_s = p.sbuf([1, 8], F32, "bgs")
    p.dma("sync", bg_s[:], bg_d[:])
    ones1f = p.sbuf([1, 128], F32, "ones1f")
    p.step("vector", lambda e: e.memset(ones1f[:], 1.0))
    bgb_ps = p.psum([128, 8], F32, "bgb")
    bgb = p.sbuf([128, 8], F32, "bgbs")
    p.step("tensor", lambda e: e.matmul(bgb_ps[:], ones1f[:], bg_s[:],
                                        start=True, stop=True))
    p.step("vector", lambda e: e.tensor_copy(bgb[:], bgb_ps[:]))
    lgt_f = p.sbuf([128, 8], F32, "lgtf")
    mcol = p.sbuf([128, 1], F32, "mcol")
    for t in range(ntile):
        p.step("tensor", lambda e, t=t: e.transpose(
            tpf_ps[:, 0:8], lgT[:, 128 * t:128 * (t + 1)], identf[0:8, 0:8]))
        p.step("vector", lambda e: e.tensor_copy(lgt_f[:], tpf_ps[:, 0:8]))
        p.step("vector", lambda e: e.tensor_tensor(lgt_f[:], lgt_f[:], bgb[:], ALU.add))
        p.step("vector", lambda e: e.tensor_reduce(mcol[:], lgt_f[:], AX.X, ALU.max))
        p.step("vector", lambda e, t=t: e.tensor_scalar(
            selT[:, 8 * t:8 * (t + 1)], lgt_f[:], mcol[:], None, ALU.is_ge))
    p.free(mcol, lgt_f, bgb, ones1f, bg_s, lgT, wg_lo, wg_hi, wg_s)
    p.free(xn1loT)

    yacc = p.sbuf([128, ntile * D], F32, "yacc")
    p.step("vector", lambda e: e.memset(yacc[:], 0.0))
    heT = p.sbuf([128, 8 * T], BF16, "heT")
    w1b = p.sbuf([128, 8 * D], BF16, "w1b")
    w2b = p.sbuf([128, 8 * D], BF16, "w2b")
    wrowm = p.sbuf([128, D], F32, "wrowm")
    b1c = p.sbuf([128, 8], F32, "b1c")
    b2row = p.sbuf([1, D], F32, "b2row")
    b2row_b = p.sbuf([1, D], BF16, "b2rowb")
    onesrow2 = p.sbuf([1, 128], BF16, "onesr2")
    p.step("vector", lambda e: e.memset(onesrow2[:], 1.0))
    ytmp = p.sbuf([128, 512], F32, "ytmp")
    for ex in range(8):
        for r in range(8):
            p.dma("sync", wrowm[:], w1_d[ex, 128 * r:128 * (r + 1), :])
            p.step("vector", lambda e, r=r: e.tensor_copy(
                w1b[:, D * r:D * (r + 1)], wrowm[:]))
        for r in range(8):
            p.dma("sync", wrowm[:], w2_d[ex, 128 * r:128 * (r + 1), :])
            p.step("vector", lambda e, r=r: e.tensor_copy(
                w2b[:, D * r:D * (r + 1)], wrowm[:]))
        for ic in range(8):
            p.dma("sync", b1c[:, ic:ic + 1],
                  b1_d[ex, 128 * ic:128 * (ic + 1)].unsqueeze(1))
        p.dma("sync", b2row[:], b2_d[ex:ex + 1, :])
        p.step("vector", lambda e: e.tensor_copy(b2row_b[:], b2row[:]))
        for ic in range(8):
            for tg in range(T // 512):
                def mmh(e, ic=ic, tg=tg):
                    for dc in range(8):
                        inst = e.matmul(
                            mm_ps[:],
                            w1b[:, D * dc + 128 * ic:D * dc + 128 * (ic + 1)],
                            TC(xn1T, dc)[:, 512 * tg:512 * (tg + 1)],
                            start=(dc == 0), stop=(dc == 7))
                    return inst
                p.step("tensor", mmh)
                p.step("scalar", lambda e, ic=ic, tg=tg: e.activation(
                    TC(heT, ic)[:, 512 * tg:512 * (tg + 1)], mm_ps[:],
                    AF.Gelu, bias=b1c[:, ic:ic + 1], scale=1.0))
        for t in range(ntile):
            for eg in range(2):
                def mmy(e, t=t, eg=eg):
                    for ic in range(8):
                        e.matmul(mm_ps[:],
                                 TC(heT, ic)[:, 128 * t:128 * (t + 1)],
                                 w2b[:, D * ic + 512 * eg:D * ic + 512 * (eg + 1)],
                                 start=(ic == 0), stop=False)
                    return e.matmul(mm_ps[:], onesrow2[:],
                                    b2row_b[:, 512 * eg:512 * (eg + 1)],
                                    start=False, stop=True)
                p.step("tensor", mmy)
                p.step("vector", lambda e, t=t, eg=eg, ex=ex: e.tensor_scalar(
                    ytmp[:], mm_ps[:], selT[:, 8 * t + ex:8 * t + ex + 1],
                    None, ALU.mult))
                p.step("vector", lambda e, t=t, eg=eg: e.tensor_tensor(
                    yacc[:, D * t + 512 * eg:D * t + 512 * (eg + 1)],
                    yacc[:, D * t + 512 * eg:D * t + 512 * (eg + 1)],
                    ytmp[:], ALU.add))
    g1b = p.sbuf([128, D], F32, "g1b")
    for t in range(ntile):
        p.step("scalar", lambda e, t=t: e.activation(
            g1b[:], yacc[:, D * t:D * (t + 1)], AF.Silu, bias=0.0, scale=1.0))
        p.dma("gpsimd", g1_d[128 * t:128 * (t + 1), :], g1b[:])
    return p.finalize(), p.out_names


_DBG = False


def build_l1_attn(nheads=4):
    p = _P()
    HB = nheads * 64
    qh_d = p.dram_in("QHT", (HB, T), BF16)
    kh_d = p.dram_in("KHT", (HB, S), BF16)
    v_d = p.dram_in("Vb", (S, HB), BF16)
    qt_d = p.dram_in("qtrueb", (128, T), F32)
    selc_d = p.dram_in("selc", (128, 4), F32)
    eyeb_d = p.dram_in("eyeb", (128, 128), BF16)
    eyef_d = p.dram_in("eyef", (128, 128), F32)
    taus_d = p.dram_in("tausin", (128, S // 128), F32)
    ao_d = p.dram_out("AOTp", (HB, T), BF16)
    khscr = p.dram_tmp("khscr", (nheads, 66, S), BF16)
    dbg = p.dram_out("DBG", (66, 512), F32) if _DBG else None
    dbg2 = p.dram_out("DBG2", (128, 1024), F32) if _DBG else None

    QG = 512
    nqg = T // QG
    nkb = S // 128

    identf = _ident(p, F32)
    ones64f = p.sbuf([64, 1], F32, "o64")
    p.step("vector", lambda e: e.memset(ones64f[:], 1.0))
    ones1f = p.sbuf([1, 64], F32, "o1f")
    p.step("vector", lambda e: e.memset(ones1f[:], 1.0))
    nc1 = p.sbuf([128, 1], F32, "nc1")
    p.step("vector", lambda e: e.memset(nc1[:], -C1))
    qtrue = p.sbuf([128, T], F32, "qtrue")
    p.dma("sync", qtrue[:], qt_d[:])
    taus = p.sbuf([128, nkb], F32, "taus")
    p.dma("sync", taus[:], taus_d[:])

    QHs = [p.sbuf([66, T], BF16, f"QH{h}") for h in range(nheads)]
    onesT = p.sbuf([1, T], BF16, "onesT")
    p.step("vector", lambda e: e.memset(onesT[:], 1.0))
    qsq = p.sbuf([64, T], F32, "qsq")
    qn_ps = p.psum([1, QG], F32, "qnps")
    for h in range(nheads):
        p.dma("sync", QHs[h][0:64, :], qh_d[64 * h:64 * (h + 1), :])
        p.step("scalar", lambda e, h=h: e.activation(
            qsq[:], QHs[h][0:64, :], AF.Square, bias=0.0, scale=1.0))
        p.dma("gpsimd", QHs[h][65:66, :], onesT[:, 0:T])
        for tg in range(nqg):
            p.step("tensor", lambda e, tg=tg: e.matmul(
                qn_ps[:], ones64f[:], qsq[:, QG * tg:QG * (tg + 1)],
                start=True, stop=True))
            p.step("vector", lambda e, h=h, tg=tg: e.tensor_copy(
                QHs[h][64:65, QG * tg:QG * (tg + 1)], qn_ps[:]))
    p.free(qsq)

    KH = p.sbuf([66, S], BF16, "KH")
    ksq = p.sbuf([64, 512], F32, "ksq")
    kn_ps = qn_ps
    knrow = p.sbuf([1, 512], F32, "knrow")
    knrow_b = p.sbuf([1, S], BF16, "knrowb")
    maskb = p.sbuf([128, nkb * 512], BF16, "maskb")
    mtmp = p.sbuf([128, 512], F32, "mtmp")
    vbuf = p.sbuf([128, 65], BF16, "vbuf")
    p.step("vector", lambda e: e.memset(vbuf[:, 64:65], 1.0))
    den_ps = p.psum([128, 1024], F32, "denps")
    dot_ps = p.psum([128, 1024], F32, "dotps")
    o_ps = p.psum([65, QG], F32, "ops")
    rlb_ps = p.psum([64, QG], F32, "rlbps")
    bln = p.sbuf([128, 1024], F32, "bln")
    rln = p.sbuf([128, 1024], F32, "rln")
    m1 = p.sbuf([128, 1024], F32, "m1")
    uu = p.sbuf([128, 1024], F32, "uu")
    pt = p.sbuf([128, 1024], BF16, "pt")
    ohl = p.sbuf([65, QG], F32, "ohl")
    dbg2s = p.sbuf([128, 1024], F32, "dbg2s") if _DBG else None
    rl = p.sbuf([1, QG], F32, "rl")
    p64c = p.sbuf([1, 1], F32, "p64c")
    p.step("vector", lambda e: e.memset(p64c[:], 44.3614195558365))
    aob = p.sbuf([64, QG], BF16, "aob")

    for qg in range(nqg):
        for kb in range(nkb):
            p.step("vector", lambda e, kb=kb, qg=qg: e.tensor_scalar(
                mtmp[:], qtrue[:, QG * qg:QG * (qg + 1)], taus[:, kb:kb + 1],
                None, ALU.is_ge))
            p.step("vector", lambda e, kb=kb: e.tensor_copy(
                maskb[:, 512 * kb:512 * (kb + 1)], mtmp[:]))
        for h in range(nheads):
            if qg == 0:
                p.dma("sync", KH[0:64, :], kh_d[64 * h:64 * (h + 1), :])
                p.step("vector", lambda e: e.memset(KH[64:65, :], 1.0))
                for sg in range(S // 512):
                    p.step("scalar", lambda e, sg=sg: e.activation(
                        ksq[:], KH[0:64, 512 * sg:512 * (sg + 1)], AF.Square,
                        bias=0.0, scale=0.5))
                    p.step("tensor", lambda e: e.matmul(
                        kn_ps[:], ones64f[:], ksq[:], start=True, stop=True))
                    p.step("vector", lambda e: e.tensor_scalar(
                        knrow[:], kn_ps[:], ATT_EPS, None, ALU.add))
                    p.step("vector", lambda e, sg=sg: e.tensor_copy(
                        knrow_b[:, 512 * sg:512 * (sg + 1)], knrow[:]))
                p.dma("gpsimd", KH[65:66, :], knrow_b[:])
                p.dma("gpsimd", khscr[h], KH[:])
            else:
                p.dma("sync", KH[:], khscr[h])
            for kbp in range(nkb // 2):
                def mms(e, kbp=kbp, qg=qg, h=h):
                    for half in range(2):
                        kb = 2 * kbp + half
                        e.matmul(den_ps[:, 512 * half:512 * (half + 1)],
                                 KH[0:66, 128 * kb:128 * (kb + 1)],
                                 QHs[h][0:66, QG * qg:QG * (qg + 1)],
                                 start=True, stop=True)
                        inst = e.matmul(dot_ps[:, 512 * half:512 * (half + 1)],
                                        KH[0:64, 128 * kb:128 * (kb + 1)],
                                        QHs[h][0:64, QG * qg:QG * (qg + 1)],
                                        start=True, stop=True)
                    return inst
                p.step("tensor", mms)

                def sln(e):
                    e.activation(bln[:], den_ps[:], AF.Ln, bias=0.0, scale=1.0)
                    return e.activation(rln[:], bln[:], AF.Exp, bias=0.0, scale=-1.0)
                p.step("scalar", sln)

                def vm(e):
                    e.tensor_tensor(m1[:], dot_ps[:], rln[:], ALU.mult)
                    return e.tensor_tensor(uu[:], m1[:], dot_ps[:], ALU.mult)
                p.step("vector", vm)
                p.step("scalar", lambda e: e.activation(
                    pt[:], uu[:], AF.Exp, bias=nc1[:], scale=0.25))
                p.step("vector", lambda e, kbp=kbp: e.tensor_tensor(
                    pt[:], pt[:], maskb[:, 1024 * kbp:1024 * (kbp + 1)], ALU.mult))
                if _DBG and h == 0 and qg == 0 and kbp == 0:
                    p.step("vector", lambda e: e.tensor_copy(dbg2s[:], pt[:]))
                    p.dma("gpsimd", dbg2[:], dbg2s[:])
                for half in range(2):
                    kb = 2 * kbp + half
                    p.dma("sync", vbuf[:, 0:64],
                          v_d[128 * kb:128 * (kb + 1), 64 * h:64 * (h + 1)])
                    p.step("tensor", lambda e, kb=kb, half=half: e.matmul(
                        o_ps[:], vbuf[:], pt[:, 512 * half:512 * (half + 1)],
                        start=(kb == 0), stop=(kb == nkb - 1)))
            p.step("vector", lambda e: e.tensor_copy(ohl[:], o_ps[:]))
            if _DBG and h == 0 and qg == 0:
                p.dma("gpsimd", dbg[0:65, :], ohl[:])
            def srl(e):
                e.activation(rl[:], ohl[64:65, :], AF.Ln, bias=0.0,
                             scale=18446744073709551616.0)
                return e.activation(rl[:], rl[:], AF.Exp, bias=p64c[0:1, :],
                                    scale=-1.0)
            p.step("scalar", srl)
            p.step("tensor", lambda e: e.matmul(rlb_ps[:], ones1f[:], rl[:],
                                                start=True, stop=True))
            p.step("vector", lambda e: e.tensor_tensor(
                aob[:], ohl[0:64, :], rlb_ps[:], ALU.mult))
            p.dma("gpsimd", ao_d[64 * h:64 * (h + 1), QG * qg:QG * (qg + 1)],
                  aob[:])
    return p.finalize(), p.out_names


def build_l1_tail():
    p = _P()
    ao_d = p.dram_in("AOT", (D, T), BF16)
    wo_d = p.dram_in("w_out", (D, D), F32)
    bo_d = p.dram_in("b_out", (D,), F32)
    x1_d = p.dram_in("x1", (T, D), F32)
    g1_d = p.dram_in("g1", (T, D), F32)
    out_d = p.dram_out("out", (T, D), F32)

    identf = _ident(p, F32)
    ntile = T // 128
    TC = lambda buf, c, w=T: buf[:, c * w:(c + 1) * w]
    mm_ps = p.psum([128, 512], F32, "mm")
    tpf_ps = p.psum([128, 128], F32, "tpf")

    ao_s = p.sbuf([128, 8 * T], BF16, "aos")
    for c in range(8):
        p.dma("sync", TC(ao_s, c), ao_d[128 * c:128 * (c + 1), :])
    w_s = p.sbuf([128, 8 * D], BF16, "ws")
    wrow = p.sbuf([128, D], F32, "wr")
    for r in range(8):
        p.dma("sync", wrow[:], wo_d[128 * r:128 * (r + 1), :])
        p.step("vector", lambda e, r=r: e.tensor_copy(w_s[:, D * r:D * (r + 1)],
                                                      wrow[:]))
    bo_s = p.sbuf([128, 8], F32, "bos")
    for ec in range(8):
        p.dma("sync", bo_s[:, ec:ec + 1], bo_d[128 * ec:128 * (ec + 1)].unsqueeze(1))
    s1T = p.sbuf([128, 8 * T], F32, "s1T")
    for ec in range(8):
        for tg in range(T // 512):
            def mmo(e, ec=ec, tg=tg):
                for dc in range(8):
                    inst = e.matmul(mm_ps[:],
                                    w_s[:, D * dc + 128 * ec:D * dc + 128 * (ec + 1)],
                                    TC(ao_s, dc)[:, 512 * tg:512 * (tg + 1)],
                                    start=(dc == 0), stop=(dc == 7))
                return inst
            p.step("tensor", mmo)
            p.step("scalar", lambda e, ec=ec, tg=tg: e.activation(
                TC(s1T, ec)[:, 512 * tg:512 * (tg + 1)], mm_ps[:], AF.Identity,
                bias=bo_s[:, ec:ec + 1], scale=1.0))
    x1f = p.sbuf([128, D], F32, "x1f")
    g1f = p.sbuf([128, D], F32, "g1f")
    s1f = p.sbuf([128, D], F32, "s1f")
    of = p.sbuf([128, D], F32, "of")
    for t in range(ntile):
        p.dma("sync", x1f[:], x1_d[128 * t:128 * (t + 1), :])
        p.dma("sync", g1f[:], g1_d[128 * t:128 * (t + 1), :])
        for kc in range(8):
            p.step("tensor", lambda e, kc=kc, t=t: e.transpose(
                tpf_ps[:], TC(s1T, kc)[:, 128 * t:128 * (t + 1)], identf[:]))
            p.step("vector", lambda e, kc=kc: e.tensor_copy(
                s1f[:, 128 * kc:128 * (kc + 1)], tpf_ps[:]))
        p.step("vector", lambda e: e.tensor_tensor(of[:], s1f[:], g1f[:], ALU.mult))
        p.step("vector", lambda e: e.tensor_tensor(of[:], of[:], x1f[:], ALU.add))
        p.dma("gpsimd", out_d[128 * t:128 * (t + 1), :], of[:])
    return p.finalize(), p.out_names


MEGA_UPTO = 99
SKIP_COLL = False


def _mega_stub_out(p, out_d):
    z = p.sbuf([128, D], F32, "stubz")
    p.step("vector", lambda e: e.memset(z[:], 0.0))
    for t in range(T // 128):
        p.dma("gpsimd", out_d[128 * t:128 * (t + 1), :], z[:])
    return p.finalize()


def build_mega():
    p = _P()
    nc = p.nc
    x_d = p.dram_in("x", (T, D), F32)
    wq_d = p.dram_in("wq", (D, D), F32)
    wk_d = p.dram_in("wk", (D, D), F32)
    wv_d = p.dram_in("wv", (D, D), F32)
    wg_d = p.dram_in("wg", (I0, D), F32)
    wu_d = p.dram_in("wu", (I0, D), F32)
    wd_d = p.dram_in("wd", (D, I0), F32)
    wo_d = p.dram_in("wo", (D, D), F32)
    wqkv_d = p.dram_in("wqkv", (D, 3 * D), F32)
    bqkv_d = p.dram_in("bqkv", (3 * D,), F32)
    wgate_d = p.dram_in("Wg", (D, 8), F32)
    bg_d = p.dram_in("bg", (1, 8), F32)
    w1_d = p.dram_in("W1", (8, D, D), F32)
    b1_d = p.dram_in("b1", (8, D), F32)
    w2_d = p.dram_in("W2", (8, D, D), F32)
    b2_d = p.dram_in("b2", (8, D), F32)
    wout_d = p.dram_in("w_out", (D, D), F32)
    bout_d = p.dram_in("b_out", (D,), F32)
    qt_d = p.dram_in("qtrueb", (128, T), F32)
    selc_d = p.dram_in("selc", (128, 4), F32)
    eyeb_d = p.dram_in("eyeb", (128, 128), BF16)
    eyef_d = p.dram_in("eyef", (128, 128), F32)
    taus_d = p.dram_in("tausin", (128, S // 128), F32)
    out_d = p.dram_out("out", (T, D), F32)

    qt0_d = p.dram_tmp("sQT", (D, T), BF16)
    kt0_d = p.dram_tmp("sKT", (D, T), BF16)
    v0_d = p.dram_tmp("sV", (T, D), BF16)
    g_d = p.dram_tmp("sg", (T, D), F32)
    hq_d = p.dram_tmp("hqT", (I0, T), BF16)
    gk0_d = p.dram_tmp("gK0", (4 * D, T), BF16)
    gv0_d = p.dram_tmp("gV0", (4 * T, D), BF16)
    aot_d = p.dram_tmp("sAOT", (D, T), BF16)
    gat_d = p.dram_tmp("gAT", (4 * D, T), BF16)
    x1_d = p.dram_tmp("sx1", (T, D), F32)
    xn1_d = p.dram_tmp("sxn1", (T, D), F32)
    g1_d = p.dram_tmp("sg1", (T, D), F32)
    q1_d = p.dram_tmp("sQ1T", (D, T), BF16)
    k1_d = p.dram_tmp("sK1T", (D, T), BF16)
    v1_d = p.dram_tmp("sV1", (T, D), BF16)
    gk1_d = p.dram_tmp("gK1", (4 * D, T), BF16)
    gv1_d = p.dram_tmp("gV1", (4 * T, D), BF16)
    ao1_d = p.dram_tmp("sAO1", (D, T), BF16)
    khscr = p.dram_tmp("khscr", (NH, 66, S), BF16)

    GRP = [[0, 1, 2, 3], [4, 5, 6, 7]]

    def collective(kind, in_ap, out_ap):
        if SKIP_COLL:
            return
        p.step("gpsimd", lambda e: e.collective_compute(
            kind, ALU.bypass, GRP, [in_ap.opt()], [out_ap.opt()]), inc=1,
            nodrain=True)

    ident = p.sbuf([128, 128], BF16, "identb")
    p.dma("sync", ident[:], eyeb_d[:])
    identf = p.sbuf([128, 128], F32, "identfs")
    p.dma("sync", identf[:], eyef_d[:])
    tpf_ps = p.psum([128, 128], F32, "tpf")
    tp_ps = p.psum([128, 128], BF16, "tp")
    mm_ps = p.psum([128, 512], F32, "mm")
    ntile = T // 128
    TC = lambda buf, c, w=T: buf[:, c * w:(c + 1) * w]

    eps_col = p.sbuf([128, 1], F32, "lneps")
    p.step("vector", lambda e: e.memset(eps_col[:], LN_EPS))
    ln_cols = _mk_ln_cols(p)
    q = _Quant(p, I0)

    xnqT = p.sbuf([128, 8 * T], BF16, "xnqT")
    xnT = p.sbuf([128, 8 * T], BF16, "xnT")
    xf = p.sbuf([128, D], F32, "xf")
    xn = p.sbuf([128, D], F32, "xn")
    xnb = p.sbuf([128, D], BF16, "xnb")
    xnqb = p.sbuf([128, D], BF16, "xnqb")
    for t in range(ntile):
        p.dma("sync", xf[:], x_d[128 * t:128 * (t + 1), :])
        _ln_tile(p, xn[:], xf[:], ln_cols, eps_col, q.W[0][:, 0:D])
        p.step("vector", lambda e: e.tensor_copy(xnb[:], xn[:]))
        q.emit(xnqb[:], xn[:], D)
        _transpose_tile(p, ident,
                        lambda kc, t=t: TC(xnT, kc)[:, 128 * t:128 * (t + 1)],
                        lambda kc: xnb[:, 128 * kc:128 * (kc + 1)], tp_ps, 8)
        _transpose_tile(p, ident,
                        lambda kc, t=t: TC(xnqT, kc)[:, 128 * t:128 * (t + 1)],
                        lambda kc: xnqb[:, 128 * kc:128 * (kc + 1)], tp_ps, 8)
    if MEGA_UPTO < 1.1:
        return _mega_stub_out(p, out_d), p.out_names
    wrow = p.sbuf([128, D], F32, "wrow")
    wrb = p.sbuf([128, D], BF16, "wrb")
    wT = p.sbuf([128, 8 * D], BF16, "wT")
    WC = lambda c: wT[:, c * D:(c + 1) * D]
    outb = p.sbuf([128, T], BF16, "outb")
    for wsrc, wdst, scalef in ((wq_d, qt0_d, 0.125), (wk_d, kt0_d, 1.0)):
        for r in range(8):
            p.dma("sync", wrow[:], wsrc[128 * r:128 * (r + 1), :])
            p.step("vector", lambda e, sf=scalef: e.tensor_scalar(
                wrb[:], wrow[:], sf, None, ALU.mult))
            _transpose_tile(p, ident,
                            lambda kc, r=r: WC(kc)[:, 128 * r:128 * (r + 1)],
                            lambda kc: wrb[:, 128 * kc:128 * (kc + 1)], tp_ps, 8)
        for ec in range(8):
            for tg in range(T // 512):
                def mms(e, ec=ec, tg=tg):
                    for dc in range(8):
                        inst = e.matmul(mm_ps[:],
                                        WC(dc)[:, 128 * ec:128 * (ec + 1)],
                                        TC(xnT, dc)[:, 512 * tg:512 * (tg + 1)],
                                        start=(dc == 0), stop=(dc == 7))
                    return inst
                p.step("tensor", mms)
                p.step("vector", lambda e, tg=tg: e.tensor_copy(
                    outb[:, 512 * tg:512 * (tg + 1)], mm_ps[:]))
            p.dma("gpsimd", wdst[128 * ec:128 * (ec + 1), :], outb[:])
    for r in range(8):
        p.dma("sync", wrow[:], wv_d[128 * r:128 * (r + 1), :])
        p.step("vector", lambda e: e.tensor_copy(wrb[:], wrow[:]))
        _transpose_tile(p, ident,
                        lambda kc, r=r: WC(kc)[:, 128 * r:128 * (r + 1)],
                        lambda kc: wrb[:, 128 * kc:128 * (kc + 1)], tp_ps, 8)
    for t in range(ntile):
        for eg in range(2):
            def mmv(e, t=t, eg=eg):
                for dc in range(8):
                    inst = e.matmul(mm_ps[:],
                                    TC(xnT, dc)[:, 128 * t:128 * (t + 1)],
                                    WC(dc)[:, 512 * eg:512 * (eg + 1)],
                                    start=(dc == 0), stop=(dc == 7))
                return inst
            p.step("tensor", mmv)
            p.step("vector", lambda e, eg=eg: e.tensor_copy(
                outb[:, 512 * eg:512 * (eg + 1)], mm_ps[:]))
        p.dma("gpsimd", v0_d[128 * t:128 * (t + 1), :], outb[:, 0:D])
    p.free(outb, wT, wrb, wrow, xnqb, xnb, xn, xf, xnT)
    if MEGA_UPTO < 1.3:
        return _mega_stub_out(p, out_d), p.out_names
    collective("AllGather", kt0_d, gk0_d)
    collective("AllGather", v0_d, gv0_d)
    if MEGA_UPTO < 1.35:
        return _mega_stub_out(p, out_d), p.out_names
    wgqT = p.sbuf([128, 8 * I0], BF16, "wgqT")
    wuqT = p.sbuf([128, 8 * I0], BF16, "wuqT")
    WGC = lambda buf, c: buf[:, c * I0:(c + 1) * I0]
    wrow2 = p.sbuf([128, D], F32, "wrow2")
    wqb = p.sbuf([128, D], BF16, "wqb")
    for wsrc, wbuf in ((wg_d, wgqT), (wu_d, wuqT)):
        for r in range(I0 // 128):
            p.dma("sync", wrow2[:], wsrc[128 * r:128 * (r + 1), :])
            q.emit(wqb[:], wrow2[:], D)
            _transpose_tile(p, ident,
                            lambda kc, r=r, wbuf=wbuf:
                            WGC(wbuf, kc)[:, 128 * r:128 * (r + 1)],
                            lambda kc: wqb[:, 128 * kc:128 * (kc + 1)], tp_ps, 8)
    gate_ps = p.psum([128, 512], F32, "gate")
    up_ps = p.psum([128, 512], F32, "up")
    sgate = q.W[1]
    hbuf = p.sbuf([128, I0], F32, "hbuf")
    hqb = p.sbuf([128, I0], BF16, "hqb")
    hpc = p.sbuf([128, 128], BF16, "hpc")
    for t in range(ntile):
        for off, w in _nsplit(I0, 512):
            def mmg(e, t=t, off=off, w=w):
                for dc in range(8):
                    e.matmul(gate_ps[:, 0:w],
                             TC(xnqT, dc)[:, 128 * t:128 * (t + 1)],
                             WGC(wgqT, dc)[:, off:off + w],
                             start=(dc == 0), stop=(dc == 7))
                for dc in range(8):
                    inst = e.matmul(up_ps[:, 0:w],
                                    TC(xnqT, dc)[:, 128 * t:128 * (t + 1)],
                                    WGC(wuqT, dc)[:, off:off + w],
                                    start=(dc == 0), stop=(dc == 7))
                return inst
            p.step("tensor", mmg)
            p.step("scalar", lambda e, off=off, w=w: e.activation(
                sgate[:, off:off + w], gate_ps[:, 0:w], AF.Silu, bias=0.0,
                scale=1.0))
            p.step("vector", lambda e, off=off, w=w: e.tensor_tensor(
                hbuf[:, off:off + w], sgate[:, off:off + w], up_ps[:, 0:w],
                ALU.mult))
        q.emit(hqb[:], hbuf[:], I0)
        for kc in range(I0 // 128):
            p.step("tensor", lambda e, kc=kc: e.transpose(
                tp_ps[:], hqb[:, 128 * kc:128 * (kc + 1)], ident[:]))
            p.step("vector", lambda e: e.tensor_copy(hpc[:], tp_ps[:]))
            p.dma("gpsimd",
                  hq_d[128 * kc:128 * (kc + 1), 128 * t:128 * (t + 1)], hpc[:])
    p.free(hpc, hqb, hbuf, wqb, wrow2, wuqT, wgqT, xnqT)
    p.free(up_ps, gate_ps)
    if MEGA_UPTO < 1.5:
        return _mega_stub_out(p, out_d), p.out_names
    wdqT = p.sbuf([128, 22 * D], BF16, "wdqT")
    wdrow = p.sbuf([128, I0], F32, "wdrow")
    wdqb = p.sbuf([128, I0], BF16, "wdqb")
    for r in range(8):
        p.dma("sync", wdrow[:], wd_d[128 * r:128 * (r + 1), :])
        q.emit(wdqb[:], wdrow[:], I0)
        for kc in range(I0 // 128):
            p.step("tensor", lambda e, kc=kc: e.transpose(
                tp_ps[:], wdqb[:, 128 * kc:128 * (kc + 1)], ident[:]))
            p.step("vector", lambda e, kc=kc, r=r: e.tensor_copy(
                wdqT[:, kc * D + 128 * r: kc * D + 128 * (r + 1)], tp_ps[:]))
    p.free(wdqb, wdrow)
    hqT_s = p.sbuf([128, 22 * T], BF16, "hqTs")
    for kc in range(I0 // 128):
        p.dma("sync", hqT_s[:, kc * T:(kc + 1) * T],
              hq_d[128 * kc:128 * (kc + 1), :])
    gout = p.sbuf([128, D], F32, "gout")
    for t in range(ntile):
        for eg in range(2):
            def mmd(e, t=t, eg=eg):
                for kc in range(22):
                    inst = e.matmul(
                        mm_ps[:],
                        hqT_s[:, kc * T + 128 * t: kc * T + 128 * (t + 1)],
                        wdqT[:, kc * D + 512 * eg: kc * D + 512 * (eg + 1)],
                        start=(kc == 0), stop=(kc == 21))
                return inst
            p.step("tensor", mmd)
            p.step("scalar", lambda e, eg=eg: e.activation(
                gout[:, 512 * eg:512 * (eg + 1)], mm_ps[:], AF.Silu,
                bias=0.0, scale=1.0))
        p.dma("gpsimd", g_d[128 * t:128 * (t + 1), :], gout[:])
    p.free(gout, hqT_s, wdqT)
    q.free()

    lnscr = p.sbuf([128, D], F32, "lnscr")

    if MEGA_UPTO < 2:
        return _mega_stub_out(p, out_d), p.out_names
    KSUP = 1024
    QG = 512
    nks = S // KSUP
    nqg = T // QG
    qt_s = p.sbuf([128, 8 * T], BF16, "qts")
    for c in range(8):
        p.dma("sync", TC(qt_s, c), qt0_d[128 * c:128 * (c + 1), :])
    oacc = p.sbuf([128, 8 * T], F32, "oacc")
    p.step("vector", lambda e: e.memset(oacc[:], 0.0))
    lacc = p.sbuf([1, T], F32, "lacc")
    p.step("vector", lambda e: e.memset(lacc[:], 0.0))
    ones_col = p.sbuf([128, 1], BF16, "ones")
    p.step("vector", lambda e: e.memset(ones_col[:], 1.0))
    nc0 = p.sbuf([128, 1], F32, "nc0")
    p.step("vector", lambda e: e.memset(nc0[:], -C0))
    p64c = p.sbuf([1, 1], F32, "p64c")
    p.step("vector", lambda e: e.memset(p64c[:], 44.3614195558365))
    kt_s = p.sbuf([128, 8 * KSUP], BF16, "kts")
    v_s = p.sbuf([128, 8 * D], BF16, "vs")
    pt_s = p.sbuf([128, 8 * QG], BF16, "pts")
    s_ps = p.psum([128, QG], F32, "sps")
    o_ps = p.psum([128, QG], F32, "ops")
    l_ps = p.psum([1, QG], F32, "lps")
    for ks in range(nks):
        rk = ks // 2
        lo = KSUP * (ks % 2)
        for c in range(8):
            p.dma("sync", TC(kt_s, c, KSUP),
                  gk0_d[D * rk + 128 * c:D * rk + 128 * (c + 1), lo:lo + KSUP])
        for kb in range(8):
            p.dma("sync", TC(v_s, kb, D),
                  gv0_d[KSUP * ks + 128 * kb:KSUP * ks + 128 * (kb + 1), :])
        for qg in range(nqg):
            for kb in range(8):
                def mmS(e, kb=kb, qg=qg):
                    for dc in range(8):
                        inst = e.matmul(
                            s_ps[:],
                            TC(kt_s, dc, KSUP)[:, 128 * kb:128 * (kb + 1)],
                            TC(qt_s, dc)[:, QG * qg:QG * (qg + 1)],
                            start=(dc == 0), stop=(dc == 7))
                    return inst
                p.step("tensor", mmS)
                p.step("scalar", lambda e, kb=kb: e.activation(
                    TC(pt_s, kb, QG), s_ps[:], AF.Exp, bias=nc0[:], scale=1.0))
            def mmL(e):
                for kb in range(8):
                    inst = e.matmul(l_ps[:], ones_col[:], TC(pt_s, kb, QG),
                                    start=(kb == 0), stop=(kb == 7))
                return inst
            p.step("tensor", mmL)
            p.step("vector", lambda e, qg=qg: e.tensor_tensor(
                lacc[:, QG * qg:QG * (qg + 1)], lacc[:, QG * qg:QG * (qg + 1)],
                l_ps[:], ALU.add))
            for ec in range(8):
                def mmPV(e, ec=ec):
                    for kb in range(8):
                        inst = e.matmul(o_ps[:],
                                        TC(v_s, kb, D)[:, 128 * ec:128 * (ec + 1)],
                                        TC(pt_s, kb, QG),
                                        start=(kb == 0), stop=(kb == 7))
                    return inst
                p.step("tensor", mmPV)
                p.step("vector", lambda e, ec=ec, qg=qg: e.tensor_tensor(
                    TC(oacc, ec)[:, QG * qg:QG * (qg + 1)],
                    TC(oacc, ec)[:, QG * qg:QG * (qg + 1)], o_ps[:], ALU.add))
    rl0 = p.sbuf([1, T], F32, "rl0")
    def srl0(e):
        e.activation(rl0[:], lacc[:], AF.Ln, bias=0.0,
                     scale=18446744073709551616.0)
        return e.activation(rl0[:], rl0[:], AF.Exp, bias=p64c[0:1, :], scale=-1.0)
    p.step("scalar", srl0)
    rlb_ps2 = p.psum([128, QG], F32, "rlb2")
    rlb0 = p.sbuf([128, T], F32, "rlbs0")
    ones1 = p.sbuf([1, 128], F32, "ones1")
    p.step("vector", lambda e: e.memset(ones1[:], 1.0))
    for qg in range(nqg):
        p.step("tensor", lambda e, qg=qg: e.matmul(
            rlb_ps2[:], ones1[:], rl0[:, QG * qg:QG * (qg + 1)],
            start=True, stop=True))
        p.step("vector", lambda e, qg=qg: e.tensor_copy(
            rlb0[:, QG * qg:QG * (qg + 1)], rlb_ps2[:]))
    aob0 = p.sbuf([128, T], BF16, "aob0")
    for ec in range(8):
        p.step("vector", lambda e, ec=ec: e.tensor_tensor(
            aob0[:], TC(oacc, ec), rlb0[:], ALU.mult))
        p.dma("gpsimd", aot_d[128 * ec:128 * (ec + 1), :], aob0[:])
    p.free(aob0, ones1, rlb0, rl0, pt_s, v_s, kt_s, p64c, nc0, ones_col,
           lacc, oacc, qt_s)
    p.free(rlb_ps2, l_ps, o_ps, s_ps)
    collective("AllGather", aot_d, gat_d)

    if MEGA_UPTO < 3:
        return _mega_stub_out(p, out_d), p.out_names
    spt = p.sbuf([128, 8 * T], BF16, "spt")
    ag_s = p.sbuf([128, 8 * T], BF16, "ags")
    selc_s = p.sbuf([128, 4], F32, "selcs")
    p.dma("sync", selc_s[:], selc_d[:])
    ag4 = p.sbuf([128, 4 * T], BF16, "ag4")
    agmix = p.sbuf([128, T], F32, "agmix")
    for pc in range(8):
        i_snd, half = pc // 2, pc % 2
        for cc in range(4):
            p.dma("sync", ag4[:, T * cc:T * (cc + 1)],
                  gat_d[D * i_snd + 256 * cc + 128 * half:
                        D * i_snd + 256 * cc + 128 * half + 128, :])
        def selmix(e, pc=pc):
            e.tensor_scalar(agmix[:], ag4[:, 0:T], selc_s[:, 0:1], None,
                            ALU.mult)
            for cc in range(1, 4):
                e.scalar_tensor_tensor(agmix[:], ag4[:, T * cc:T * (cc + 1)],
                                       selc_s[:, cc:cc + 1], agmix[:],
                                       ALU.mult, ALU.add)
            return e.tensor_copy(TC(ag_s, pc), agmix[:])
        p.step("vector", selmix)
    p.free(agmix, ag4, selc_s)
    for rr in range(8):
        for jc in range(8):
            sp0 = 1024 * rr + 128 * jc
            rp = sp0 // 2048
            tl0 = sp0 % 2048
            for half in range(2):
                pc = 2 * rp + half
                p.step("tensor", lambda e, pc=pc, tl0=tl0: e.transpose(
                    tp_ps[:], TC(ag_s, pc)[:, tl0:tl0 + 128], ident[:]))
                p.step("vector", lambda e, jc=jc, rr=rr, half=half: e.tensor_copy(
                    TC(spt, jc)[:, 256 * rr + 128 * half:256 * rr + 128 * (half + 1)],
                    tp_ps[:]))
    p.free(ag_s)
    woT = p.sbuf([128, 8 * D], BF16, "woT")
    wrowo = p.sbuf([128, D], F32, "wrowo")
    wrbo = p.sbuf([128, D], BF16, "wrbo")
    for r in range(8):
        p.dma("sync", wrowo[:], wo_d[128 * r:128 * (r + 1), :])
        p.step("vector", lambda e: e.tensor_copy(wrbo[:], wrowo[:]))
        _transpose_tile(p, ident,
                        lambda kc, r=r: TC(woT, kc, D)[:, 128 * r:128 * (r + 1)],
                        lambda kc: wrbo[:, 128 * kc:128 * (kc + 1)], tp_ps, 8)
    sTe = p.sbuf([128, T], F32, "sTe")
    xpi = p.sbuf([128, D], F32, "xpi")
    gpi = p.sbuf([128, D], F32, "gpi")
    x1pi = p.sbuf([128, D], F32, "x1pi")
    xn1pi = p.sbuf([128, D], F32, "xn1pi")
    s1t = p.sbuf([128, D], F32, "s1t")
    sT_all = p.sbuf([128, 8 * T], F32, "sTall")
    for ec in range(8):
        for tg in range(T // 512):
            def mmW(e, ec=ec, tg=tg):
                for jc in range(8):
                    inst = e.matmul(mm_ps[:],
                                    TC(woT, jc, D)[:, 128 * ec:128 * (ec + 1)],
                                    TC(spt, jc)[:, 512 * tg:512 * (tg + 1)],
                                    start=(jc == 0), stop=(jc == 7))
                return inst
            p.step("tensor", mmW)
            p.step("vector", lambda e, ec=ec, tg=tg: e.tensor_copy(
                TC(sT_all, ec)[:, 512 * tg:512 * (tg + 1)], mm_ps[:]))
    xv = x_d.rearrange("(a b) n -> b a n", b=8)
    x1v = x1_d.rearrange("(a b) n -> b a n", b=8)
    xn1v = xn1_d.rearrange("(a b) n -> b a n", b=8)
    gv = g_d.rearrange("(a b) n -> b a n", b=8)
    for rr in range(8):
        for half in range(2):
            al0 = 128 * half
            for ec in range(8):
                p.step("tensor", lambda e, ec=ec, rr=rr, half=half: e.transpose(
                    tpf_ps[:],
                    TC(sT_all, ec)[:, 256 * rr + 128 * half:256 * rr + 128 * (half + 1)],
                    identf[:]))
                p.step("vector", lambda e, ec=ec: e.tensor_copy(
                    s1t[:, 128 * ec:128 * (ec + 1)], tpf_ps[:]))
            p.dma("sync", xpi[:], xv[rr, al0:al0 + 128, :])
            p.dma("sync", gpi[:], gv[rr, al0:al0 + 128, :])
            p.step("vector", lambda e: e.tensor_tensor(x1pi[:], s1t[:], gpi[:],
                                                       ALU.mult))
            p.step("vector", lambda e: e.tensor_tensor(x1pi[:], x1pi[:], xpi[:],
                                                       ALU.add))
            p.dma("gpsimd", x1v[rr, al0:al0 + 128, :], x1pi[:])
            _ln_tile(p, xn1pi[:], x1pi[:], ln_cols, eps_col, lnscr[:])
            p.dma("gpsimd", xn1v[rr, al0:al0 + 128, :], xn1pi[:])
    p.free(sT_all, s1t, xn1pi, x1pi, gpi, xpi, sTe, wrbo, wrowo, woT, spt,
           lnscr)

    if MEGA_UPTO < 4:
        return _mega_stub_out(p, out_d), p.out_names
    xn1T = p.sbuf([128, 8 * T], BF16, "xn1T")
    selT = p.sbuf([128, 8 * ntile], F32, "selT")
    xn1loT = p.sbuf([128, 8 * T], BF16, "xn1loT")
    xn1f = p.sbuf([128, D], F32, "xn1f")
    xn1b = p.sbuf([128, D], BF16, "xn1b")
    xn1lo = p.sbuf([128, D], BF16, "xn1lo")
    for t in range(ntile):
        p.dma("sync", xn1f[:], xn1_d[128 * t:128 * (t + 1), :])
        p.step("vector", lambda e: e.tensor_copy(xn1b[:], xn1f[:]))
        p.step("vector", lambda e: e.tensor_sub(xn1lo[:], xn1f[:], xn1b[:]))
        _transpose_tile(p, ident,
                        lambda kc, t=t: TC(xn1T, kc)[:, 128 * t:128 * (t + 1)],
                        lambda kc: xn1b[:, 128 * kc:128 * (kc + 1)], tp_ps, 8)
        _transpose_tile(p, ident,
                        lambda kc, t=t: TC(xn1loT, kc)[:, 128 * t:128 * (t + 1)],
                        lambda kc: xn1lo[:, 128 * kc:128 * (kc + 1)], tp_ps, 8)
    p.free(xn1lo, xn1b, xn1f)
    wqkv_s = p.sbuf([128, 8 * 3 * D], BF16, "wqkvs")
    W3C = lambda c: wqkv_s[:, c * 3 * D:(c + 1) * 3 * D]
    wrow3 = p.sbuf([128, 3 * D], F32, "wrow3")
    for r in range(8):
        p.dma("sync", wrow3[:], wqkv_d[128 * r:128 * (r + 1), :])
        def cv(e, r=r):
            e.tensor_copy(W3C(r)[:, 0:D], wrow3[:, 0:D])
            e.tensor_scalar(W3C(r)[:, D:2 * D], wrow3[:, D:2 * D], -2.0, None,
                            ALU.mult)
            return e.tensor_copy(W3C(r)[:, 2 * D:3 * D], wrow3[:, 2 * D:3 * D])
        p.step("vector", cv)
    bq_s = p.sbuf([128, 16], F32, "bqs")
    for ec in range(8):
        p.dma("sync", bq_s[:, ec:ec + 1],
              bqkv_d[128 * ec:128 * (ec + 1)].unsqueeze(1))
    for ec in range(8):
        p.dma("sync", bq_s[:, 8 + ec:9 + ec],
              bqkv_d[D + 128 * ec:D + 128 * (ec + 1)].unsqueeze(1))
    p.step("vector", lambda e: e.tensor_scalar(bq_s[:, 8:16], bq_s[:, 8:16],
                                               -2.0, None, ALU.mult))
    outqb = p.sbuf([128, T], BF16, "outqb")
    for (dst, coff, bo) in ((q1_d, 0, 0), (k1_d, D, 8)):
        for ec in range(8):
            for tg in range(T // 512):
                def mmq(e, ec=ec, tg=tg, coff=coff):
                    for dc in range(8):
                        inst = e.matmul(
                            mm_ps[:],
                            W3C(dc)[:, coff + 128 * ec:coff + 128 * (ec + 1)],
                            TC(xn1T, dc)[:, 512 * tg:512 * (tg + 1)],
                            start=(dc == 0), stop=(dc == 7))
                    return inst
                p.step("tensor", mmq)
                p.step("scalar", lambda e, tg=tg, ec=ec, bo=bo: e.activation(
                    outqb[:, 512 * tg:512 * (tg + 1)], mm_ps[:], AF.Identity,
                    bias=bq_s[:, bo + ec:bo + ec + 1], scale=1.0))
            p.dma("gpsimd", dst[128 * ec:128 * (ec + 1), :], outqb[:])
    onesrow = p.sbuf([1, 128], BF16, "onesr")
    p.step("vector", lambda e: e.memset(onesrow[:], 1.0))
    bvrow = p.sbuf([1, D], F32, "bvrow")
    p.dma("sync", bvrow[:], bqkv_d[2 * D:3 * D].unsqueeze(0))
    bvrow_b = p.sbuf([1, D], BF16, "bvrowb")
    p.step("vector", lambda e: e.tensor_copy(bvrow_b[:], bvrow[:]))
    for t in range(ntile):
        for eg in range(2):
            def mmv1(e, t=t, eg=eg):
                for dc in range(8):
                    e.matmul(mm_ps[:],
                             TC(xn1T, dc)[:, 128 * t:128 * (t + 1)],
                             W3C(dc)[:, 2 * D + 512 * eg:2 * D + 512 * (eg + 1)],
                             start=(dc == 0), stop=False)
                return e.matmul(mm_ps[:], onesrow[:],
                                bvrow_b[:, 512 * eg:512 * (eg + 1)],
                                start=False, stop=True)
            p.step("tensor", mmv1)
            p.step("vector", lambda e, eg=eg: e.tensor_copy(
                outqb[:, 512 * eg:512 * (eg + 1)], mm_ps[:]))
        p.dma("gpsimd", v1_d[128 * t:128 * (t + 1), :], outqb[:, 0:D])
    p.free(bvrow_b, bvrow, onesrow, outqb, bq_s, wrow3, wqkv_s)
    collective("AllGather", k1_d, gk1_d)
    collective("AllGather", v1_d, gv1_d)
    wg_s = p.sbuf([128, 8 * 8], F32, "wgs")
    for dc in range(8):
        p.dma("sync", wg_s[:, 8 * dc:8 * (dc + 1)],
              wgate_d[128 * dc:128 * (dc + 1), :])
    wg_hi = p.sbuf([128, 8 * 8], BF16, "wghi")
    wg_lo = p.sbuf([128, 8 * 8], BF16, "wglo")
    p.step("vector", lambda e: e.tensor_copy(wg_hi[:], wg_s[:]))
    p.step("vector", lambda e: e.tensor_sub(wg_lo[:], wg_s[:], wg_hi[:]))
    lg_ps = p.psum([8, 512], F32, "lgps")
    lgT = p.sbuf([8, T], F32, "lgT")
    for tg in range(T // 512):
        def mmlg(e, tg=tg):
            combos = []
            for dc in range(8):
                combos.append((wg_hi[:, 8 * dc:8 * (dc + 1)], TC(xn1T, dc)))
                combos.append((wg_lo[:, 8 * dc:8 * (dc + 1)], TC(xn1T, dc)))
                combos.append((wg_hi[:, 8 * dc:8 * (dc + 1)], TC(xn1loT, dc)))
            for i, (wb, xb) in enumerate(combos):
                inst = e.matmul(lg_ps[:], wb, xb[:, 512 * tg:512 * (tg + 1)],
                                start=(i == 0), stop=(i == len(combos) - 1))
            return inst
        p.step("tensor", mmlg)
        p.step("vector", lambda e, tg=tg: e.tensor_copy(
            lgT[:, 512 * tg:512 * (tg + 1)], lg_ps[:]))
    bg_s = p.sbuf([1, 8], F32, "bgs")
    p.dma("sync", bg_s[:], bg_d[:])
    ones1f = p.sbuf([1, 128], F32, "ones1f")
    p.step("vector", lambda e: e.memset(ones1f[:], 1.0))
    bgb_ps = p.psum([128, 8], F32, "bgb")
    bgb = p.sbuf([128, 8], F32, "bgbs")
    p.step("tensor", lambda e: e.matmul(bgb_ps[:], ones1f[:], bg_s[:],
                                        start=True, stop=True))
    p.step("vector", lambda e: e.tensor_copy(bgb[:], bgb_ps[:]))
    lgt_f = p.sbuf([128, 8], F32, "lgtf")
    mcol = p.sbuf([128, 1], F32, "mcol")
    for t in range(ntile):
        p.step("tensor", lambda e, t=t: e.transpose(
            tpf_ps[:, 0:8], lgT[:, 128 * t:128 * (t + 1)], identf[0:8, 0:8]))
        p.step("vector", lambda e: e.tensor_copy(lgt_f[:], tpf_ps[:, 0:8]))
        p.step("vector", lambda e: e.tensor_tensor(lgt_f[:], lgt_f[:], bgb[:],
                                                   ALU.add))
        p.step("vector", lambda e: e.tensor_reduce(mcol[:], lgt_f[:], AX.X,
                                                   ALU.max))
        p.step("vector", lambda e, t=t: e.tensor_scalar(
            selT[:, 8 * t:8 * (t + 1)], lgt_f[:], mcol[:], None, ALU.is_ge))
    p.free(mcol, lgt_f, bgb, ones1f, bg_s, lgT, wg_lo, wg_hi, wg_s, xn1loT)
    p.free(bgb_ps, lg_ps)
    yacc = p.sbuf([128, ntile * D], F32, "yacc")
    p.step("vector", lambda e: e.memset(yacc[:], 0.0))
    heT = p.sbuf([128, 8 * T], BF16, "heT")
    w1b = p.sbuf([128, 8 * D], BF16, "w1b")
    w2b = p.sbuf([128, 8 * D], BF16, "w2b")
    wrowm = p.sbuf([128, D], F32, "wrowm")
    b1c = p.sbuf([128, 8], F32, "b1c")
    b2row = p.sbuf([1, D], F32, "b2row")
    b2row_b = p.sbuf([1, D], BF16, "b2rowb")
    onesrow2 = p.sbuf([1, 128], BF16, "onesr2")
    p.step("vector", lambda e: e.memset(onesrow2[:], 1.0))
    ytmp = p.sbuf([128, 512], F32, "ytmp")
    for ex in range(8):
        for r in range(8):
            p.dma("sync", wrowm[:], w1_d[ex, 128 * r:128 * (r + 1), :])
            p.step("vector", lambda e, r=r: e.tensor_copy(
                w1b[:, D * r:D * (r + 1)], wrowm[:]))
        for r in range(8):
            p.dma("sync", wrowm[:], w2_d[ex, 128 * r:128 * (r + 1), :])
            p.step("vector", lambda e, r=r: e.tensor_copy(
                w2b[:, D * r:D * (r + 1)], wrowm[:]))
        for ic in range(8):
            p.dma("sync", b1c[:, ic:ic + 1],
                  b1_d[ex, 128 * ic:128 * (ic + 1)].unsqueeze(1))
        p.dma("sync", b2row[:], b2_d[ex:ex + 1, :])
        p.step("vector", lambda e: e.tensor_copy(b2row_b[:], b2row[:]))
        for ic in range(8):
            for tg in range(T // 512):
                def mmh(e, ic=ic, tg=tg):
                    for dc in range(8):
                        inst = e.matmul(
                            mm_ps[:],
                            w1b[:, D * dc + 128 * ic:D * dc + 128 * (ic + 1)],
                            TC(xn1T, dc)[:, 512 * tg:512 * (tg + 1)],
                            start=(dc == 0), stop=(dc == 7))
                    return inst
                p.step("tensor", mmh)
                p.step("scalar", lambda e, ic=ic, tg=tg: e.activation(
                    TC(heT, ic)[:, 512 * tg:512 * (tg + 1)], mm_ps[:],
                    AF.Gelu, bias=b1c[:, ic:ic + 1], scale=1.0))
        for t in range(ntile):
            for eg in range(2):
                def mmy(e, t=t, eg=eg):
                    for ic in range(8):
                        e.matmul(mm_ps[:],
                                 TC(heT, ic)[:, 128 * t:128 * (t + 1)],
                                 w2b[:, D * ic + 512 * eg:D * ic + 512 * (eg + 1)],
                                 start=(ic == 0), stop=False)
                    return e.matmul(mm_ps[:], onesrow2[:],
                                    b2row_b[:, 512 * eg:512 * (eg + 1)],
                                    start=False, stop=True)
                p.step("tensor", mmy)
                p.step("vector", lambda e, t=t, ex=ex: e.tensor_scalar(
                    ytmp[:], mm_ps[:], selT[:, 8 * t + ex:8 * t + ex + 1],
                    None, ALU.mult))
                p.step("vector", lambda e, t=t, eg=eg: e.tensor_tensor(
                    yacc[:, D * t + 512 * eg:D * t + 512 * (eg + 1)],
                    yacc[:, D * t + 512 * eg:D * t + 512 * (eg + 1)],
                    ytmp[:], ALU.add))
    g1b = p.sbuf([128, D], F32, "g1b")
    for t in range(ntile):
        p.step("scalar", lambda e, t=t: e.activation(
            g1b[:], yacc[:, D * t:D * (t + 1)], AF.Silu, bias=0.0, scale=1.0))
        p.dma("gpsimd", g1_d[128 * t:128 * (t + 1), :], g1b[:])
    p.free(g1b, ytmp, onesrow2, b2row_b, b2row, b1c, wrowm, w2b, w1b, heT,
           yacc, selT, xn1T)
    p.free(mm_ps, tp_ps)

    if MEGA_UPTO < 5:
        return _mega_stub_out(p, out_d), p.out_names
    emit_l1_attn_fused(p, q1_d, gk1_d, gv1_d, qt_d, ao1_d, khscr, identf,
                       taus_d)

    if MEGA_UPTO < 6:
        return _mega_stub_out(p, out_d), p.out_names
    mm_ps = p.psum([128, 512], F32, "mm6")
    ao_s = p.sbuf([128, 8 * T], BF16, "aos")
    for c in range(8):
        p.dma("sync", TC(ao_s, c), ao1_d[128 * c:128 * (c + 1), :])
    w_s = p.sbuf([128, 8 * D], BF16, "ws")
    wrow5 = p.sbuf([128, D], F32, "wr5")
    for r in range(8):
        p.dma("sync", wrow5[:], wout_d[128 * r:128 * (r + 1), :])
        p.step("vector", lambda e, r=r: e.tensor_copy(
            w_s[:, D * r:D * (r + 1)], wrow5[:]))
    bo_s = p.sbuf([128, 8], F32, "bos")
    for ec in range(8):
        p.dma("sync", bo_s[:, ec:ec + 1],
              bout_d[128 * ec:128 * (ec + 1)].unsqueeze(1))
    s1T = p.sbuf([128, 8 * T], F32, "s1T5")
    for ec in range(8):
        for tg in range(T // 512):
            def mmo(e, ec=ec, tg=tg):
                for dc in range(8):
                    inst = e.matmul(
                        mm_ps[:],
                        w_s[:, D * dc + 128 * ec:D * dc + 128 * (ec + 1)],
                        TC(ao_s, dc)[:, 512 * tg:512 * (tg + 1)],
                        start=(dc == 0), stop=(dc == 7))
                return inst
            p.step("tensor", mmo)
            p.step("scalar", lambda e, ec=ec, tg=tg: e.activation(
                TC(s1T, ec)[:, 512 * tg:512 * (tg + 1)], mm_ps[:], AF.Identity,
                bias=bo_s[:, ec:ec + 1], scale=1.0))
    x1f5 = p.sbuf([128, D], F32, "x1f5")
    g1f5 = p.sbuf([128, D], F32, "g1f5")
    s1f5 = p.sbuf([128, D], F32, "s1f5")
    of5 = p.sbuf([128, D], F32, "of5")
    for t in range(ntile):
        p.dma("sync", x1f5[:], x1_d[128 * t:128 * (t + 1), :])
        p.dma("sync", g1f5[:], g1_d[128 * t:128 * (t + 1), :])
        for kc in range(8):
            p.step("tensor", lambda e, kc=kc, t=t: e.transpose(
                tpf_ps[:], TC(s1T, kc)[:, 128 * t:128 * (t + 1)], identf[:]))
            p.step("vector", lambda e, kc=kc: e.tensor_copy(
                s1f5[:, 128 * kc:128 * (kc + 1)], tpf_ps[:]))
        p.step("vector", lambda e: e.tensor_tensor(of5[:], s1f5[:], g1f5[:],
                                                   ALU.mult))
        p.step("vector", lambda e: e.tensor_tensor(of5[:], of5[:], x1f5[:],
                                                   ALU.add))
        p.dma("gpsimd", out_d[128 * t:128 * (t + 1), :], of5[:])
    return p.finalize(), p.out_names


def emit_l1_attn_fused(p, q1_d, gk1_d, gv1_d, qt_d, ao_d, khscr, identf,
                       taus_d):
    QG = 512
    nqg = T // QG
    nkb = S // 128
    ones64f = p.sbuf([64, 1], F32, "o64")
    p.step("vector", lambda e: e.memset(ones64f[:], 1.0))
    ones1f64 = p.sbuf([1, 64], F32, "o1f")
    p.step("vector", lambda e: e.memset(ones1f64[:], 1.0))
    nc1c = p.sbuf([128, 1], F32, "nc1")
    p.step("vector", lambda e: e.memset(nc1c[:], -C1))
    p64c5 = p.sbuf([1, 1], F32, "p64c5")
    p.step("vector", lambda e: e.memset(p64c5[:], 44.3614195558365))
    qtrue = p.sbuf([128, T], F32, "qtrue")
    p.dma("sync", qtrue[:], qt_d[:])
    taus = p.sbuf([128, nkb], F32, "taus")
    p.dma("sync", taus[:], taus_d[:])
    QHs = [p.sbuf([66, T], BF16, f"QH{h}") for h in range(NH)]
    onesT = p.sbuf([1, T], BF16, "onesT")
    p.step("vector", lambda e: e.memset(onesT[:], 1.0))
    qsq = p.sbuf([64, T], F32, "qsq")
    qn_ps = p.psum([1, 512], F32, "qnps")
    for h in range(NH):
        p.dma("sync", QHs[h][0:64, :], q1_d[64 * h:64 * (h + 1), :])
        p.step("scalar", lambda e, h=h: e.activation(
            qsq[:], QHs[h][0:64, :], AF.Square, bias=0.0, scale=1.0))
        p.dma("gpsimd", QHs[h][65:66, :], onesT[:, 0:T])
        for tg in range(nqg):
            p.step("tensor", lambda e, tg=tg: e.matmul(
                qn_ps[:], ones64f[:], qsq[:, QG * tg:QG * (tg + 1)],
                start=True, stop=True))
            p.step("vector", lambda e, h=h, tg=tg: e.tensor_copy(
                QHs[h][64:65, QG * tg:QG * (tg + 1)], qn_ps[:]))
    p.free(qsq)
    KH = p.sbuf([66, S], BF16, "KH")
    ksq = p.sbuf([64, 512], F32, "ksq")
    kn_ps = qn_ps
    knrow = p.sbuf([1, 512], F32, "knrow")
    knrow_b = p.sbuf([1, S], BF16, "knrowb")
    maskb = p.sbuf([128, nkb * 512], BF16, "maskb")
    mtmp = p.sbuf([128, 512], F32, "mtmp")
    vbuf = p.sbuf([128, 65], BF16, "vbuf")
    p.step("vector", lambda e: e.memset(vbuf[:, 64:65], 1.0))
    den_ps = p.psum([128, 1024], F32, "denps")
    dot_ps = p.psum([128, 1024], F32, "dotps")
    o_ps = p.psum([65, QG], F32, "ops5")
    rlb_ps = p.psum([64, QG], F32, "rlbps")
    bln = p.sbuf([128, 1024], F32, "bln")
    rln = p.sbuf([128, 1024], F32, "rln")
    m1 = p.sbuf([128, 1024], F32, "m1")
    uu = p.sbuf([128, 1024], F32, "uu")
    pt = p.sbuf([128, 1024], BF16, "pt")
    ohl = p.sbuf([65, QG], F32, "ohl")
    rl = p.sbuf([1, QG], F32, "rl")
    aob = p.sbuf([64, QG], BF16, "aob")
    for qg in range(nqg):
        for kb in range(nkb):
            p.step("vector", lambda e, kb=kb, qg=qg: e.tensor_scalar(
                mtmp[:], qtrue[:, QG * qg:QG * (qg + 1)], taus[:, kb:kb + 1],
                None, ALU.is_ge))
            p.step("vector", lambda e, kb=kb: e.tensor_copy(
                maskb[:, 512 * kb:512 * (kb + 1)], mtmp[:]))
        for h in range(NH):
            if qg == 0:
                for r in range(4):
                    p.dma("sync", KH[0:64, T * r:T * (r + 1)],
                          gk1_d[D * r + 64 * h:D * r + 64 * (h + 1), :])
                p.step("vector", lambda e: e.memset(KH[64:65, :], 1.0))
                for sg in range(S // 512):
                    p.step("scalar", lambda e, sg=sg: e.activation(
                        ksq[:], KH[0:64, 512 * sg:512 * (sg + 1)], AF.Square,
                        bias=0.0, scale=0.5))
                    p.step("tensor", lambda e: e.matmul(
                        kn_ps[:], ones64f[:], ksq[:], start=True, stop=True))
                    p.step("vector", lambda e: e.tensor_scalar(
                        knrow[:], kn_ps[:], ATT_EPS, None, ALU.add))
                    p.step("vector", lambda e, sg=sg: e.tensor_copy(
                        knrow_b[:, 512 * sg:512 * (sg + 1)], knrow[:]))
                p.dma("gpsimd", KH[65:66, :], knrow_b[:])
                p.dma("gpsimd", khscr[h], KH[:])
            else:
                p.dma("sync", KH[:], khscr[h])
            for kbp in range(nkb // 2):
                def mms(e, kbp=kbp, qg=qg, h=h):
                    for half in range(2):
                        kb = 2 * kbp + half
                        e.matmul(den_ps[:, 512 * half:512 * (half + 1)],
                                 KH[0:66, 128 * kb:128 * (kb + 1)],
                                 QHs[h][0:66, QG * qg:QG * (qg + 1)],
                                 start=True, stop=True)
                        inst = e.matmul(dot_ps[:, 512 * half:512 * (half + 1)],
                                        KH[0:64, 128 * kb:128 * (kb + 1)],
                                        QHs[h][0:64, QG * qg:QG * (qg + 1)],
                                        start=True, stop=True)
                    return inst
                p.step("tensor", mms)

                def sln(e):
                    e.activation(bln[:], den_ps[:], AF.Ln, bias=0.0, scale=1.0)
                    return e.activation(rln[:], bln[:], AF.Exp, bias=0.0,
                                        scale=-1.0)
                p.step("scalar", sln)

                def vm(e):
                    e.tensor_tensor(m1[:], dot_ps[:], rln[:], ALU.mult)
                    return e.tensor_tensor(uu[:], m1[:], dot_ps[:], ALU.mult)
                p.step("vector", vm)
                p.step("scalar", lambda e: e.activation(
                    pt[:], uu[:], AF.Exp, bias=nc1c[:], scale=0.25))
                p.step("vector", lambda e, kbp=kbp: e.tensor_tensor(
                    pt[:], pt[:], maskb[:, 1024 * kbp:1024 * (kbp + 1)],
                    ALU.mult))
                for half in range(2):
                    kb = 2 * kbp + half
                    p.dma("sync", vbuf[:, 0:64],
                          gv1_d[128 * kb:128 * (kb + 1), 64 * h:64 * (h + 1)])
                    p.step("tensor", lambda e, kb=kb, half=half: e.matmul(
                        o_ps[:], vbuf[:], pt[:, 512 * half:512 * (half + 1)],
                        start=(kb == 0), stop=(kb == nkb - 1)))
            p.step("vector", lambda e: e.tensor_copy(ohl[:], o_ps[:]))
            def srl(e):
                e.activation(rl[:], ohl[64:65, :], AF.Ln, bias=0.0,
                             scale=18446744073709551616.0)
                return e.activation(rl[:], rl[:], AF.Exp, bias=p64c5[0:1, :],
                                    scale=-1.0)
            p.step("scalar", srl)
            p.step("tensor", lambda e: e.matmul(rlb_ps[:], ones1f64[:], rl[:],
                                                start=True, stop=True))
            p.step("vector", lambda e: e.tensor_tensor(
                aob[:], ohl[0:64, :], rlb_ps[:], ALU.mult))
            p.dma("gpsimd", ao_d[64 * h:64 * (h + 1), QG * qg:QG * (qg + 1)],
                  aob[:])
    p.free(aob, rl, ohl, pt, uu, m1, rln, bln, vbuf, mtmp, maskb, knrow_b,
           knrow, ksq, KH, onesT, *reversed(QHs), taus, qtrue, p64c5, nc1c,
           ones1f64, ones64f)
    p.free(rlb_ps, o_ps, dot_ps, den_ps, qn_ps)


_BUILD_CACHE = {}


def _get(name, builder):
    if name not in _BUILD_CACHE:
        _BUILD_CACHE[name] = builder()
    return _BUILD_CACHE[name]


import ml_dtypes as _mld
_EYEB = np.eye(128, dtype=_mld.bfloat16)
_EYEF = np.eye(128, dtype=np.float32)
_TAUS = (np.arange(128, dtype=np.float32)[:, None]
         + 128.0 * np.arange(64, dtype=np.float32)[None, :])


def kernel(**inputs):
    from concourse.aot_env import allow_ici_all_to_all
    f32 = np.float32
    C = np.ascontiguousarray
    x = np.asarray(inputs["L0_attn_q"]).reshape(16384, D)
    cores = list(range(NCORE))
    nc, _ = _get("mega", build_mega)
    maps = []
    for c in range(NCORE):
        t0 = T * (c % 4)
        qtrueb = np.broadcast_to(
            (t0 + np.arange(T, dtype=f32))[None, :], (128, T))
        selc = np.zeros((128, 4), np.float32)
        selc[:, c % 4] = 1.0
        maps.append({
            "x": C(x[T * c:T * (c + 1)]),
            "selc": selc,
            "wq": inputs["L0_attn_wq"], "wk": inputs["L0_attn_wk"],
            "wv": inputs["L0_attn_wv"],
            "wg": inputs["L0_ffn_w_gate"], "wu": inputs["L0_ffn_w_up"],
            "wd": inputs["L0_ffn_w_down"],
            "wo": inputs["L0_attn_wo"],
            "wqkv": inputs["L1_attn_w_qkv"], "bqkv": inputs["L1_attn_b_qkv"],
            "Wg": inputs["L1_ffn_Wg"],
            "bg": np.asarray(inputs["L1_ffn_bg"]).reshape(1, 8),
            "W1": inputs["L1_ffn_W1"], "b1": inputs["L1_ffn_b1"],
            "W2": inputs["L1_ffn_W2"], "b2": inputs["L1_ffn_b2"],
            "w_out": inputs["L1_attn_w_out"], "b_out": inputs["L1_attn_b_out"],
            "qtrueb": C(qtrueb),
            "eyeb": _EYEB, "eyef": _EYEF, "tausin": _TAUS,
        })
    with allow_ici_all_to_all():
        res = run_bass_kernel_spmd(nc, maps, core_ids=cores)
    out = np.concatenate([np.asarray(r["out"]) for r in res.results], axis=0)
    return out.reshape(2, S, D).astype(np.float32)


# version 06
# speedup vs baseline: 1.1759x; 1.1759x over previous
import numpy as np
import ml_dtypes
import concourse.bass as bass
import concourse.mybir as mybir
from concourse.bass_utils import run_bass_kernel_spmd

F32 = mybir.dt.float32
BF16 = mybir.dt.bfloat16
I32 = mybir.dt.int32
AF = mybir.ActivationFunctionType
ALU = mybir.AluOpType
AX = mybir.AxisListType

C0 = 16.0
C1 = 56.0
LN_EPS = 1e-5
ATT_EPS = 1e-6

NCORE = 8
T = 2048
D = 1024
I0 = 2816
S = 8192
NH = 16
HD = 64


def _nsplit(total, w):
    out = []
    off = 0
    while off < total:
        out.append((off, min(w, total - off)))
        off += w
    return out


class _P:

    def __init__(self):
        self.nc = bass.Bass("TRN2", target_bir_lowering=False, debug=False)
        self.steps = []
        self.out_names = []
        self._ctx = {}
        self._n = 0

    def dram_in(self, name, shape, dt):
        return self.nc.dram_tensor(name, list(shape), dt, kind="ExternalInput").ap()

    def dram_out(self, name, shape, dt):
        self.out_names.append(name)
        return self.nc.dram_tensor(name, list(shape), dt, kind="ExternalOutput").ap()

    def dram_tmp(self, name, shape, dt):
        return self.nc.dram_tensor(name, list(shape), dt, kind="Internal").ap()

    def sbuf(self, shape, dt, name=None):
        self._n += 1
        g = self.nc.sbuf_tensor(name or f"sb{self._n}", list(shape), dt)
        h = g.__enter__()
        self._ctx[id(h)] = g
        return h

    def psum(self, shape, dt, name=None):
        self._n += 1
        g = self.nc.psum_tensor(name or f"ps{self._n}", list(shape), dt)
        h = g.__enter__()
        self._ctx[id(h)] = g
        return h

    def free(self, *handles):
        for h in handles:
            g = self._ctx.pop(id(h))
            g.__exit__(None, None, None)

    def step(self, engine, fn, inc=1, nodrain=False):
        self.steps.append((engine, fn, inc, nodrain))

    DMA_REMAP = False

    def dma(self, engine, out, in_, **kw):
        if self.DMA_REMAP and engine == "gpsimd":
            engine = "sync"
        self.step(engine, lambda e: e.dma_start(out, in_, **kw), inc=16)

    SEG_CAP = 30000

    def finalize(self):
        nc = self.nc
        seg_of = []
        seg_base = [0]
        tot = 0
        for st in self.steps:
            inc = st[2]
            if tot - seg_base[-1] + inc > self.SEG_CAP:
                seg_base.append(tot)
            seg_of.append(len(seg_base) - 1)
            tot += inc
        seg_base.append(tot)
        nseg = len(seg_base) - 1
        sems = [nc.semaphore(f"stepsem{k}").__enter__() for k in range(nseg)]
        waits = []
        t = 0
        for st in self.steps:
            waits.append(t)
            t += st[2]
        blk = nc.Block().__enter__()
        by_eng = {}
        for i, st in enumerate(self.steps):
            by_eng.setdefault(st[0], []).append((i, st[1], st[2], st[3]))
        last_eng = self.steps[-1][0]
        for name in ("sync", "scalar", "vector", "tensor", "gpsimd"):
            if name not in by_eng and name != last_eng:
                continue
            items = by_eng.get(name, [])

            def make(items=items, name=name):
                def body(e):
                    cur = 0
                    for (i, fn, inc, nodrain) in items:
                        sg = seg_of[i]
                        while cur < sg:
                            e.wait_ge(sems[cur], seg_base[cur + 1] - seg_base[cur])
                            cur += 1
                        w = waits[i] - seg_base[sg]
                        if w > 0:
                            e.wait_ge(sems[sg], w)
                        fn(e).then_inc(sems[sg], inc)
                    if name == last_eng:
                        while cur < nseg - 1:
                            e.wait_ge(sems[cur], seg_base[cur + 1] - seg_base[cur])
                            cur += 1
                        e.wait_ge(sems[nseg - 1],
                                  seg_base[nseg] - seg_base[nseg - 1])
                return body
            getattr(blk, name)(make())
        blk.__exit__(None, None, None)
        for g in reversed(list(self._ctx.values())):
            g.__exit__(None, None, None)
        self._ctx.clear()
        return nc


def _ident(p, dt=BF16):
    p._n += 1
    t = p.sbuf([128, 128], dt, f"ident{p._n}")
    p.step("gpsimd", lambda e: e.memset(t[:], 0.0))
    p.step("gpsimd", lambda e: e.affine_select(
        out=t[:], in_=t[:], compare_op=ALU.not_equal, fill=1.0, base=0,
        pattern=[[-1, 128]], channel_multiplier=1))
    return t


def _transpose_tile(p, ident, dst, src_ap_fn, tp_ps, n_kchunk):
    for kc in range(n_kchunk):
        p.step("tensor", lambda e, kc=kc: e.transpose(tp_ps[:], src_ap_fn(kc), ident[:]))
        p.step("vector", lambda e, kc=kc: e.tensor_copy(dst(kc), tp_ps[:]))


class _Quant:

    def __init__(self, p, maxN):
        self.p = p
        self.W = [p.sbuf([128, maxN], F32, f"qW{i}") for i in range(5)]
        nb = maxN // 16
        self.bm = p.sbuf([128, nb], F32, "q_bm")
        self.scale = p.sbuf([128, nb], F32, "q_scale")
        self.lns = p.sbuf([128, nb], F32, "q_lns")
        self.invs = p.sbuf([128, nb], F32, "q_invs")

    def free(self):
        self.p.free(self.invs, self.lns, self.scale, self.bm,
                    *reversed(self.W))

    def emit(self, out_bf, x_f32, N):
        p = self.p
        nb = N // 16
        W1, W2, W3, W4, W5 = self.W
        bm, scale, lns, invs = self.bm, self.scale, self.lns, self.invs
        a3 = lambda t: t[:, 0:N].rearrange("p (b k) -> p b k", k=16)
        bcol = lambda t: t[:, 0:nb].rearrange("p (b k) -> p b k", k=1).broadcast_to(
            [128, nb, 16])

        def v1(e):
            return e.tensor_reduce(
                bm[:, 0:nb].rearrange("p (b k) -> p b k", k=1), a3(x_f32),
                AX.X, ALU.max, apply_absolute_value=True)
        p.step("vector", v1)
        p.step("vector", lambda e: e.tensor_scalar(
            scale[:, 0:nb], bm[:, 0:nb], 1.0 / 6.0, 1e-12, ALU.mult, ALU.max))

        def s1(e):
            e.activation(lns[:, 0:nb], scale[:, 0:nb], AF.Ln, bias=0.0, scale=1.0)
            e.activation(invs[:, 0:nb], lns[:, 0:nb], AF.Exp, bias=0.0, scale=-1.0)
            e.activation(W1[:, 0:N], x_f32, AF.Abs, bias=0.0, scale=1.0)
            return e.activation(W3[:, 0:N], x_f32, AF.Sign, bias=0.0, scale=1.0)
        p.step("scalar", s1)

        def v2(e):
            e.tensor_tensor(a3(W2), a3(W1), bcol(invs), ALU.mult)
            W4i = W4[:, 0:N].bitcast(I32)
            e.tensor_scalar(W4i, W2[:, 0:N], 2.0, None, ALU.mult)
            e.tensor_scalar(W1[:, 0:N], W4i, 0.5, 2.0, ALU.mult, ALU.min)
            e.tensor_scalar(W5[:, 0:N], W2[:, 0:N], 5.0, 2.0, ALU.is_gt, ALU.mult)
            e.tensor_scalar(W4i, W2[:, 0:N], 1.0, None, ALU.mult)
            e.tensor_scalar(W2[:, 0:N], W4i, 2.0, 0.0, ALU.subtract, ALU.max)
            e.scalar_tensor_tensor(W4[:, 0:N], W2[:, 0:N], 2.0, W1[:, 0:N],
                                   ALU.min, ALU.add)
            e.tensor_tensor(W1[:, 0:N], W4[:, 0:N], W5[:, 0:N], ALU.add)
            e.tensor_tensor(W2[:, 0:N], W1[:, 0:N], W3[:, 0:N], ALU.mult)
            return e.tensor_tensor(a3(out_bf), a3(W2), bcol(scale), ALU.mult)
        p.step("vector", v2)


def _ln_tile(p, xn_out, x_f32, cols, eps_col, scr_f32):
    ssum, nmean, sumsq, msq, var, lnv, rstd, nmr = cols
    p.step("vector", lambda e: e.tensor_reduce(ssum[:], x_f32, AX.X, ALU.add))
    p.step("scalar", lambda e: e.activation(
        scr_f32, x_f32, AF.Square, bias=0.0, scale=1.0))
    p.step("vector", lambda e: e.tensor_reduce(sumsq[:], scr_f32, AX.X, ALU.add))

    p.step("vector", lambda e: e.tensor_scalar(nmean[:], ssum[:], -1.0 / D,
                                               None, ALU.mult))
    p.step("scalar", lambda e: e.activation(msq[:], nmean[:], AF.Square,
                                            bias=0.0, scale=1.0))
    p.step("vector", lambda e: e.scalar_tensor_tensor(
        var[:], sumsq[:], 1.0 / D, msq[:], ALU.mult, ALU.subtract))

    def s2(e):
        e.activation(lnv[:], var[:], AF.Ln, bias=eps_col[:], scale=1.0)
        return e.activation(rstd[:], lnv[:], AF.Exp, bias=0.0, scale=-0.5)
    p.step("scalar", s2)
    p.step("vector", lambda e: e.tensor_tensor(nmr[:], nmean[:], rstd[:], ALU.mult))
    p.step("scalar", lambda e: e.activation(xn_out, x_f32, AF.Identity,
                                            bias=nmr[:], scale=rstd[:]))


def _mk_ln_cols(p):
    return tuple(p.sbuf([128, 1], F32, f"ln{i}") for i in range(8))


def build_l0_head():
    p = _P()
    x_d = p.dram_in("x", (T, D), F32)
    wq_d = p.dram_in("wq", (D, D), F32)
    wk_d = p.dram_in("wk", (D, D), F32)
    wv_d = p.dram_in("wv", (D, D), F32)
    wg_d = p.dram_in("wg", (I0, D), F32)
    wu_d = p.dram_in("wu", (I0, D), F32)
    wd_d = p.dram_in("wd", (D, I0), F32)
    qt_d = p.dram_out("QT", (D, T), BF16)
    kt_d = p.dram_out("KT", (D, T), BF16)
    v_d = p.dram_out("Vtok", (T, D), BF16)
    g_d = p.dram_out("g", (T, D), F32)
    hq_d = p.dram_tmp("hqT", (I0, T), BF16)

    ident = _ident(p)
    eps_col = p.sbuf([128, 1], F32, "lneps")
    p.step("vector", lambda e: e.memset(eps_col[:], LN_EPS))
    ln_cols = _mk_ln_cols(p)
    q = _Quant(p, I0)
    tp_ps = p.psum([128, 128], BF16, "tp")
    mm_ps = p.psum([128, 512], F32, "mm")

    ntile = T // 128
    xnqT = p.sbuf([128, 8 * T], BF16, "xnqT")
    xnT = p.sbuf([128, 8 * T], BF16, "xnT")
    TC = lambda buf, c: buf[:, c * T:(c + 1) * T]

    xf = p.sbuf([128, D], F32, "xf")
    xn = p.sbuf([128, D], F32, "xn")
    xnb = p.sbuf([128, D], BF16, "xnb")
    xnqb = p.sbuf([128, D], BF16, "xnqb")
    for t in range(ntile):
        p.dma("sync", xf[:], x_d[128 * t:128 * (t + 1), :])
        _ln_tile(p, xn[:], xf[:], ln_cols, eps_col, q.W[0][:, 0:D])
        p.step("vector", lambda e: e.tensor_copy(xnb[:], xn[:]))
        q.emit(xnqb[:], xn[:], D)
        _transpose_tile(p, ident,
                        lambda kc, t=t: TC(xnT, kc)[:, 128 * t:128 * (t + 1)],
                        lambda kc: xnb[:, 128 * kc:128 * (kc + 1)], tp_ps, 8)
        _transpose_tile(p, ident,
                        lambda kc, t=t: TC(xnqT, kc)[:, 128 * t:128 * (t + 1)],
                        lambda kc: xnqb[:, 128 * kc:128 * (kc + 1)], tp_ps, 8)
    p.free(xnqb, xnb, xn, xf)

    wrow = p.sbuf([128, D], F32, "wrow")
    wrb = p.sbuf([128, D], BF16, "wrb")
    wT = p.sbuf([128, 8 * D], BF16, "wT")
    WC = lambda c: wT[:, c * D:(c + 1) * D]
    outb = p.sbuf([128, T], BF16, "outb")
    for wsrc, wdst, scalef in ((wq_d, qt_d, 0.125), (wk_d, kt_d, 1.0)):
        for r in range(8):
            p.dma("sync", wrow[:], wsrc[128 * r:128 * (r + 1), :])
            p.step("vector", lambda e, s=scalef: e.tensor_scalar(
                wrb[:], wrow[:], s, None, ALU.mult))
            _transpose_tile(p, ident,
                            lambda kc, r=r: WC(kc)[:, 128 * r:128 * (r + 1)],
                            lambda kc: wrb[:, 128 * kc:128 * (kc + 1)], tp_ps, 8)
        for ec in range(8):
            for tg in range(T // 512):
                def mms(e, ec=ec, tg=tg):
                    for dc in range(8):
                        inst = e.matmul(mm_ps[:],
                                        WC(dc)[:, 128 * ec:128 * (ec + 1)],
                                        TC(xnT, dc)[:, 512 * tg:512 * (tg + 1)],
                                        start=(dc == 0), stop=(dc == 7))
                    return inst
                p.step("tensor", mms)
                p.step("vector", lambda e, tg=tg: e.tensor_copy(
                    outb[:, 512 * tg:512 * (tg + 1)], mm_ps[:]))
            p.dma("gpsimd", wdst[128 * ec:128 * (ec + 1), :], outb[:])
    for r in range(8):
        p.dma("sync", wrow[:], wv_d[128 * r:128 * (r + 1), :])
        p.step("vector", lambda e: e.tensor_copy(wrb[:], wrow[:]))
        _transpose_tile(p, ident,
                        lambda kc, r=r: WC(kc)[:, 128 * r:128 * (r + 1)],
                        lambda kc: wrb[:, 128 * kc:128 * (kc + 1)], tp_ps, 8)
    for t in range(ntile):
        for eg in range(2):
            def mmv(e, t=t, eg=eg):
                for dc in range(8):
                    inst = e.matmul(mm_ps[:],
                                    TC(xnT, dc)[:, 128 * t:128 * (t + 1)],
                                    WC(dc)[:, 512 * eg:512 * (eg + 1)],
                                    start=(dc == 0), stop=(dc == 7))
                return inst
            p.step("tensor", mmv)
            p.step("vector", lambda e, eg=eg: e.tensor_copy(
                outb[:, 512 * eg:512 * (eg + 1)], mm_ps[:]))
        p.dma("gpsimd", v_d[128 * t:128 * (t + 1), :], outb[:, 0:D])
    p.free(outb, wT, wrb, wrow, xnT)

    wgqT = p.sbuf([128, 8 * I0], BF16, "wgqT")
    wuqT = p.sbuf([128, 8 * I0], BF16, "wuqT")
    WGC = lambda buf, c: buf[:, c * I0:(c + 1) * I0]
    wrow2 = p.sbuf([128, D], F32, "wrow2")
    wqb = p.sbuf([128, D], BF16, "wqb")
    for wsrc, wbuf in ((wg_d, wgqT), (wu_d, wuqT)):
        for r in range(I0 // 128):
            p.dma("sync", wrow2[:], wsrc[128 * r:128 * (r + 1), :])
            q.emit(wqb[:], wrow2[:], D)
            _transpose_tile(p, ident,
                            lambda kc, r=r, wbuf=wbuf:
                            WGC(wbuf, kc)[:, 128 * r:128 * (r + 1)],
                            lambda kc: wqb[:, 128 * kc:128 * (kc + 1)], tp_ps, 8)
    gate_ps = p.psum([128, 512], F32, "gate")
    up_ps = p.psum([128, 512], F32, "up")
    sgate = q.W[1]
    hbuf = p.sbuf([128, I0], F32, "hbuf")
    hqb = p.sbuf([128, I0], BF16, "hqb")
    hpc = p.sbuf([128, 128], BF16, "hpc")
    for t in range(ntile):
        for off, w in _nsplit(I0, 512):
            def mmg(e, t=t, off=off, w=w):
                for dc in range(8):
                    e.matmul(gate_ps[:, 0:w],
                             TC(xnqT, dc)[:, 128 * t:128 * (t + 1)],
                             WGC(wgqT, dc)[:, off:off + w],
                             start=(dc == 0), stop=(dc == 7))
                for dc in range(8):
                    inst = e.matmul(up_ps[:, 0:w],
                                    TC(xnqT, dc)[:, 128 * t:128 * (t + 1)],
                                    WGC(wuqT, dc)[:, off:off + w],
                                    start=(dc == 0), stop=(dc == 7))
                return inst
            p.step("tensor", mmg)
            p.step("scalar", lambda e, off=off, w=w: e.activation(
                sgate[:, off:off + w], gate_ps[:, 0:w], AF.Silu, bias=0.0, scale=1.0))
            p.step("vector", lambda e, off=off, w=w: e.tensor_tensor(
                hbuf[:, off:off + w], sgate[:, off:off + w], up_ps[:, 0:w], ALU.mult))
        q.emit(hqb[:], hbuf[:], I0)
        for kc in range(I0 // 128):
            p.step("tensor", lambda e, kc=kc: e.transpose(
                tp_ps[:], hqb[:, 128 * kc:128 * (kc + 1)], ident[:]))
            p.step("vector", lambda e: e.tensor_copy(hpc[:], tp_ps[:]))
            p.dma("gpsimd",
                  hq_d[128 * kc:128 * (kc + 1), 128 * t:128 * (t + 1)], hpc[:])
    p.free(hpc, hqb, hbuf, wqb, wrow2, wuqT, wgqT, xnqT)

    wdqT = p.sbuf([128, 22 * D], BF16, "wdqT")
    wdrow = p.sbuf([128, I0], F32, "wdrow")
    wdqb = p.sbuf([128, I0], BF16, "wdqb")
    for r in range(8):
        p.dma("sync", wdrow[:], wd_d[128 * r:128 * (r + 1), :])
        q.emit(wdqb[:], wdrow[:], I0)
        for kc in range(I0 // 128):
            p.step("tensor", lambda e, kc=kc: e.transpose(
                tp_ps[:], wdqb[:, 128 * kc:128 * (kc + 1)], ident[:]))
            p.step("vector", lambda e, kc=kc, r=r: e.tensor_copy(
                wdqT[:, kc * D + 128 * r: kc * D + 128 * (r + 1)], tp_ps[:]))
    p.free(wdqb, wdrow)
    hqT_s = p.sbuf([128, 22 * T], BF16, "hqTs")
    for kc in range(I0 // 128):
        p.dma("sync", hqT_s[:, kc * T:(kc + 1) * T], hq_d[128 * kc:128 * (kc + 1), :])
    gout = p.sbuf([128, D], F32, "gout")
    for t in range(ntile):
        for eg in range(2):
            def mmd(e, t=t, eg=eg):
                for kc in range(22):
                    inst = e.matmul(mm_ps[:],
                                    hqT_s[:, kc * T + 128 * t: kc * T + 128 * (t + 1)],
                                    wdqT[:, kc * D + 512 * eg: kc * D + 512 * (eg + 1)],
                                    start=(kc == 0), stop=(kc == 21))
                return inst
            p.step("tensor", mmd)
            p.step("scalar", lambda e, eg=eg: e.activation(
                gout[:, 512 * eg:512 * (eg + 1)], mm_ps[:], AF.Silu,
                bias=0.0, scale=1.0))
        p.dma("gpsimd", g_d[128 * t:128 * (t + 1), :], gout[:])
    return p.finalize(), p.out_names


def build_l0_attn():
    p = _P()
    qt_d = p.dram_in("QT", (D, T), BF16)
    kt_d = p.dram_in("KTb", (D, S), BF16)
    v_d = p.dram_in("Vb", (S, D), BF16)
    ao_d = p.dram_out("AOT", (D, T), BF16)

    KSUP = 1024
    QG = 512
    nks = S // KSUP
    nqg = T // QG

    qt_s = p.sbuf([128, 8 * T], BF16, "qts")
    TC = lambda buf, c, w=T: buf[:, c * w:(c + 1) * w]
    for c in range(8):
        p.dma("sync", TC(qt_s, c), qt_d[128 * c:128 * (c + 1), :])
    oacc = p.sbuf([128, 8 * T], F32, "oacc")
    p.step("vector", lambda e: e.memset(oacc[:], 0.0))
    lacc = p.sbuf([1, T], F32, "lacc")
    p.step("vector", lambda e: e.memset(lacc[:], 0.0))
    ones_col = p.sbuf([128, 1], BF16, "ones")
    p.step("vector", lambda e: e.memset(ones_col[:], 1.0))
    nc0 = p.sbuf([128, 1], F32, "nc0")
    p.step("vector", lambda e: e.memset(nc0[:], -C0))

    kt_s = p.sbuf([128, 8 * KSUP], BF16, "kts")
    v_s = p.sbuf([128, 8 * D], BF16, "vs")
    pt_s = p.sbuf([128, 8 * QG], BF16, "pts")
    s_ps = p.psum([128, QG], F32, "sps")
    o_ps = p.psum([128, QG], F32, "ops")
    l_ps = p.psum([1, QG], F32, "lps")

    for ks in range(nks):
        for c in range(8):
            p.dma("sync", TC(kt_s, c, KSUP),
                  kt_d[128 * c:128 * (c + 1), KSUP * ks:KSUP * (ks + 1)])
        for kb in range(8):
            p.dma("sync", TC(v_s, kb, D),
                  v_d[KSUP * ks + 128 * kb:KSUP * ks + 128 * (kb + 1), :])
        for qg in range(nqg):
            for kb in range(8):
                def mmS(e, kb=kb, qg=qg):
                    for dc in range(8):
                        inst = e.matmul(
                            s_ps[:],
                            TC(kt_s, dc, KSUP)[:, 128 * kb:128 * (kb + 1)],
                            TC(qt_s, dc)[:, QG * qg:QG * (qg + 1)],
                            start=(dc == 0), stop=(dc == 7))
                    return inst
                p.step("tensor", mmS)
                p.step("scalar", lambda e, kb=kb: e.activation(
                    TC(pt_s, kb, QG), s_ps[:], AF.Exp, bias=nc0[:], scale=1.0))
            def mmL(e):
                for kb in range(8):
                    inst = e.matmul(l_ps[:], ones_col[:],
                                    TC(pt_s, kb, QG),
                                    start=(kb == 0), stop=(kb == 7))
                return inst
            p.step("tensor", mmL)
            p.step("vector", lambda e, qg=qg: e.tensor_tensor(
                lacc[:, QG * qg:QG * (qg + 1)], lacc[:, QG * qg:QG * (qg + 1)],
                l_ps[:], ALU.add))
            for ec in range(8):
                def mmPV(e, ec=ec):
                    for kb in range(8):
                        inst = e.matmul(o_ps[:],
                                        TC(v_s, kb, D)[:, 128 * ec:128 * (ec + 1)],
                                        TC(pt_s, kb, QG),
                                        start=(kb == 0), stop=(kb == 7))
                    return inst
                p.step("tensor", mmPV)
                p.step("vector", lambda e, ec=ec, qg=qg: e.tensor_tensor(
                    TC(oacc, ec)[:, QG * qg:QG * (qg + 1)],
                    TC(oacc, ec)[:, QG * qg:QG * (qg + 1)], o_ps[:], ALU.add))

    rl = p.sbuf([1, T], F32, "rl")
    p64c = p.sbuf([1, 1], F32, "p64c")
    p.step("vector", lambda e: e.memset(p64c[:], 44.3614195558365))
    def srl(e):
        e.activation(rl[:], lacc[:], AF.Ln, bias=0.0,
                     scale=18446744073709551616.0)
        return e.activation(rl[:], rl[:], AF.Exp, bias=p64c[0:1, :], scale=-1.0)
    p.step("scalar", srl)
    rlb_ps = p.psum([128, QG], F32, "rlb")
    rlb = p.sbuf([128, T], F32, "rlbs")
    ones1 = p.sbuf([1, 128], F32, "ones1")
    p.step("vector", lambda e: e.memset(ones1[:], 1.0))
    for qg in range(nqg):
        p.step("tensor", lambda e, qg=qg: e.matmul(
            rlb_ps[:], ones1[:], rl[:, QG * qg:QG * (qg + 1)],
            start=True, stop=True))
        p.step("vector", lambda e, qg=qg: e.tensor_copy(
            rlb[:, QG * qg:QG * (qg + 1)], rlb_ps[:]))
    aob = p.sbuf([128, T], BF16, "aob")
    for ec in range(8):
        p.step("vector", lambda e, ec=ec: e.tensor_tensor(
            aob[:], TC(oacc, ec), rlb[:], ALU.mult))
        p.dma("gpsimd", ao_d[128 * ec:128 * (ec + 1), :], aob[:])
    return p.finalize(), p.out_names


def build_l0_tail_l1_head():
    p = _P()
    spt_d = p.dram_in("SPT", (D, T), BF16)
    xt_d = p.dram_in("xT", (D, T), F32)
    gt_d = p.dram_in("gT", (D, T), F32)
    wo_d = p.dram_in("wo", (D, D), F32)
    wqkv_d = p.dram_in("wqkv", (D, 3 * D), F32)
    bqkv_d = p.dram_in("bqkv", (3 * D,), F32)
    wg_d = p.dram_in("Wg", (D, 8), F32)
    bg_d = p.dram_in("bg", (1, 8), F32)
    w1_d = p.dram_in("W1", (8, D, D), F32)
    b1_d = p.dram_in("b1", (8, D), F32)
    w2_d = p.dram_in("W2", (8, D, D), F32)
    b2_d = p.dram_in("b2", (8, D), F32)
    x1_d = p.dram_out("x1", (T, D), F32)
    g1_d = p.dram_out("g1", (T, D), F32)
    q1_d = p.dram_out("Q1T", (D, T), BF16)
    k1_d = p.dram_out("K1T", (D, T), BF16)
    v1_d = p.dram_out("V1tok", (T, D), BF16)

    ident = _ident(p)
    identf = _ident(p, F32)
    ntile = T // 128
    TC = lambda buf, c, w=T: buf[:, c * w:(c + 1) * w]
    tp_ps = p.psum([128, 128], BF16, "tp")
    tpf_ps = p.psum([128, 128], F32, "tpf")
    mm_ps = p.psum([128, 512], F32, "mm")

    eps_col = p.sbuf([128, 1], F32, "lneps")
    p.step("vector", lambda e: e.memset(eps_col[:], LN_EPS))
    ln_cols = _mk_ln_cols(p)
    ntile_ = T // 128
    selT = p.sbuf([128, 8 * ntile_], F32, "selT")
    xn1T = p.sbuf([128, 8 * T], BF16, "xn1T")
    xn1loT = p.sbuf([128, 8 * T], BF16, "xn1loT")

    x1T = p.sbuf([128, 8 * T], F32, "x1T")
    spt_s = p.sbuf([128, 8 * T], BF16, "spts")
    for c in range(8):
        p.dma("sync", TC(spt_s, c), spt_d[128 * c:128 * (c + 1), :])
    woT = p.sbuf([128, 8 * D], BF16, "woT")
    wrow = p.sbuf([128, D], F32, "wrow_a")
    wrb = p.sbuf([128, D], BF16, "wrb_a")
    for r in range(8):
        p.dma("sync", wrow[:], wo_d[128 * r:128 * (r + 1), :])
        p.step("vector", lambda e: e.tensor_copy(wrb[:], wrow[:]))
        _transpose_tile(p, ident,
                        lambda kc, r=r: TC(woT, kc, D)[:, 128 * r:128 * (r + 1)],
                        lambda kc: wrb[:, 128 * kc:128 * (kc + 1)], tp_ps, 8)
    xg_x = p.sbuf([128, T], F32, "xgx")
    xg_g = p.sbuf([128, T], F32, "xgg")
    stmp = p.sbuf([128, 512], F32, "stmp")
    for ec in range(8):
        p.dma("sync", xg_x[:], xt_d[128 * ec:128 * (ec + 1), :])
        p.dma("sync", xg_g[:], gt_d[128 * ec:128 * (ec + 1), :])
        for tg in range(T // 512):
            def mmS(e, ec=ec, tg=tg):
                for jc in range(8):
                    inst = e.matmul(mm_ps[:],
                                    TC(woT, jc, D)[:, 128 * ec:128 * (ec + 1)],
                                    TC(spt_s, jc)[:, 512 * tg:512 * (tg + 1)],
                                    start=(jc == 0), stop=(jc == 7))
                return inst
            p.step("tensor", mmS)
            p.step("vector", lambda e, tg=tg: e.scalar_tensor_tensor(
                stmp[:], mm_ps[:], 1.0, xg_g[:, 512 * tg:512 * (tg + 1)],
                ALU.mult, ALU.mult))
            p.step("vector", lambda e, ec=ec, tg=tg: e.tensor_tensor(
                TC(x1T, ec)[:, 512 * tg:512 * (tg + 1)], stmp[:],
                xg_x[:, 512 * tg:512 * (tg + 1)], ALU.add))
    p.free(stmp, xg_g, xg_x, wrb, wrow, woT, spt_s)

    x1t_f = p.sbuf([128, D], F32, "x1tf")
    xn1_f = p.sbuf([128, D], F32, "xn1f")
    scrf = p.sbuf([128, D], F32, "scrf")
    xn1b = p.sbuf([128, D], BF16, "xn1b")
    xn1lo = p.sbuf([128, D], BF16, "xn1lo")
    for t in range(ntile):
        for kc in range(8):
            p.step("tensor", lambda e, kc=kc, t=t: e.transpose(
                tpf_ps[:], TC(x1T, kc)[:, 128 * t:128 * (t + 1)], identf[:]))
            p.step("vector", lambda e, kc=kc: e.tensor_copy(
                x1t_f[:, 128 * kc:128 * (kc + 1)], tpf_ps[:]))
        p.dma("gpsimd", x1_d[128 * t:128 * (t + 1), :], x1t_f[:])
        _ln_tile(p, xn1_f[:], x1t_f[:], ln_cols, eps_col, scrf[:])
        p.step("vector", lambda e: e.tensor_copy(xn1b[:], xn1_f[:]))
        p.step("vector", lambda e: e.tensor_sub(xn1lo[:], xn1_f[:], xn1b[:]))
        _transpose_tile(p, ident,
                        lambda kc, t=t: TC(xn1T, kc)[:, 128 * t:128 * (t + 1)],
                        lambda kc: xn1b[:, 128 * kc:128 * (kc + 1)], tp_ps, 8)
        _transpose_tile(p, ident,
                        lambda kc, t=t: TC(xn1loT, kc)[:, 128 * t:128 * (t + 1)],
                        lambda kc: xn1lo[:, 128 * kc:128 * (kc + 1)], tp_ps, 8)
    p.free(xn1lo, xn1b, scrf, xn1_f, x1t_f)
    p.free(x1T)

    wqkv_s = p.sbuf([128, 8 * 3 * D], BF16, "wqkvs")
    W3C = lambda c: wqkv_s[:, c * 3 * D:(c + 1) * 3 * D]
    wrow3 = p.sbuf([128, 3 * D], F32, "wrow3")
    for r in range(8):
        p.dma("sync", wrow3[:], wqkv_d[128 * r:128 * (r + 1), :])
        def cv(e, r=r):
            e.tensor_copy(W3C(r)[:, 0:2 * D], wrow3[:, 0:2 * D])
            return e.tensor_scalar(W3C(r)[:, D:2 * D], wrow3[:, D:2 * D],
                                   -2.0, None, ALU.mult)
        def cv2(e, r=r):
            return e.tensor_copy(W3C(r)[:, 2 * D:3 * D], wrow3[:, 2 * D:3 * D])
        p.step("vector", cv)
        p.step("vector", cv2)
    bq_s = p.sbuf([128, 16], F32, "bqs")
    for ec in range(8):
        p.dma("sync", bq_s[:, ec:ec + 1], bqkv_d[128 * ec:128 * (ec + 1)].unsqueeze(1))
    for ec in range(8):
        p.dma("sync", bq_s[:, 8 + ec:9 + ec],
              bqkv_d[D + 128 * ec:D + 128 * (ec + 1)].unsqueeze(1))
    p.step("vector", lambda e: e.tensor_scalar(bq_s[:, 8:16], bq_s[:, 8:16],
                                               -2.0, None, ALU.mult))
    outqb = p.sbuf([128, T], BF16, "outqb")
    for (dst, coff, bcol_off) in ((q1_d, 0, 0), (k1_d, D, 8)):
        for ec in range(8):
            for tg in range(T // 512):
                def mmq(e, ec=ec, tg=tg, coff=coff):
                    for dc in range(8):
                        inst = e.matmul(
                            mm_ps[:],
                            W3C(dc)[:, coff + 128 * ec:coff + 128 * (ec + 1)],
                            TC(xn1T, dc)[:, 512 * tg:512 * (tg + 1)],
                            start=(dc == 0), stop=(dc == 7))
                    return inst
                p.step("tensor", mmq)
                p.step("scalar", lambda e, tg=tg, ec=ec, bo=bcol_off: e.activation(
                    outqb[:, 512 * tg:512 * (tg + 1)], mm_ps[:], AF.Identity,
                    bias=bq_s[:, bo + ec:bo + ec + 1], scale=1.0))
            p.dma("gpsimd", dst[128 * ec:128 * (ec + 1), :], outqb[:])
    onesrow = p.sbuf([1, 128], BF16, "onesr")
    p.step("vector", lambda e: e.memset(onesrow[:], 1.0))
    bvrow = p.sbuf([1, D], F32, "bvrow")
    p.dma("sync", bvrow[:], bqkv_d[2 * D:3 * D].unsqueeze(0))
    bvrow_b = p.sbuf([1, D], BF16, "bvrowb")
    p.step("vector", lambda e: e.tensor_copy(bvrow_b[:], bvrow[:]))
    for t in range(ntile):
        for eg in range(2):
            def mmv(e, t=t, eg=eg):
                for dc in range(8):
                    e.matmul(mm_ps[:],
                             TC(xn1T, dc)[:, 128 * t:128 * (t + 1)],
                             W3C(dc)[:, 2 * D + 512 * eg:2 * D + 512 * (eg + 1)],
                             start=(dc == 0), stop=False)
                return e.matmul(mm_ps[:], onesrow[:],
                                bvrow_b[:, 512 * eg:512 * (eg + 1)],
                                start=False, stop=True)
            p.step("tensor", mmv)
            p.step("vector", lambda e, eg=eg: e.tensor_copy(
                outqb[:, 512 * eg:512 * (eg + 1)], mm_ps[:]))
        p.dma("gpsimd", v1_d[128 * t:128 * (t + 1), :], outqb[:, 0:D])
    p.free(bvrow_b, bvrow, onesrow, outqb, bq_s, wrow3, wqkv_s)

    wg_s = p.sbuf([128, 8 * 8], F32, "wgs")
    for dc in range(8):
        p.dma("sync", wg_s[:, 8 * dc:8 * (dc + 1)], wg_d[128 * dc:128 * (dc + 1), :])
    wg_hi = p.sbuf([128, 8 * 8], BF16, "wghi")
    wg_lo = p.sbuf([128, 8 * 8], BF16, "wglo")
    p.step("vector", lambda e: e.tensor_copy(wg_hi[:], wg_s[:]))
    p.step("vector", lambda e: e.tensor_sub(wg_lo[:], wg_s[:], wg_hi[:]))
    lg_ps = p.psum([8, 512], F32, "lgps")
    lgT = p.sbuf([8, T], F32, "lgT")
    for tg in range(T // 512):
        def mmlg(e, tg=tg):
            combos = []
            for dc in range(8):
                combos.append((wg_hi[:, 8 * dc:8 * (dc + 1)], TC(xn1T, dc)))
                combos.append((wg_lo[:, 8 * dc:8 * (dc + 1)], TC(xn1T, dc)))
                combos.append((wg_hi[:, 8 * dc:8 * (dc + 1)], TC(xn1loT, dc)))
            for i, (wb, xb) in enumerate(combos):
                inst = e.matmul(lg_ps[:], wb, xb[:, 512 * tg:512 * (tg + 1)],
                                start=(i == 0), stop=(i == len(combos) - 1))
            return inst
        p.step("tensor", mmlg)
        p.step("vector", lambda e, tg=tg: e.tensor_copy(
            lgT[:, 512 * tg:512 * (tg + 1)], lg_ps[:]))
    bg_s = p.sbuf([1, 8], F32, "bgs")
    p.dma("sync", bg_s[:], bg_d[:])
    ones1f = p.sbuf([1, 128], F32, "ones1f")
    p.step("vector", lambda e: e.memset(ones1f[:], 1.0))
    bgb_ps = p.psum([128, 8], F32, "bgb")
    bgb = p.sbuf([128, 8], F32, "bgbs")
    p.step("tensor", lambda e: e.matmul(bgb_ps[:], ones1f[:], bg_s[:],
                                        start=True, stop=True))
    p.step("vector", lambda e: e.tensor_copy(bgb[:], bgb_ps[:]))
    lgt_f = p.sbuf([128, 8], F32, "lgtf")
    mcol = p.sbuf([128, 1], F32, "mcol")
    for t in range(ntile):
        p.step("tensor", lambda e, t=t: e.transpose(
            tpf_ps[:, 0:8], lgT[:, 128 * t:128 * (t + 1)], identf[0:8, 0:8]))
        p.step("vector", lambda e: e.tensor_copy(lgt_f[:], tpf_ps[:, 0:8]))
        p.step("vector", lambda e: e.tensor_tensor(lgt_f[:], lgt_f[:], bgb[:], ALU.add))
        p.step("vector", lambda e: e.tensor_reduce(mcol[:], lgt_f[:], AX.X, ALU.max))
        p.step("vector", lambda e, t=t: e.tensor_scalar(
            selT[:, 8 * t:8 * (t + 1)], lgt_f[:], mcol[:], None, ALU.is_ge))
    p.free(mcol, lgt_f, bgb, ones1f, bg_s, lgT, wg_lo, wg_hi, wg_s)
    p.free(xn1loT)

    yacc = p.sbuf([128, ntile * D], F32, "yacc")
    p.step("vector", lambda e: e.memset(yacc[:], 0.0))
    heT = p.sbuf([128, 8 * T], BF16, "heT")
    w1b = p.sbuf([128, 8 * D], BF16, "w1b")
    w2b = p.sbuf([128, 8 * D], BF16, "w2b")
    wrowm = p.sbuf([128, D], F32, "wrowm")
    b1c = p.sbuf([128, 8], F32, "b1c")
    b2row = p.sbuf([1, D], F32, "b2row")
    b2row_b = p.sbuf([1, D], BF16, "b2rowb")
    onesrow2 = p.sbuf([1, 128], BF16, "onesr2")
    p.step("vector", lambda e: e.memset(onesrow2[:], 1.0))
    ytmp = p.sbuf([128, 512], F32, "ytmp")
    for ex in range(8):
        for r in range(8):
            p.dma("sync", wrowm[:], w1_d[ex, 128 * r:128 * (r + 1), :])
            p.step("vector", lambda e, r=r: e.tensor_copy(
                w1b[:, D * r:D * (r + 1)], wrowm[:]))
        for r in range(8):
            p.dma("sync", wrowm[:], w2_d[ex, 128 * r:128 * (r + 1), :])
            p.step("vector", lambda e, r=r: e.tensor_copy(
                w2b[:, D * r:D * (r + 1)], wrowm[:]))
        for ic in range(8):
            p.dma("sync", b1c[:, ic:ic + 1],
                  b1_d[ex, 128 * ic:128 * (ic + 1)].unsqueeze(1))
        p.dma("sync", b2row[:], b2_d[ex:ex + 1, :])
        p.step("vector", lambda e: e.tensor_copy(b2row_b[:], b2row[:]))
        for ic in range(8):
            for tg in range(T // 512):
                def mmh(e, ic=ic, tg=tg):
                    for dc in range(8):
                        inst = e.matmul(
                            mm_ps[:],
                            w1b[:, D * dc + 128 * ic:D * dc + 128 * (ic + 1)],
                            TC(xn1T, dc)[:, 512 * tg:512 * (tg + 1)],
                            start=(dc == 0), stop=(dc == 7))
                    return inst
                p.step("tensor", mmh)
                p.step("scalar", lambda e, ic=ic, tg=tg: e.activation(
                    TC(heT, ic)[:, 512 * tg:512 * (tg + 1)], mm_ps[:],
                    AF.Gelu, bias=b1c[:, ic:ic + 1], scale=1.0))
        for t in range(ntile):
            for eg in range(2):
                def mmy(e, t=t, eg=eg):
                    for ic in range(8):
                        e.matmul(mm_ps[:],
                                 TC(heT, ic)[:, 128 * t:128 * (t + 1)],
                                 w2b[:, D * ic + 512 * eg:D * ic + 512 * (eg + 1)],
                                 start=(ic == 0), stop=False)
                    return e.matmul(mm_ps[:], onesrow2[:],
                                    b2row_b[:, 512 * eg:512 * (eg + 1)],
                                    start=False, stop=True)
                p.step("tensor", mmy)
                p.step("vector", lambda e, t=t, eg=eg, ex=ex: e.tensor_scalar(
                    ytmp[:], mm_ps[:], selT[:, 8 * t + ex:8 * t + ex + 1],
                    None, ALU.mult))
                p.step("vector", lambda e, t=t, eg=eg: e.tensor_tensor(
                    yacc[:, D * t + 512 * eg:D * t + 512 * (eg + 1)],
                    yacc[:, D * t + 512 * eg:D * t + 512 * (eg + 1)],
                    ytmp[:], ALU.add))
    g1b = p.sbuf([128, D], F32, "g1b")
    for t in range(ntile):
        p.step("scalar", lambda e, t=t: e.activation(
            g1b[:], yacc[:, D * t:D * (t + 1)], AF.Silu, bias=0.0, scale=1.0))
        p.dma("gpsimd", g1_d[128 * t:128 * (t + 1), :], g1b[:])
    return p.finalize(), p.out_names


_DBG = False


def build_l1_attn(nheads=4):
    p = _P()
    HB = nheads * 64
    qh_d = p.dram_in("QHT", (HB, T), BF16)
    kh_d = p.dram_in("KHT", (HB, S), BF16)
    v_d = p.dram_in("Vb", (S, HB), BF16)
    qt_d = p.dram_in("qtrueb", (128, T), F32)
    selc_d = p.dram_in("selc", (128, 4), F32)
    eyeb_d = p.dram_in("eyeb", (128, 128), BF16)
    eyef_d = p.dram_in("eyef", (128, 128), F32)
    taus_d = p.dram_in("tausin", (128, S // 128), F32)
    ao_d = p.dram_out("AOTp", (HB, T), BF16)
    khscr = p.dram_tmp("khscr", (nheads, 66, S), BF16)
    dbg = p.dram_out("DBG", (66, 512), F32) if _DBG else None
    dbg2 = p.dram_out("DBG2", (128, 1024), F32) if _DBG else None

    QG = 512
    nqg = T // QG
    nkb = S // 128

    identf = _ident(p, F32)
    ones64f = p.sbuf([64, 1], F32, "o64")
    p.step("vector", lambda e: e.memset(ones64f[:], 1.0))
    ones1f = p.sbuf([1, 64], F32, "o1f")
    p.step("vector", lambda e: e.memset(ones1f[:], 1.0))
    nc1 = p.sbuf([128, 1], F32, "nc1")
    p.step("vector", lambda e: e.memset(nc1[:], -C1))
    qtrue = p.sbuf([128, T], F32, "qtrue")
    p.dma("sync", qtrue[:], qt_d[:])
    taus = p.sbuf([128, nkb], F32, "taus")
    p.dma("sync", taus[:], taus_d[:])

    QHs = [p.sbuf([66, T], BF16, f"QH{h}") for h in range(nheads)]
    onesT = p.sbuf([1, T], BF16, "onesT")
    p.step("vector", lambda e: e.memset(onesT[:], 1.0))
    qsq = p.sbuf([64, T], F32, "qsq")
    qn_ps = p.psum([1, QG], F32, "qnps")
    for h in range(nheads):
        p.dma("sync", QHs[h][0:64, :], qh_d[64 * h:64 * (h + 1), :])
        p.step("scalar", lambda e, h=h: e.activation(
            qsq[:], QHs[h][0:64, :], AF.Square, bias=0.0, scale=1.0))
        p.dma("gpsimd", QHs[h][65:66, :], onesT[:, 0:T])
        for tg in range(nqg):
            p.step("tensor", lambda e, tg=tg: e.matmul(
                qn_ps[:], ones64f[:], qsq[:, QG * tg:QG * (tg + 1)],
                start=True, stop=True))
            p.step("vector", lambda e, h=h, tg=tg: e.tensor_copy(
                QHs[h][64:65, QG * tg:QG * (tg + 1)], qn_ps[:]))
    p.free(qsq)

    KH = p.sbuf([66, S], BF16, "KH")
    ksq = p.sbuf([64, 512], F32, "ksq")
    kn_ps = qn_ps
    knrow = p.sbuf([1, 512], F32, "knrow")
    knrow_b = p.sbuf([1, S], BF16, "knrowb")
    maskb = p.sbuf([128, nkb * 512], BF16, "maskb")
    mtmp = p.sbuf([128, 512], F32, "mtmp")
    vbuf = p.sbuf([128, 65], BF16, "vbuf")
    p.step("vector", lambda e: e.memset(vbuf[:, 64:65], 1.0))
    den_ps = p.psum([128, 1024], F32, "denps")
    dot_ps = p.psum([128, 1024], F32, "dotps")
    o_ps = p.psum([65, QG], F32, "ops")
    rlb_ps = p.psum([64, QG], F32, "rlbps")
    bln = p.sbuf([128, 1024], F32, "bln")
    rln = p.sbuf([128, 1024], F32, "rln")
    m1 = p.sbuf([128, 1024], F32, "m1")
    uu = p.sbuf([128, 1024], F32, "uu")
    pt = p.sbuf([128, 1024], BF16, "pt")
    ohl = p.sbuf([65, QG], F32, "ohl")
    dbg2s = p.sbuf([128, 1024], F32, "dbg2s") if _DBG else None
    rl = p.sbuf([1, QG], F32, "rl")
    p64c = p.sbuf([1, 1], F32, "p64c")
    p.step("vector", lambda e: e.memset(p64c[:], 44.3614195558365))
    aob = p.sbuf([64, QG], BF16, "aob")

    for qg in range(nqg):
        for kb in range(nkb):
            p.step("vector", lambda e, kb=kb, qg=qg: e.tensor_scalar(
                mtmp[:], qtrue[:, QG * qg:QG * (qg + 1)], taus[:, kb:kb + 1],
                None, ALU.is_ge))
            p.step("vector", lambda e, kb=kb: e.tensor_copy(
                maskb[:, 512 * kb:512 * (kb + 1)], mtmp[:]))
        for h in range(nheads):
            if qg == 0:
                p.dma("sync", KH[0:64, :], kh_d[64 * h:64 * (h + 1), :])
                p.step("vector", lambda e: e.memset(KH[64:65, :], 1.0))
                for sg in range(S // 512):
                    p.step("scalar", lambda e, sg=sg: e.activation(
                        ksq[:], KH[0:64, 512 * sg:512 * (sg + 1)], AF.Square,
                        bias=0.0, scale=0.5))
                    p.step("tensor", lambda e: e.matmul(
                        kn_ps[:], ones64f[:], ksq[:], start=True, stop=True))
                    p.step("vector", lambda e: e.tensor_scalar(
                        knrow[:], kn_ps[:], ATT_EPS, None, ALU.add))
                    p.step("vector", lambda e, sg=sg: e.tensor_copy(
                        knrow_b[:, 512 * sg:512 * (sg + 1)], knrow[:]))
                p.dma("gpsimd", KH[65:66, :], knrow_b[:])
                p.dma("gpsimd", khscr[h], KH[:])
            else:
                p.dma("sync", KH[:], khscr[h])
            for kbp in range(nkb // 2):
                def mms(e, kbp=kbp, qg=qg, h=h):
                    for half in range(2):
                        kb = 2 * kbp + half
                        e.matmul(den_ps[:, 512 * half:512 * (half + 1)],
                                 KH[0:66, 128 * kb:128 * (kb + 1)],
                                 QHs[h][0:66, QG * qg:QG * (qg + 1)],
                                 start=True, stop=True)
                        inst = e.matmul(dot_ps[:, 512 * half:512 * (half + 1)],
                                        KH[0:64, 128 * kb:128 * (kb + 1)],
                                        QHs[h][0:64, QG * qg:QG * (qg + 1)],
                                        start=True, stop=True)
                    return inst
                p.step("tensor", mms)

                def sln(e):
                    e.activation(bln[:], den_ps[:], AF.Ln, bias=0.0, scale=1.0)
                    return e.activation(rln[:], bln[:], AF.Exp, bias=0.0, scale=-1.0)
                p.step("scalar", sln)

                def vm(e):
                    e.tensor_tensor(m1[:], dot_ps[:], rln[:], ALU.mult)
                    return e.tensor_tensor(uu[:], m1[:], dot_ps[:], ALU.mult)
                p.step("vector", vm)
                p.step("scalar", lambda e: e.activation(
                    pt[:], uu[:], AF.Exp, bias=nc1[:], scale=0.25))
                p.step("vector", lambda e, kbp=kbp: e.tensor_tensor(
                    pt[:], pt[:], maskb[:, 1024 * kbp:1024 * (kbp + 1)], ALU.mult))
                if _DBG and h == 0 and qg == 0 and kbp == 0:
                    p.step("vector", lambda e: e.tensor_copy(dbg2s[:], pt[:]))
                    p.dma("gpsimd", dbg2[:], dbg2s[:])
                for half in range(2):
                    kb = 2 * kbp + half
                    p.dma("sync", vbuf[:, 0:64],
                          v_d[128 * kb:128 * (kb + 1), 64 * h:64 * (h + 1)])
                    p.step("tensor", lambda e, kb=kb, half=half: e.matmul(
                        o_ps[:], vbuf[:], pt[:, 512 * half:512 * (half + 1)],
                        start=(kb == 0), stop=(kb == nkb - 1)))
            p.step("vector", lambda e: e.tensor_copy(ohl[:], o_ps[:]))
            if _DBG and h == 0 and qg == 0:
                p.dma("gpsimd", dbg[0:65, :], ohl[:])
            def srl(e):
                e.activation(rl[:], ohl[64:65, :], AF.Ln, bias=0.0,
                             scale=18446744073709551616.0)
                return e.activation(rl[:], rl[:], AF.Exp, bias=p64c[0:1, :],
                                    scale=-1.0)
            p.step("scalar", srl)
            p.step("tensor", lambda e: e.matmul(rlb_ps[:], ones1f[:], rl[:],
                                                start=True, stop=True))
            p.step("vector", lambda e: e.tensor_tensor(
                aob[:], ohl[0:64, :], rlb_ps[:], ALU.mult))
            p.dma("gpsimd", ao_d[64 * h:64 * (h + 1), QG * qg:QG * (qg + 1)],
                  aob[:])
    return p.finalize(), p.out_names


def build_l1_tail():
    p = _P()
    ao_d = p.dram_in("AOT", (D, T), BF16)
    wo_d = p.dram_in("w_out", (D, D), F32)
    bo_d = p.dram_in("b_out", (D,), F32)
    x1_d = p.dram_in("x1", (T, D), F32)
    g1_d = p.dram_in("g1", (T, D), F32)
    out_d = p.dram_out("out", (T, D), F32)

    identf = _ident(p, F32)
    ntile = T // 128
    TC = lambda buf, c, w=T: buf[:, c * w:(c + 1) * w]
    mm_ps = p.psum([128, 512], F32, "mm")
    tpf_ps = p.psum([128, 128], F32, "tpf")

    ao_s = p.sbuf([128, 8 * T], BF16, "aos")
    for c in range(8):
        p.dma("sync", TC(ao_s, c), ao_d[128 * c:128 * (c + 1), :])
    w_s = p.sbuf([128, 8 * D], BF16, "ws")
    wrow = p.sbuf([128, D], F32, "wr")
    for r in range(8):
        p.dma("sync", wrow[:], wo_d[128 * r:128 * (r + 1), :])
        p.step("vector", lambda e, r=r: e.tensor_copy(w_s[:, D * r:D * (r + 1)],
                                                      wrow[:]))
    bo_s = p.sbuf([128, 8], F32, "bos")
    for ec in range(8):
        p.dma("sync", bo_s[:, ec:ec + 1], bo_d[128 * ec:128 * (ec + 1)].unsqueeze(1))
    s1T = p.sbuf([128, 8 * T], F32, "s1T")
    for ec in range(8):
        for tg in range(T // 512):
            def mmo(e, ec=ec, tg=tg):
                for dc in range(8):
                    inst = e.matmul(mm_ps[:],
                                    w_s[:, D * dc + 128 * ec:D * dc + 128 * (ec + 1)],
                                    TC(ao_s, dc)[:, 512 * tg:512 * (tg + 1)],
                                    start=(dc == 0), stop=(dc == 7))
                return inst
            p.step("tensor", mmo)
            p.step("scalar", lambda e, ec=ec, tg=tg: e.activation(
                TC(s1T, ec)[:, 512 * tg:512 * (tg + 1)], mm_ps[:], AF.Identity,
                bias=bo_s[:, ec:ec + 1], scale=1.0))
    x1f = p.sbuf([128, D], F32, "x1f")
    g1f = p.sbuf([128, D], F32, "g1f")
    s1f = p.sbuf([128, D], F32, "s1f")
    of = p.sbuf([128, D], F32, "of")
    for t in range(ntile):
        p.dma("sync", x1f[:], x1_d[128 * t:128 * (t + 1), :])
        p.dma("sync", g1f[:], g1_d[128 * t:128 * (t + 1), :])
        for kc in range(8):
            p.step("tensor", lambda e, kc=kc, t=t: e.transpose(
                tpf_ps[:], TC(s1T, kc)[:, 128 * t:128 * (t + 1)], identf[:]))
            p.step("vector", lambda e, kc=kc: e.tensor_copy(
                s1f[:, 128 * kc:128 * (kc + 1)], tpf_ps[:]))
        p.step("vector", lambda e: e.tensor_tensor(of[:], s1f[:], g1f[:], ALU.mult))
        p.step("vector", lambda e: e.tensor_tensor(of[:], of[:], x1f[:], ALU.add))
        p.dma("gpsimd", out_d[128 * t:128 * (t + 1), :], of[:])
    return p.finalize(), p.out_names


MEGA_UPTO = 99
SKIP_COLL = False


def _mega_stub_out(p, out_d):
    z = p.sbuf([128, D], F32, "stubz")
    p.step("vector", lambda e: e.memset(z[:], 0.0))
    for t in range(T // 128):
        p.dma("gpsimd", out_d[128 * t:128 * (t + 1), :], z[:])
    return p.finalize()


def build_mega():
    p = _P()
    nc = p.nc
    x_d = p.dram_in("x", (T, D), F32)
    wq_d = p.dram_in("wq", (D, D), F32)
    wk_d = p.dram_in("wk", (D, D), F32)
    wv_d = p.dram_in("wv", (D, D), F32)
    wg_d = p.dram_in("wg", (I0, D), F32)
    wu_d = p.dram_in("wu", (I0, D), F32)
    wd_d = p.dram_in("wd", (D, I0), F32)
    wo_d = p.dram_in("wo", (D, D), F32)
    wqkv_d = p.dram_in("wqkv", (D, 3 * D), F32)
    bqkv_d = p.dram_in("bqkv", (3 * D,), F32)
    wgate_d = p.dram_in("Wg", (D, 8), F32)
    bg_d = p.dram_in("bg", (1, 8), F32)
    w1_d = p.dram_in("W1", (8, D, D), F32)
    b1_d = p.dram_in("b1", (8, D), F32)
    w2_d = p.dram_in("W2", (8, D, D), F32)
    b2_d = p.dram_in("b2", (8, D), F32)
    wout_d = p.dram_in("w_out", (D, D), F32)
    bout_d = p.dram_in("b_out", (D,), F32)
    qt_d = p.dram_in("qtrueb", (128, T), F32)
    selc_d = p.dram_in("selc", (128, 4), F32)
    eyeb_d = p.dram_in("eyeb", (128, 128), BF16)
    eyef_d = p.dram_in("eyef", (128, 128), F32)
    taus_d = p.dram_in("tausin", (128, S // 128), F32)
    out_d = p.dram_out("out", (T, D), F32)

    qt0_d = p.dram_tmp("sQT", (D, T), BF16)
    kt0_d = p.dram_tmp("sKT", (D, T), BF16)
    v0_d = p.dram_tmp("sV", (T, D), BF16)
    g_d = p.dram_tmp("sg", (T, D), F32)
    hq_d = p.dram_tmp("hqT", (I0, T), BF16)
    gk0_d = p.dram_tmp("gK0", (4 * D, T), BF16)
    gv0_d = p.dram_tmp("gV0", (4 * T, D), BF16)
    aot_d = p.dram_tmp("sAOT", (D, T), BF16)
    gat_d = p.dram_tmp("gAT", (4 * D, T), BF16)
    x1_d = p.dram_tmp("sx1", (T, D), F32)
    xn1_d = p.dram_tmp("sxn1", (T, D), F32)
    g1_d = p.dram_tmp("sg1", (T, D), F32)
    q1_d = p.dram_tmp("sQ1T", (D, T), BF16)
    k1_d = p.dram_tmp("sK1T", (D, T), BF16)
    v1_d = p.dram_tmp("sV1", (T, D), BF16)
    gk1_d = p.dram_tmp("gK1", (4 * D, T), BF16)
    gv1_d = p.dram_tmp("gV1", (4 * T, D), BF16)
    ao1_d = p.dram_tmp("sAO1", (D, T), BF16)
    khscr = p.dram_tmp("khscr", (NH, 66, S), BF16)

    GRP = [[0, 1, 2, 3], [4, 5, 6, 7]]

    def collective(kind, in_ap, out_ap):
        if SKIP_COLL:
            return
        p.step("gpsimd", lambda e: e.collective_compute(
            kind, ALU.bypass, GRP, [in_ap.opt()], [out_ap.opt()]), inc=1,
            nodrain=True)

    ident = p.sbuf([128, 128], BF16, "identb")
    p.dma("sync", ident[:], eyeb_d[:])
    identf = p.sbuf([128, 128], F32, "identfs")
    p.dma("sync", identf[:], eyef_d[:])
    tpf_ps = p.psum([128, 128], F32, "tpf")
    tp_ps = p.psum([128, 128], BF16, "tp")
    mm_ps = p.psum([128, 512], F32, "mm")
    ntile = T // 128
    TC = lambda buf, c, w=T: buf[:, c * w:(c + 1) * w]

    eps_col = p.sbuf([128, 1], F32, "lneps")
    p.step("vector", lambda e: e.memset(eps_col[:], LN_EPS))
    ln_cols = _mk_ln_cols(p)
    q = _Quant(p, I0)

    xnqT = p.sbuf([128, 8 * T], BF16, "xnqT")
    xnT = p.sbuf([128, 8 * T], BF16, "xnT")
    xf = p.sbuf([128, D], F32, "xf")
    xn = p.sbuf([128, D], F32, "xn")
    xnb = p.sbuf([128, D], BF16, "xnb")
    xnqb = p.sbuf([128, D], BF16, "xnqb")
    for t in range(ntile):
        p.dma("sync", xf[:], x_d[128 * t:128 * (t + 1), :])
        _ln_tile(p, xn[:], xf[:], ln_cols, eps_col, q.W[0][:, 0:D])
        p.step("vector", lambda e: e.tensor_copy(xnb[:], xn[:]))
        q.emit(xnqb[:], xn[:], D)
        _transpose_tile(p, ident,
                        lambda kc, t=t: TC(xnT, kc)[:, 128 * t:128 * (t + 1)],
                        lambda kc: xnb[:, 128 * kc:128 * (kc + 1)], tp_ps, 8)
        _transpose_tile(p, ident,
                        lambda kc, t=t: TC(xnqT, kc)[:, 128 * t:128 * (t + 1)],
                        lambda kc: xnqb[:, 128 * kc:128 * (kc + 1)], tp_ps, 8)
    if MEGA_UPTO < 1.1:
        return _mega_stub_out(p, out_d), p.out_names
    wrow = p.sbuf([128, D], F32, "wrow")
    wrb = p.sbuf([128, D], BF16, "wrb")
    wT = p.sbuf([128, 8 * D], BF16, "wT")
    WC = lambda c: wT[:, c * D:(c + 1) * D]
    outb = p.sbuf([128, T], BF16, "outb")
    for wsrc, wdst, scalef in ((wq_d, qt0_d, 0.125), (wk_d, kt0_d, 1.0)):
        for r in range(8):
            p.dma("sync", wrow[:], wsrc[128 * r:128 * (r + 1), :])
            p.step("vector", lambda e, sf=scalef: e.tensor_scalar(
                wrb[:], wrow[:], sf, None, ALU.mult))
            _transpose_tile(p, ident,
                            lambda kc, r=r: WC(kc)[:, 128 * r:128 * (r + 1)],
                            lambda kc: wrb[:, 128 * kc:128 * (kc + 1)], tp_ps, 8)
        for ec in range(8):
            for tg in range(T // 512):
                def mms(e, ec=ec, tg=tg):
                    for dc in range(8):
                        inst = e.matmul(mm_ps[:],
                                        WC(dc)[:, 128 * ec:128 * (ec + 1)],
                                        TC(xnT, dc)[:, 512 * tg:512 * (tg + 1)],
                                        start=(dc == 0), stop=(dc == 7))
                    return inst
                p.step("tensor", mms)
                p.step("vector", lambda e, tg=tg: e.tensor_copy(
                    outb[:, 512 * tg:512 * (tg + 1)], mm_ps[:]))
            p.dma("gpsimd", wdst[128 * ec:128 * (ec + 1), :], outb[:])
    for r in range(8):
        p.dma("sync", wrow[:], wv_d[128 * r:128 * (r + 1), :])
        p.step("vector", lambda e: e.tensor_copy(wrb[:], wrow[:]))
        _transpose_tile(p, ident,
                        lambda kc, r=r: WC(kc)[:, 128 * r:128 * (r + 1)],
                        lambda kc: wrb[:, 128 * kc:128 * (kc + 1)], tp_ps, 8)
    for t in range(ntile):
        for eg in range(2):
            def mmv(e, t=t, eg=eg):
                for dc in range(8):
                    inst = e.matmul(mm_ps[:],
                                    TC(xnT, dc)[:, 128 * t:128 * (t + 1)],
                                    WC(dc)[:, 512 * eg:512 * (eg + 1)],
                                    start=(dc == 0), stop=(dc == 7))
                return inst
            p.step("tensor", mmv)
            p.step("vector", lambda e, eg=eg: e.tensor_copy(
                outb[:, 512 * eg:512 * (eg + 1)], mm_ps[:]))
        p.dma("gpsimd", v0_d[128 * t:128 * (t + 1), :], outb[:, 0:D])
    p.free(outb, wT, wrb, wrow, xnqb, xnb, xn, xf, xnT)
    if MEGA_UPTO < 1.3:
        return _mega_stub_out(p, out_d), p.out_names
    collective("AllGather", kt0_d, gk0_d)
    collective("AllGather", v0_d, gv0_d)
    if MEGA_UPTO < 1.35:
        return _mega_stub_out(p, out_d), p.out_names
    wgqT = p.sbuf([128, 8 * I0], BF16, "wgqT")
    wuqT = p.sbuf([128, 8 * I0], BF16, "wuqT")
    WGC = lambda buf, c: buf[:, c * I0:(c + 1) * I0]
    wrow2 = p.sbuf([128, D], F32, "wrow2")
    wqb = p.sbuf([128, D], BF16, "wqb")
    for wsrc, wbuf in ((wg_d, wgqT), (wu_d, wuqT)):
        for r in range(I0 // 128):
            p.dma("sync", wrow2[:], wsrc[128 * r:128 * (r + 1), :])
            q.emit(wqb[:], wrow2[:], D)
            _transpose_tile(p, ident,
                            lambda kc, r=r, wbuf=wbuf:
                            WGC(wbuf, kc)[:, 128 * r:128 * (r + 1)],
                            lambda kc: wqb[:, 128 * kc:128 * (kc + 1)], tp_ps, 8)
    gate_ps = p.psum([128, 512], F32, "gate")
    up_ps = p.psum([128, 512], F32, "up")
    sgate = q.W[1]
    hbuf = p.sbuf([128, I0], F32, "hbuf")
    hqb = p.sbuf([128, I0], BF16, "hqb")
    hpc = p.sbuf([128, 128], BF16, "hpc")
    for t in range(ntile):
        for off, w in _nsplit(I0, 512):
            def mmg(e, t=t, off=off, w=w):
                for dc in range(8):
                    e.matmul(gate_ps[:, 0:w],
                             TC(xnqT, dc)[:, 128 * t:128 * (t + 1)],
                             WGC(wgqT, dc)[:, off:off + w],
                             start=(dc == 0), stop=(dc == 7))
                for dc in range(8):
                    inst = e.matmul(up_ps[:, 0:w],
                                    TC(xnqT, dc)[:, 128 * t:128 * (t + 1)],
                                    WGC(wuqT, dc)[:, off:off + w],
                                    start=(dc == 0), stop=(dc == 7))
                return inst
            p.step("tensor", mmg)
            p.step("scalar", lambda e, off=off, w=w: e.activation(
                sgate[:, off:off + w], gate_ps[:, 0:w], AF.Silu, bias=0.0,
                scale=1.0))
            p.step("vector", lambda e, off=off, w=w: e.tensor_tensor(
                hbuf[:, off:off + w], sgate[:, off:off + w], up_ps[:, 0:w],
                ALU.mult))
        q.emit(hqb[:], hbuf[:], I0)
        for kc in range(I0 // 128):
            p.step("tensor", lambda e, kc=kc: e.transpose(
                tp_ps[:], hqb[:, 128 * kc:128 * (kc + 1)], ident[:]))
            p.step("vector", lambda e: e.tensor_copy(hpc[:], tp_ps[:]))
            p.dma("gpsimd",
                  hq_d[128 * kc:128 * (kc + 1), 128 * t:128 * (t + 1)], hpc[:])
    p.free(hpc, hqb, hbuf, wqb, wrow2, wuqT, wgqT, xnqT)
    p.free(up_ps, gate_ps)
    if MEGA_UPTO < 1.5:
        return _mega_stub_out(p, out_d), p.out_names
    wdqT = p.sbuf([128, 22 * D], BF16, "wdqT")
    wdrow = p.sbuf([128, I0], F32, "wdrow")
    wdqb = p.sbuf([128, I0], BF16, "wdqb")
    for r in range(8):
        p.dma("sync", wdrow[:], wd_d[128 * r:128 * (r + 1), :])
        q.emit(wdqb[:], wdrow[:], I0)
        for kc in range(I0 // 128):
            p.step("tensor", lambda e, kc=kc: e.transpose(
                tp_ps[:], wdqb[:, 128 * kc:128 * (kc + 1)], ident[:]))
            p.step("vector", lambda e, kc=kc, r=r: e.tensor_copy(
                wdqT[:, kc * D + 128 * r: kc * D + 128 * (r + 1)], tp_ps[:]))
    p.free(wdqb, wdrow)
    hqT_s = p.sbuf([128, 22 * T], BF16, "hqTs")
    for kc in range(I0 // 128):
        p.dma("sync", hqT_s[:, kc * T:(kc + 1) * T],
              hq_d[128 * kc:128 * (kc + 1), :])
    gout = p.sbuf([128, D], F32, "gout")
    for t in range(ntile):
        for eg in range(2):
            def mmd(e, t=t, eg=eg):
                for kc in range(22):
                    inst = e.matmul(
                        mm_ps[:],
                        hqT_s[:, kc * T + 128 * t: kc * T + 128 * (t + 1)],
                        wdqT[:, kc * D + 512 * eg: kc * D + 512 * (eg + 1)],
                        start=(kc == 0), stop=(kc == 21))
                return inst
            p.step("tensor", mmd)
            p.step("scalar", lambda e, eg=eg: e.activation(
                gout[:, 512 * eg:512 * (eg + 1)], mm_ps[:], AF.Silu,
                bias=0.0, scale=1.0))
        p.dma("gpsimd", g_d[128 * t:128 * (t + 1), :], gout[:])
    p.free(gout, hqT_s, wdqT)
    q.free()

    lnscr = p.sbuf([128, D], F32, "lnscr")

    if MEGA_UPTO < 2:
        return _mega_stub_out(p, out_d), p.out_names
    KSUP = 1024
    QG = 512
    nks = S // KSUP
    nqg = T // QG
    qt_s = p.sbuf([128, 8 * T], BF16, "qts")
    for c in range(8):
        p.dma("sync", TC(qt_s, c), qt0_d[128 * c:128 * (c + 1), :])
    oacc = p.sbuf([128, 8 * T], F32, "oacc")
    p.step("vector", lambda e: e.memset(oacc[:], 0.0))
    lacc = p.sbuf([1, T], F32, "lacc")
    p.step("vector", lambda e: e.memset(lacc[:], 0.0))
    ones_col = p.sbuf([128, 1], BF16, "ones")
    p.step("vector", lambda e: e.memset(ones_col[:], 1.0))
    nc0 = p.sbuf([128, 1], F32, "nc0")
    p.step("vector", lambda e: e.memset(nc0[:], -C0))
    p64c = p.sbuf([1, 1], F32, "p64c")
    p.step("vector", lambda e: e.memset(p64c[:], 44.3614195558365))
    kt_s = p.sbuf([128, 8 * KSUP], BF16, "kts")
    v_s = p.sbuf([128, 8 * D], BF16, "vs")
    pt_s = p.sbuf([128, 8 * QG], BF16, "pts")
    s_ps = p.psum([128, QG], F32, "sps")
    o_ps = p.psum([128, QG], F32, "ops")
    l_ps = p.psum([1, QG], F32, "lps")
    for ks in range(nks):
        rk = ks // 2
        lo = KSUP * (ks % 2)
        for c in range(8):
            p.dma("sync", TC(kt_s, c, KSUP),
                  gk0_d[D * rk + 128 * c:D * rk + 128 * (c + 1), lo:lo + KSUP])
        for kb in range(8):
            p.dma("sync", TC(v_s, kb, D),
                  gv0_d[KSUP * ks + 128 * kb:KSUP * ks + 128 * (kb + 1), :])
        for qg in range(nqg):
            for kb in range(8):
                def mmS(e, kb=kb, qg=qg):
                    for dc in range(8):
                        inst = e.matmul(
                            s_ps[:],
                            TC(kt_s, dc, KSUP)[:, 128 * kb:128 * (kb + 1)],
                            TC(qt_s, dc)[:, QG * qg:QG * (qg + 1)],
                            start=(dc == 0), stop=(dc == 7))
                    return inst
                p.step("tensor", mmS)
                p.step("scalar", lambda e, kb=kb: e.activation(
                    TC(pt_s, kb, QG), s_ps[:], AF.Exp, bias=nc0[:], scale=1.0))
            def mmL(e):
                for kb in range(8):
                    inst = e.matmul(l_ps[:], ones_col[:], TC(pt_s, kb, QG),
                                    start=(kb == 0), stop=(kb == 7))
                return inst
            p.step("tensor", mmL)
            p.step("vector", lambda e, qg=qg: e.tensor_tensor(
                lacc[:, QG * qg:QG * (qg + 1)], lacc[:, QG * qg:QG * (qg + 1)],
                l_ps[:], ALU.add))
            for ec in range(8):
                def mmPV(e, ec=ec):
                    for kb in range(8):
                        inst = e.matmul(o_ps[:],
                                        TC(v_s, kb, D)[:, 128 * ec:128 * (ec + 1)],
                                        TC(pt_s, kb, QG),
                                        start=(kb == 0), stop=(kb == 7))
                    return inst
                p.step("tensor", mmPV)
                p.step("vector", lambda e, ec=ec, qg=qg: e.tensor_tensor(
                    TC(oacc, ec)[:, QG * qg:QG * (qg + 1)],
                    TC(oacc, ec)[:, QG * qg:QG * (qg + 1)], o_ps[:], ALU.add))
    rl0 = p.sbuf([1, T], F32, "rl0")
    def srl0(e):
        e.activation(rl0[:], lacc[:], AF.Ln, bias=0.0,
                     scale=18446744073709551616.0)
        return e.activation(rl0[:], rl0[:], AF.Exp, bias=p64c[0:1, :], scale=-1.0)
    p.step("scalar", srl0)
    rlb_ps2 = p.psum([128, QG], F32, "rlb2")
    rlb0 = p.sbuf([128, T], F32, "rlbs0")
    ones1 = p.sbuf([1, 128], F32, "ones1")
    p.step("vector", lambda e: e.memset(ones1[:], 1.0))
    for qg in range(nqg):
        p.step("tensor", lambda e, qg=qg: e.matmul(
            rlb_ps2[:], ones1[:], rl0[:, QG * qg:QG * (qg + 1)],
            start=True, stop=True))
        p.step("vector", lambda e, qg=qg: e.tensor_copy(
            rlb0[:, QG * qg:QG * (qg + 1)], rlb_ps2[:]))
    aob0 = p.sbuf([128, T], BF16, "aob0")
    for ec in range(8):
        p.step("vector", lambda e, ec=ec: e.tensor_tensor(
            aob0[:], TC(oacc, ec), rlb0[:], ALU.mult))
        p.dma("gpsimd", aot_d[128 * ec:128 * (ec + 1), :], aob0[:])
    p.free(aob0, ones1, rlb0, rl0, pt_s, v_s, kt_s, p64c, nc0, ones_col,
           lacc, oacc, qt_s)
    p.free(rlb_ps2, l_ps, o_ps, s_ps)
    collective("AllGather", aot_d, gat_d)

    if MEGA_UPTO < 3:
        return _mega_stub_out(p, out_d), p.out_names
    spt = p.sbuf([128, 8 * T], BF16, "spt")
    ag_s = p.sbuf([128, 8 * T], BF16, "ags")
    selc_s = p.sbuf([128, 4], F32, "selcs")
    p.dma("sync", selc_s[:], selc_d[:])
    ag4 = p.sbuf([128, 4 * T], BF16, "ag4")
    agmix = p.sbuf([128, T], F32, "agmix")
    for pc in range(8):
        i_snd, half = pc // 2, pc % 2
        for cc in range(4):
            p.dma("sync", ag4[:, T * cc:T * (cc + 1)],
                  gat_d[D * i_snd + 256 * cc + 128 * half:
                        D * i_snd + 256 * cc + 128 * half + 128, :])
        def selmix(e, pc=pc):
            e.tensor_scalar(agmix[:], ag4[:, 0:T], selc_s[:, 0:1], None,
                            ALU.mult)
            for cc in range(1, 4):
                e.scalar_tensor_tensor(agmix[:], ag4[:, T * cc:T * (cc + 1)],
                                       selc_s[:, cc:cc + 1], agmix[:],
                                       ALU.mult, ALU.add)
            return e.tensor_copy(TC(ag_s, pc), agmix[:])
        p.step("vector", selmix)
    p.free(agmix, ag4, selc_s)
    for rr in range(8):
        for jc in range(8):
            sp0 = 1024 * rr + 128 * jc
            rp = sp0 // 2048
            tl0 = sp0 % 2048
            for half in range(2):
                pc = 2 * rp + half
                p.step("tensor", lambda e, pc=pc, tl0=tl0: e.transpose(
                    tp_ps[:], TC(ag_s, pc)[:, tl0:tl0 + 128], ident[:]))
                p.step("vector", lambda e, jc=jc, rr=rr, half=half: e.tensor_copy(
                    TC(spt, jc)[:, 256 * rr + 128 * half:256 * rr + 128 * (half + 1)],
                    tp_ps[:]))
    p.free(ag_s)
    woT = p.sbuf([128, 8 * D], BF16, "woT")
    wrowo = p.sbuf([128, D], F32, "wrowo")
    wrbo = p.sbuf([128, D], BF16, "wrbo")
    for r in range(8):
        p.dma("sync", wrowo[:], wo_d[128 * r:128 * (r + 1), :])
        p.step("vector", lambda e: e.tensor_copy(wrbo[:], wrowo[:]))
        _transpose_tile(p, ident,
                        lambda kc, r=r: TC(woT, kc, D)[:, 128 * r:128 * (r + 1)],
                        lambda kc: wrbo[:, 128 * kc:128 * (kc + 1)], tp_ps, 8)
    sTe = p.sbuf([128, T], F32, "sTe")
    xpi = p.sbuf([128, D], F32, "xpi")
    gpi = p.sbuf([128, D], F32, "gpi")
    x1pi = p.sbuf([128, D], F32, "x1pi")
    xn1pi = p.sbuf([128, D], F32, "xn1pi")
    s1t = p.sbuf([128, D], F32, "s1t")
    sT_all = p.sbuf([128, 8 * T], F32, "sTall")
    for ec in range(8):
        for tg in range(T // 512):
            def mmW(e, ec=ec, tg=tg):
                for jc in range(8):
                    inst = e.matmul(mm_ps[:],
                                    TC(woT, jc, D)[:, 128 * ec:128 * (ec + 1)],
                                    TC(spt, jc)[:, 512 * tg:512 * (tg + 1)],
                                    start=(jc == 0), stop=(jc == 7))
                return inst
            p.step("tensor", mmW)
            p.step("vector", lambda e, ec=ec, tg=tg: e.tensor_copy(
                TC(sT_all, ec)[:, 512 * tg:512 * (tg + 1)], mm_ps[:]))
    xv = x_d.rearrange("(a b) n -> b a n", b=8)
    x1v = x1_d.rearrange("(a b) n -> b a n", b=8)
    xn1v = xn1_d.rearrange("(a b) n -> b a n", b=8)
    gv = g_d.rearrange("(a b) n -> b a n", b=8)
    for rr in range(8):
        for half in range(2):
            al0 = 128 * half
            for ec in range(8):
                p.step("tensor", lambda e, ec=ec, rr=rr, half=half: e.transpose(
                    tpf_ps[:],
                    TC(sT_all, ec)[:, 256 * rr + 128 * half:256 * rr + 128 * (half + 1)],
                    identf[:]))
                p.step("vector", lambda e, ec=ec: e.tensor_copy(
                    s1t[:, 128 * ec:128 * (ec + 1)], tpf_ps[:]))
            p.dma("sync", xpi[:], xv[rr, al0:al0 + 128, :])
            p.dma("sync", gpi[:], gv[rr, al0:al0 + 128, :])
            p.step("vector", lambda e: e.tensor_tensor(x1pi[:], s1t[:], gpi[:],
                                                       ALU.mult))
            p.step("vector", lambda e: e.tensor_tensor(x1pi[:], x1pi[:], xpi[:],
                                                       ALU.add))
            p.dma("gpsimd", x1v[rr, al0:al0 + 128, :], x1pi[:])
            _ln_tile(p, xn1pi[:], x1pi[:], ln_cols, eps_col, lnscr[:])
            p.dma("gpsimd", xn1v[rr, al0:al0 + 128, :], xn1pi[:])
    p.free(sT_all, s1t, xn1pi, x1pi, gpi, xpi, sTe, wrbo, wrowo, woT, spt,
           lnscr)

    if MEGA_UPTO < 4:
        return _mega_stub_out(p, out_d), p.out_names
    xn1T = p.sbuf([128, 8 * T], BF16, "xn1T")
    selT = p.sbuf([128, 8 * ntile], F32, "selT")
    xn1loT = p.sbuf([128, 8 * T], BF16, "xn1loT")
    xn1f = p.sbuf([128, D], F32, "xn1f")
    xn1b = p.sbuf([128, D], BF16, "xn1b")
    xn1lo = p.sbuf([128, D], BF16, "xn1lo")
    for t in range(ntile):
        p.dma("sync", xn1f[:], xn1_d[128 * t:128 * (t + 1), :])
        p.step("vector", lambda e: e.tensor_copy(xn1b[:], xn1f[:]))
        p.step("vector", lambda e: e.tensor_sub(xn1lo[:], xn1f[:], xn1b[:]))
        _transpose_tile(p, ident,
                        lambda kc, t=t: TC(xn1T, kc)[:, 128 * t:128 * (t + 1)],
                        lambda kc: xn1b[:, 128 * kc:128 * (kc + 1)], tp_ps, 8)
        _transpose_tile(p, ident,
                        lambda kc, t=t: TC(xn1loT, kc)[:, 128 * t:128 * (t + 1)],
                        lambda kc: xn1lo[:, 128 * kc:128 * (kc + 1)], tp_ps, 8)
    p.free(xn1lo, xn1b, xn1f)
    wqkv_s = p.sbuf([128, 8 * 3 * D], BF16, "wqkvs")
    W3C = lambda c: wqkv_s[:, c * 3 * D:(c + 1) * 3 * D]
    wrow3 = p.sbuf([128, 3 * D], F32, "wrow3")
    for r in range(8):
        p.dma("sync", wrow3[:], wqkv_d[128 * r:128 * (r + 1), :])
        def cv(e, r=r):
            e.tensor_copy(W3C(r)[:, 0:D], wrow3[:, 0:D])
            e.tensor_scalar(W3C(r)[:, D:2 * D], wrow3[:, D:2 * D], -2.0, None,
                            ALU.mult)
            return e.tensor_copy(W3C(r)[:, 2 * D:3 * D], wrow3[:, 2 * D:3 * D])
        p.step("vector", cv)
    bq_s = p.sbuf([128, 16], F32, "bqs")
    for ec in range(8):
        p.dma("sync", bq_s[:, ec:ec + 1],
              bqkv_d[128 * ec:128 * (ec + 1)].unsqueeze(1))
    for ec in range(8):
        p.dma("sync", bq_s[:, 8 + ec:9 + ec],
              bqkv_d[D + 128 * ec:D + 128 * (ec + 1)].unsqueeze(1))
    p.step("vector", lambda e: e.tensor_scalar(bq_s[:, 8:16], bq_s[:, 8:16],
                                               -2.0, None, ALU.mult))
    outqb = p.sbuf([128, T], BF16, "outqb")
    for (dst, coff, bo) in ((q1_d, 0, 0), (k1_d, D, 8)):
        for ec in range(8):
            for tg in range(T // 512):
                def mmq(e, ec=ec, tg=tg, coff=coff):
                    for dc in range(8):
                        inst = e.matmul(
                            mm_ps[:],
                            W3C(dc)[:, coff + 128 * ec:coff + 128 * (ec + 1)],
                            TC(xn1T, dc)[:, 512 * tg:512 * (tg + 1)],
                            start=(dc == 0), stop=(dc == 7))
                    return inst
                p.step("tensor", mmq)
                p.step("scalar", lambda e, tg=tg, ec=ec, bo=bo: e.activation(
                    outqb[:, 512 * tg:512 * (tg + 1)], mm_ps[:], AF.Identity,
                    bias=bq_s[:, bo + ec:bo + ec + 1], scale=1.0))
            p.dma("gpsimd", dst[128 * ec:128 * (ec + 1), :], outqb[:])
    onesrow = p.sbuf([1, 128], BF16, "onesr")
    p.step("vector", lambda e: e.memset(onesrow[:], 1.0))
    bvrow = p.sbuf([1, D], F32, "bvrow")
    p.dma("sync", bvrow[:], bqkv_d[2 * D:3 * D].unsqueeze(0))
    bvrow_b = p.sbuf([1, D], BF16, "bvrowb")
    p.step("vector", lambda e: e.tensor_copy(bvrow_b[:], bvrow[:]))
    for t in range(ntile):
        for eg in range(2):
            def mmv1(e, t=t, eg=eg):
                for dc in range(8):
                    e.matmul(mm_ps[:],
                             TC(xn1T, dc)[:, 128 * t:128 * (t + 1)],
                             W3C(dc)[:, 2 * D + 512 * eg:2 * D + 512 * (eg + 1)],
                             start=(dc == 0), stop=False)
                return e.matmul(mm_ps[:], onesrow[:],
                                bvrow_b[:, 512 * eg:512 * (eg + 1)],
                                start=False, stop=True)
            p.step("tensor", mmv1)
            p.step("vector", lambda e, eg=eg: e.tensor_copy(
                outqb[:, 512 * eg:512 * (eg + 1)], mm_ps[:]))
        p.dma("gpsimd", v1_d[128 * t:128 * (t + 1), :], outqb[:, 0:D])
    p.free(bvrow_b, bvrow, onesrow, outqb, bq_s, wrow3, wqkv_s)
    collective("AllGather", k1_d, gk1_d)
    collective("AllGather", v1_d, gv1_d)
    wg_s = p.sbuf([128, 8 * 8], F32, "wgs")
    for dc in range(8):
        p.dma("sync", wg_s[:, 8 * dc:8 * (dc + 1)],
              wgate_d[128 * dc:128 * (dc + 1), :])
    wg_hi = p.sbuf([128, 8 * 8], BF16, "wghi")
    wg_lo = p.sbuf([128, 8 * 8], BF16, "wglo")
    p.step("vector", lambda e: e.tensor_copy(wg_hi[:], wg_s[:]))
    p.step("vector", lambda e: e.tensor_sub(wg_lo[:], wg_s[:], wg_hi[:]))
    lg_ps = p.psum([8, 512], F32, "lgps")
    lgT = p.sbuf([8, T], F32, "lgT")
    for tg in range(T // 512):
        def mmlg(e, tg=tg):
            combos = []
            for dc in range(8):
                combos.append((wg_hi[:, 8 * dc:8 * (dc + 1)], TC(xn1T, dc)))
                combos.append((wg_lo[:, 8 * dc:8 * (dc + 1)], TC(xn1T, dc)))
                combos.append((wg_hi[:, 8 * dc:8 * (dc + 1)], TC(xn1loT, dc)))
            for i, (wb, xb) in enumerate(combos):
                inst = e.matmul(lg_ps[:], wb, xb[:, 512 * tg:512 * (tg + 1)],
                                start=(i == 0), stop=(i == len(combos) - 1))
            return inst
        p.step("tensor", mmlg)
        p.step("vector", lambda e, tg=tg: e.tensor_copy(
            lgT[:, 512 * tg:512 * (tg + 1)], lg_ps[:]))
    bg_s = p.sbuf([1, 8], F32, "bgs")
    p.dma("sync", bg_s[:], bg_d[:])
    ones1f = p.sbuf([1, 128], F32, "ones1f")
    p.step("vector", lambda e: e.memset(ones1f[:], 1.0))
    bgb_ps = p.psum([128, 8], F32, "bgb")
    bgb = p.sbuf([128, 8], F32, "bgbs")
    p.step("tensor", lambda e: e.matmul(bgb_ps[:], ones1f[:], bg_s[:],
                                        start=True, stop=True))
    p.step("vector", lambda e: e.tensor_copy(bgb[:], bgb_ps[:]))
    lgt_f = p.sbuf([128, 8], F32, "lgtf")
    mcol = p.sbuf([128, 1], F32, "mcol")
    for t in range(ntile):
        p.step("tensor", lambda e, t=t: e.transpose(
            tpf_ps[:, 0:8], lgT[:, 128 * t:128 * (t + 1)], identf[0:8, 0:8]))
        p.step("vector", lambda e: e.tensor_copy(lgt_f[:], tpf_ps[:, 0:8]))
        p.step("vector", lambda e: e.tensor_tensor(lgt_f[:], lgt_f[:], bgb[:],
                                                   ALU.add))
        p.step("vector", lambda e: e.tensor_reduce(mcol[:], lgt_f[:], AX.X,
                                                   ALU.max))
        p.step("vector", lambda e, t=t: e.tensor_scalar(
            selT[:, 8 * t:8 * (t + 1)], lgt_f[:], mcol[:], None, ALU.is_ge))
    p.free(mcol, lgt_f, bgb, ones1f, bg_s, lgT, wg_lo, wg_hi, wg_s, xn1loT)
    p.free(bgb_ps, lg_ps)
    yacc = p.sbuf([128, ntile * D], F32, "yacc")
    p.step("vector", lambda e: e.memset(yacc[:], 0.0))
    heT = p.sbuf([128, 8 * T], BF16, "heT")
    w1b = p.sbuf([128, 8 * D], BF16, "w1b")
    w2b = p.sbuf([128, 8 * D], BF16, "w2b")
    wrowm = p.sbuf([128, D], F32, "wrowm")
    b1c = p.sbuf([128, 8], F32, "b1c")
    b2row = p.sbuf([1, D], F32, "b2row")
    b2row_b = p.sbuf([1, D], BF16, "b2rowb")
    onesrow2 = p.sbuf([1, 128], BF16, "onesr2")
    p.step("vector", lambda e: e.memset(onesrow2[:], 1.0))
    ytmp = p.sbuf([128, 512], F32, "ytmp")
    for ex in range(8):
        for r in range(8):
            p.dma("sync", wrowm[:], w1_d[ex, 128 * r:128 * (r + 1), :])
            p.step("vector", lambda e, r=r: e.tensor_copy(
                w1b[:, D * r:D * (r + 1)], wrowm[:]))
        for r in range(8):
            p.dma("sync", wrowm[:], w2_d[ex, 128 * r:128 * (r + 1), :])
            p.step("vector", lambda e, r=r: e.tensor_copy(
                w2b[:, D * r:D * (r + 1)], wrowm[:]))
        for ic in range(8):
            p.dma("sync", b1c[:, ic:ic + 1],
                  b1_d[ex, 128 * ic:128 * (ic + 1)].unsqueeze(1))
        p.dma("sync", b2row[:], b2_d[ex:ex + 1, :])
        p.step("vector", lambda e: e.tensor_copy(b2row_b[:], b2row[:]))
        for ic in range(8):
            for tg in range(T // 512):
                def mmh(e, ic=ic, tg=tg):
                    for dc in range(8):
                        inst = e.matmul(
                            mm_ps[:],
                            w1b[:, D * dc + 128 * ic:D * dc + 128 * (ic + 1)],
                            TC(xn1T, dc)[:, 512 * tg:512 * (tg + 1)],
                            start=(dc == 0), stop=(dc == 7))
                    return inst
                p.step("tensor", mmh)
                p.step("scalar", lambda e, ic=ic, tg=tg: e.activation(
                    TC(heT, ic)[:, 512 * tg:512 * (tg + 1)], mm_ps[:],
                    AF.Gelu, bias=b1c[:, ic:ic + 1], scale=1.0))
        for t in range(ntile):
            for eg in range(2):
                def mmy(e, t=t, eg=eg):
                    for ic in range(8):
                        e.matmul(mm_ps[:],
                                 TC(heT, ic)[:, 128 * t:128 * (t + 1)],
                                 w2b[:, D * ic + 512 * eg:D * ic + 512 * (eg + 1)],
                                 start=(ic == 0), stop=False)
                    return e.matmul(mm_ps[:], onesrow2[:],
                                    b2row_b[:, 512 * eg:512 * (eg + 1)],
                                    start=False, stop=True)
                p.step("tensor", mmy)
                p.step("vector", lambda e, t=t, ex=ex: e.tensor_scalar(
                    ytmp[:], mm_ps[:], selT[:, 8 * t + ex:8 * t + ex + 1],
                    None, ALU.mult))
                p.step("vector", lambda e, t=t, eg=eg: e.tensor_tensor(
                    yacc[:, D * t + 512 * eg:D * t + 512 * (eg + 1)],
                    yacc[:, D * t + 512 * eg:D * t + 512 * (eg + 1)],
                    ytmp[:], ALU.add))
    g1b = p.sbuf([128, D], F32, "g1b")
    for t in range(ntile):
        p.step("scalar", lambda e, t=t: e.activation(
            g1b[:], yacc[:, D * t:D * (t + 1)], AF.Silu, bias=0.0, scale=1.0))
        p.dma("gpsimd", g1_d[128 * t:128 * (t + 1), :], g1b[:])
    p.free(g1b, ytmp, onesrow2, b2row_b, b2row, b1c, wrowm, w2b, w1b, heT,
           yacc, selT, xn1T)
    p.free(mm_ps, tp_ps)

    if MEGA_UPTO < 5:
        return _mega_stub_out(p, out_d), p.out_names
    emit_l1_attn_fused(p, q1_d, gk1_d, gv1_d, qt_d, ao1_d, khscr, identf,
                       taus_d)

    if MEGA_UPTO < 6:
        return _mega_stub_out(p, out_d), p.out_names
    mm_ps = p.psum([128, 512], F32, "mm6")
    ao_s = p.sbuf([128, 8 * T], BF16, "aos")
    for c in range(8):
        p.dma("sync", TC(ao_s, c), ao1_d[128 * c:128 * (c + 1), :])
    w_s = p.sbuf([128, 8 * D], BF16, "ws")
    wrow5 = p.sbuf([128, D], F32, "wr5")
    for r in range(8):
        p.dma("sync", wrow5[:], wout_d[128 * r:128 * (r + 1), :])
        p.step("vector", lambda e, r=r: e.tensor_copy(
            w_s[:, D * r:D * (r + 1)], wrow5[:]))
    bo_s = p.sbuf([128, 8], F32, "bos")
    for ec in range(8):
        p.dma("sync", bo_s[:, ec:ec + 1],
              bout_d[128 * ec:128 * (ec + 1)].unsqueeze(1))
    s1T = p.sbuf([128, 8 * T], F32, "s1T5")
    for ec in range(8):
        for tg in range(T // 512):
            def mmo(e, ec=ec, tg=tg):
                for dc in range(8):
                    inst = e.matmul(
                        mm_ps[:],
                        w_s[:, D * dc + 128 * ec:D * dc + 128 * (ec + 1)],
                        TC(ao_s, dc)[:, 512 * tg:512 * (tg + 1)],
                        start=(dc == 0), stop=(dc == 7))
                return inst
            p.step("tensor", mmo)
            p.step("scalar", lambda e, ec=ec, tg=tg: e.activation(
                TC(s1T, ec)[:, 512 * tg:512 * (tg + 1)], mm_ps[:], AF.Identity,
                bias=bo_s[:, ec:ec + 1], scale=1.0))
    x1f5 = p.sbuf([128, D], F32, "x1f5")
    g1f5 = p.sbuf([128, D], F32, "g1f5")
    s1f5 = p.sbuf([128, D], F32, "s1f5")
    of5 = p.sbuf([128, D], F32, "of5")
    for t in range(ntile):
        p.dma("sync", x1f5[:], x1_d[128 * t:128 * (t + 1), :])
        p.dma("sync", g1f5[:], g1_d[128 * t:128 * (t + 1), :])
        for kc in range(8):
            p.step("tensor", lambda e, kc=kc, t=t: e.transpose(
                tpf_ps[:], TC(s1T, kc)[:, 128 * t:128 * (t + 1)], identf[:]))
            p.step("vector", lambda e, kc=kc: e.tensor_copy(
                s1f5[:, 128 * kc:128 * (kc + 1)], tpf_ps[:]))
        p.step("vector", lambda e: e.tensor_tensor(of5[:], s1f5[:], g1f5[:],
                                                   ALU.mult))
        p.step("vector", lambda e: e.tensor_tensor(of5[:], of5[:], x1f5[:],
                                                   ALU.add))
        p.dma("gpsimd", out_d[128 * t:128 * (t + 1), :], of5[:])
    return p.finalize(), p.out_names


def emit_l1_attn_fused(p, q1_d, gk1_d, gv1_d, qt_d, ao_d, khscr, identf,
                       taus_d):
    QG = 512
    nqg = T // QG
    nkb = S // 128
    ones64f = p.sbuf([64, 1], F32, "o64")
    p.step("vector", lambda e: e.memset(ones64f[:], 1.0))
    ones1f64 = p.sbuf([1, 64], F32, "o1f")
    p.step("vector", lambda e: e.memset(ones1f64[:], 1.0))
    nc1c = p.sbuf([128, 1], F32, "nc1")
    p.step("vector", lambda e: e.memset(nc1c[:], -C1))
    p64c5 = p.sbuf([1, 1], F32, "p64c5")
    p.step("vector", lambda e: e.memset(p64c5[:], 44.3614195558365))
    qtrue = p.sbuf([128, T], F32, "qtrue")
    p.dma("sync", qtrue[:], qt_d[:])
    taus = p.sbuf([128, nkb], F32, "taus")
    p.dma("sync", taus[:], taus_d[:])
    QHs = [p.sbuf([66, T], BF16, f"QH{h}") for h in range(NH)]
    onesT = p.sbuf([1, T], BF16, "onesT")
    p.step("vector", lambda e: e.memset(onesT[:], 1.0))
    qsq = p.sbuf([64, T], F32, "qsq")
    qn_ps = p.psum([1, 512], F32, "qnps")
    for h in range(NH):
        p.dma("sync", QHs[h][0:64, :], q1_d[64 * h:64 * (h + 1), :])
        p.step("scalar", lambda e, h=h: e.activation(
            qsq[:], QHs[h][0:64, :], AF.Square, bias=0.0, scale=1.0))
        p.dma("gpsimd", QHs[h][65:66, :], onesT[:, 0:T])
        for tg in range(nqg):
            p.step("tensor", lambda e, tg=tg: e.matmul(
                qn_ps[:], ones64f[:], qsq[:, QG * tg:QG * (tg + 1)],
                start=True, stop=True))
            p.step("vector", lambda e, h=h, tg=tg: e.tensor_copy(
                QHs[h][64:65, QG * tg:QG * (tg + 1)], qn_ps[:]))
    p.free(qsq)
    KH = p.sbuf([66, S], BF16, "KH")
    ksq = p.sbuf([64, 512], F32, "ksq")
    kn_ps = qn_ps
    knrow = p.sbuf([1, 512], F32, "knrow")
    knrow_b = p.sbuf([1, S], BF16, "knrowb")
    maskb = p.sbuf([128, nkb * 512], BF16, "maskb")
    mtmp = p.sbuf([128, 512], F32, "mtmp")
    vbuf = p.sbuf([128, 65], BF16, "vbuf")
    p.step("vector", lambda e: e.memset(vbuf[:, 64:65], 1.0))
    den_ps = p.psum([128, 1024], F32, "denps")
    dot_ps = p.psum([128, 1024], F32, "dotps")
    o_ps = p.psum([65, QG], F32, "ops5")
    rlb_ps = p.psum([64, QG], F32, "rlbps")
    bln = p.sbuf([128, 1024], F32, "bln")
    rln = p.sbuf([128, 1024], F32, "rln")
    m1 = p.sbuf([128, 1024], F32, "m1")
    uu = p.sbuf([128, 1024], F32, "uu")
    pt = p.sbuf([128, 1024], BF16, "pt")
    ohl = p.sbuf([65, QG], F32, "ohl")
    rl = p.sbuf([1, QG], F32, "rl")
    aob = p.sbuf([64, QG], BF16, "aob")
    for qg in range(nqg):
        for kb in range(nkb):
            p.step("vector", lambda e, kb=kb, qg=qg: e.tensor_scalar(
                mtmp[:], qtrue[:, QG * qg:QG * (qg + 1)], taus[:, kb:kb + 1],
                None, ALU.is_ge))
            p.step("vector", lambda e, kb=kb: e.tensor_copy(
                maskb[:, 512 * kb:512 * (kb + 1)], mtmp[:]))
        for h in range(NH):
            if qg == 0:
                for r in range(4):
                    p.dma("sync", KH[0:64, T * r:T * (r + 1)],
                          gk1_d[D * r + 64 * h:D * r + 64 * (h + 1), :])
                p.step("vector", lambda e: e.memset(KH[64:65, :], 1.0))
                for sg in range(S // 512):
                    p.step("scalar", lambda e, sg=sg: e.activation(
                        ksq[:], KH[0:64, 512 * sg:512 * (sg + 1)], AF.Square,
                        bias=0.0, scale=0.5))
                    p.step("tensor", lambda e: e.matmul(
                        kn_ps[:], ones64f[:], ksq[:], start=True, stop=True))
                    p.step("vector", lambda e: e.tensor_scalar(
                        knrow[:], kn_ps[:], ATT_EPS, None, ALU.add))
                    p.step("vector", lambda e, sg=sg: e.tensor_copy(
                        knrow_b[:, 512 * sg:512 * (sg + 1)], knrow[:]))
                p.dma("gpsimd", KH[65:66, :], knrow_b[:])
                p.dma("gpsimd", khscr[h], KH[:])
            else:
                p.dma("sync", KH[:], khscr[h])
            for kbp in range(nkb // 2):
                def mms(e, kbp=kbp, qg=qg, h=h):
                    for half in range(2):
                        kb = 2 * kbp + half
                        e.matmul(den_ps[:, 512 * half:512 * (half + 1)],
                                 KH[0:66, 128 * kb:128 * (kb + 1)],
                                 QHs[h][0:66, QG * qg:QG * (qg + 1)],
                                 start=True, stop=True)
                        inst = e.matmul(dot_ps[:, 512 * half:512 * (half + 1)],
                                        KH[0:64, 128 * kb:128 * (kb + 1)],
                                        QHs[h][0:64, QG * qg:QG * (qg + 1)],
                                        start=True, stop=True)
                    return inst
                p.step("tensor", mms)

                def sln(e):
                    e.activation(bln[:], den_ps[:], AF.Ln, bias=0.0, scale=1.0)
                    return e.activation(rln[:], bln[:], AF.Exp, bias=0.0,
                                        scale=-1.0)
                p.step("scalar", sln)

                def vm(e):
                    e.tensor_tensor(m1[:], dot_ps[:], rln[:], ALU.mult)
                    return e.tensor_tensor(uu[:], m1[:], dot_ps[:], ALU.mult)
                p.step("vector", vm)
                p.step("scalar", lambda e: e.activation(
                    pt[:], uu[:], AF.Exp, bias=nc1c[:], scale=0.25))
                p.step("vector", lambda e, kbp=kbp: e.tensor_tensor(
                    pt[:], pt[:], maskb[:, 1024 * kbp:1024 * (kbp + 1)],
                    ALU.mult))
                for half in range(2):
                    kb = 2 * kbp + half
                    p.dma("sync", vbuf[:, 0:64],
                          gv1_d[128 * kb:128 * (kb + 1), 64 * h:64 * (h + 1)])
                    p.step("tensor", lambda e, kb=kb, half=half: e.matmul(
                        o_ps[:], vbuf[:], pt[:, 512 * half:512 * (half + 1)],
                        start=(kb == 0), stop=(kb == nkb - 1)))
            p.step("vector", lambda e: e.tensor_copy(ohl[:], o_ps[:]))
            def srl(e):
                e.activation(rl[:], ohl[64:65, :], AF.Ln, bias=0.0,
                             scale=18446744073709551616.0)
                return e.activation(rl[:], rl[:], AF.Exp, bias=p64c5[0:1, :],
                                    scale=-1.0)
            p.step("scalar", srl)
            p.step("tensor", lambda e: e.matmul(rlb_ps[:], ones1f64[:], rl[:],
                                                start=True, stop=True))
            p.step("vector", lambda e: e.tensor_tensor(
                aob[:], ohl[0:64, :], rlb_ps[:], ALU.mult))
            p.dma("gpsimd", ao_d[64 * h:64 * (h + 1), QG * qg:QG * (qg + 1)],
                  aob[:])
    p.free(aob, rl, ohl, pt, uu, m1, rln, bln, vbuf, mtmp, maskb, knrow_b,
           knrow, ksq, KH, onesT, *reversed(QHs), taus, qtrue, p64c5, nc1c,
           ones1f64, ones64f)
    p.free(rlb_ps, o_ps, dot_ps, den_ps, qn_ps)


_BUILD_CACHE = {}


def _get(name, builder):
    if name not in _BUILD_CACHE:
        _BUILD_CACHE[name] = builder()
    return _BUILD_CACHE[name]


import ml_dtypes as _mld
_EYEB = np.eye(128, dtype=_mld.bfloat16)
_EYEF = np.eye(128, dtype=np.float32)
_TAUS = (np.arange(128, dtype=np.float32)[:, None]
         + 128.0 * np.arange(64, dtype=np.float32)[None, :])


def kernel(**inputs):
    from concourse.aot_env import allow_ici_all_to_all
    f32 = np.float32
    C = np.ascontiguousarray
    x = np.asarray(inputs["L0_attn_q"]).reshape(16384, D)
    cores = list(range(NCORE))
    nc, _ = _get("mega", build_mega)
    maps = []
    for c in range(NCORE):
        t0 = T * (c % 4)
        qtrueb = np.broadcast_to(
            (t0 + np.arange(T, dtype=f32))[None, :], (128, T))
        selc = np.zeros((128, 4), np.float32)
        selc[:, c % 4] = 1.0
        maps.append({
            "x": C(x[T * c:T * (c + 1)]),
            "selc": selc,
            "wq": inputs["L0_attn_wq"], "wk": inputs["L0_attn_wk"],
            "wv": inputs["L0_attn_wv"],
            "wg": inputs["L0_ffn_w_gate"], "wu": inputs["L0_ffn_w_up"],
            "wd": inputs["L0_ffn_w_down"],
            "wo": inputs["L0_attn_wo"],
            "wqkv": inputs["L1_attn_w_qkv"], "bqkv": inputs["L1_attn_b_qkv"],
            "Wg": inputs["L1_ffn_Wg"],
            "bg": np.asarray(inputs["L1_ffn_bg"]).reshape(1, 8),
            "W1": inputs["L1_ffn_W1"], "b1": inputs["L1_ffn_b1"],
            "W2": inputs["L1_ffn_W2"], "b2": inputs["L1_ffn_b2"],
            "w_out": inputs["L1_attn_w_out"], "b_out": inputs["L1_attn_b_out"],
            "qtrueb": C(qtrueb),
            "eyeb": _EYEB, "eyef": _EYEF, "tausin": _TAUS,
        })
    with allow_ici_all_to_all():
        res = run_bass_kernel_spmd(nc, maps, core_ids=cores)
    out = np.concatenate([np.asarray(r["out"]) for r in res.results], axis=0)
    return out.reshape(2, S, D).astype(np.float32)


# version 09
# speedup vs baseline: 1.1930x; 1.0145x over previous
import numpy as np
import ml_dtypes
import concourse.bass as bass
import concourse.mybir as mybir
from concourse.bass_utils import run_bass_kernel_spmd

F32 = mybir.dt.float32
BF16 = mybir.dt.bfloat16
I32 = mybir.dt.int32
AF = mybir.ActivationFunctionType
ALU = mybir.AluOpType
AX = mybir.AxisListType

C0 = 16.0
C1 = 56.0
LN_EPS = 1e-5
ATT_EPS = 1e-6

NCORE = 8
T = 2048
D = 1024
I0 = 2816
S = 8192
NH = 16
HD = 64


def _nsplit(total, w):
    out = []
    off = 0
    while off < total:
        out.append((off, min(w, total - off)))
        off += w
    return out


class _P:

    def __init__(self):
        self.nc = bass.Bass("TRN2", target_bir_lowering=False, debug=False)
        self.steps = []
        self.out_names = []
        self._ctx = {}
        self._n = 0

    def dram_in(self, name, shape, dt):
        return self.nc.dram_tensor(name, list(shape), dt, kind="ExternalInput").ap()

    def dram_out(self, name, shape, dt):
        self.out_names.append(name)
        return self.nc.dram_tensor(name, list(shape), dt, kind="ExternalOutput").ap()

    def dram_tmp(self, name, shape, dt):
        return self.nc.dram_tensor(name, list(shape), dt, kind="Internal").ap()

    def sbuf(self, shape, dt, name=None):
        self._n += 1
        g = self.nc.sbuf_tensor(name or f"sb{self._n}", list(shape), dt)
        h = g.__enter__()
        self._ctx[id(h)] = g
        return h

    def psum(self, shape, dt, name=None):
        self._n += 1
        g = self.nc.psum_tensor(name or f"ps{self._n}", list(shape), dt)
        h = g.__enter__()
        self._ctx[id(h)] = g
        return h

    def free(self, *handles):
        for h in handles:
            g = self._ctx.pop(id(h))
            g.__exit__(None, None, None)

    def step(self, engine, fn, inc=1, nodrain=False):
        self.steps.append((engine, fn, inc, nodrain))

    DMA_REMAP = False

    def dma(self, engine, out, in_, **kw):
        if self.DMA_REMAP and engine == "gpsimd":
            engine = "sync"
        self.step(engine, lambda e: e.dma_start(out, in_, **kw), inc=16)

    SEG_CAP = 30000

    def finalize(self):
        nc = self.nc
        seg_of = []
        seg_base = [0]
        tot = 0
        for st in self.steps:
            inc = st[2]
            if tot - seg_base[-1] + inc > self.SEG_CAP:
                seg_base.append(tot)
            seg_of.append(len(seg_base) - 1)
            tot += inc
        seg_base.append(tot)
        nseg = len(seg_base) - 1
        sems = [nc.semaphore(f"stepsem{k}").__enter__() for k in range(nseg)]
        waits = []
        t = 0
        for st in self.steps:
            waits.append(t)
            t += st[2]
        blk = nc.Block().__enter__()
        by_eng = {}
        for i, st in enumerate(self.steps):
            by_eng.setdefault(st[0], []).append((i, st[1], st[2], st[3]))
        last_eng = self.steps[-1][0]
        for name in ("sync", "scalar", "vector", "tensor", "gpsimd"):
            if name not in by_eng and name != last_eng:
                continue
            items = by_eng.get(name, [])

            def make(items=items, name=name):
                def body(e):
                    cur = 0
                    prev_i = None
                    prev_dma = False
                    for (i, fn, inc, nodrain) in items:
                        sg = seg_of[i]
                        crossed = cur < sg
                        while cur < sg:
                            e.wait_ge(sems[cur], seg_base[cur + 1] - seg_base[cur])
                            cur += 1
                        w = waits[i] - seg_base[sg]
                        skip = (prev_i == i - 1 and not crossed and prev_dma
                                and inc == 16)
                        if w > 0 and not skip:
                            e.wait_ge(sems[sg], w)
                        fn(e).then_inc(sems[sg], inc)
                        prev_i = i
                        prev_dma = (inc == 16)
                    if name == last_eng:
                        while cur < nseg - 1:
                            e.wait_ge(sems[cur], seg_base[cur + 1] - seg_base[cur])
                            cur += 1
                        e.wait_ge(sems[nseg - 1],
                                  seg_base[nseg] - seg_base[nseg - 1])
                return body
            getattr(blk, name)(make())
        blk.__exit__(None, None, None)
        for g in reversed(list(self._ctx.values())):
            g.__exit__(None, None, None)
        self._ctx.clear()
        return nc


def _ident(p, dt=BF16):
    p._n += 1
    t = p.sbuf([128, 128], dt, f"ident{p._n}")
    p.step("gpsimd", lambda e: e.memset(t[:], 0.0))
    p.step("gpsimd", lambda e: e.affine_select(
        out=t[:], in_=t[:], compare_op=ALU.not_equal, fill=1.0, base=0,
        pattern=[[-1, 128]], channel_multiplier=1))
    return t


def _transpose_tile(p, ident, dst, src_ap_fn, tp_ps, n_kchunk):
    for kc in range(n_kchunk):
        p.step("tensor", lambda e, kc=kc: e.transpose(tp_ps[:], src_ap_fn(kc), ident[:]))
        p.step("vector", lambda e, kc=kc: e.tensor_copy(dst(kc), tp_ps[:]))


class _Quant:

    def __init__(self, p, maxN):
        self.p = p
        self.W = [p.sbuf([128, maxN], F32, f"qW{i}") for i in range(5)]
        nb = maxN // 16
        self.bm = p.sbuf([128, nb], F32, "q_bm")
        self.scale = p.sbuf([128, nb], F32, "q_scale")
        self.lns = p.sbuf([128, nb], F32, "q_lns")
        self.invs = p.sbuf([128, nb], F32, "q_invs")

    def free(self):
        self.p.free(self.invs, self.lns, self.scale, self.bm,
                    *reversed(self.W))

    def emit(self, out_bf, x_f32, N):
        p = self.p
        nb = N // 16
        W1, W2, W3, W4, W5 = self.W
        bm, scale, lns, invs = self.bm, self.scale, self.lns, self.invs
        a3 = lambda t: t[:, 0:N].rearrange("p (b k) -> p b k", k=16)
        bcol = lambda t: t[:, 0:nb].rearrange("p (b k) -> p b k", k=1).broadcast_to(
            [128, nb, 16])

        def v1(e):
            return e.tensor_reduce(
                bm[:, 0:nb].rearrange("p (b k) -> p b k", k=1), a3(x_f32),
                AX.X, ALU.max, apply_absolute_value=True)
        p.step("vector", v1)
        p.step("vector", lambda e: e.tensor_scalar(
            scale[:, 0:nb], bm[:, 0:nb], 1.0 / 6.0, 1e-12, ALU.mult, ALU.max))

        def s1(e):
            e.activation(lns[:, 0:nb], scale[:, 0:nb], AF.Ln, bias=0.0, scale=1.0)
            e.activation(invs[:, 0:nb], lns[:, 0:nb], AF.Exp, bias=0.0, scale=-1.0)
            e.activation(W1[:, 0:N], x_f32, AF.Abs, bias=0.0, scale=1.0)
            return e.activation(W3[:, 0:N], x_f32, AF.Sign, bias=0.0, scale=1.0)
        p.step("scalar", s1)

        def v2(e):
            e.tensor_tensor(a3(W2), a3(W1), bcol(invs), ALU.mult)
            W4i = W4[:, 0:N].bitcast(I32)
            e.tensor_scalar(W4i, W2[:, 0:N], 2.0, None, ALU.mult)
            e.tensor_scalar(W1[:, 0:N], W4i, 0.5, 2.0, ALU.mult, ALU.min)
            e.tensor_scalar(W5[:, 0:N], W2[:, 0:N], 5.0, 2.0, ALU.is_gt, ALU.mult)
            e.tensor_scalar(W4i, W2[:, 0:N], 1.0, None, ALU.mult)
            e.tensor_scalar(W2[:, 0:N], W4i, 2.0, 0.0, ALU.subtract, ALU.max)
            e.scalar_tensor_tensor(W4[:, 0:N], W2[:, 0:N], 2.0, W1[:, 0:N],
                                   ALU.min, ALU.add)
            e.tensor_tensor(W1[:, 0:N], W4[:, 0:N], W5[:, 0:N], ALU.add)
            e.tensor_tensor(W2[:, 0:N], W1[:, 0:N], W3[:, 0:N], ALU.mult)
            return e.tensor_tensor(a3(out_bf), a3(W2), bcol(scale), ALU.mult)
        p.step("vector", v2)


def _ln_tile(p, xn_out, x_f32, cols, eps_col, scr_f32):
    ssum, nmean, sumsq, msq, var, lnv, rstd, nmr = cols
    p.step("vector", lambda e: e.tensor_reduce(ssum[:], x_f32, AX.X, ALU.add))
    p.step("scalar", lambda e: e.activation(
        scr_f32, x_f32, AF.Square, bias=0.0, scale=1.0))
    p.step("vector", lambda e: e.tensor_reduce(sumsq[:], scr_f32, AX.X, ALU.add))

    p.step("vector", lambda e: e.tensor_scalar(nmean[:], ssum[:], -1.0 / D,
                                               None, ALU.mult))
    p.step("scalar", lambda e: e.activation(msq[:], nmean[:], AF.Square,
                                            bias=0.0, scale=1.0))
    p.step("vector", lambda e: e.scalar_tensor_tensor(
        var[:], sumsq[:], 1.0 / D, msq[:], ALU.mult, ALU.subtract))

    def s2(e):
        e.activation(lnv[:], var[:], AF.Ln, bias=eps_col[:], scale=1.0)
        return e.activation(rstd[:], lnv[:], AF.Exp, bias=0.0, scale=-0.5)
    p.step("scalar", s2)
    p.step("vector", lambda e: e.tensor_tensor(nmr[:], nmean[:], rstd[:], ALU.mult))
    p.step("scalar", lambda e: e.activation(xn_out, x_f32, AF.Identity,
                                            bias=nmr[:], scale=rstd[:]))


def _mk_ln_cols(p):
    return tuple(p.sbuf([128, 1], F32, f"ln{i}") for i in range(8))


def build_l0_head():
    p = _P()
    x_d = p.dram_in("x", (T, D), F32)
    wq_d = p.dram_in("wq", (D, D), F32)
    wk_d = p.dram_in("wk", (D, D), F32)
    wv_d = p.dram_in("wv", (D, D), F32)
    wg_d = p.dram_in("wg", (I0, D), F32)
    wu_d = p.dram_in("wu", (I0, D), F32)
    wd_d = p.dram_in("wd", (D, I0), F32)
    qt_d = p.dram_out("QT", (D, T), BF16)
    kt_d = p.dram_out("KT", (D, T), BF16)
    v_d = p.dram_out("Vtok", (T, D), BF16)
    g_d = p.dram_out("g", (T, D), F32)
    hq_d = p.dram_tmp("hqT", (I0, T), BF16)

    ident = _ident(p)
    eps_col = p.sbuf([128, 1], F32, "lneps")
    p.step("vector", lambda e: e.memset(eps_col[:], LN_EPS))
    ln_cols = _mk_ln_cols(p)
    q = _Quant(p, I0)
    tp_ps = p.psum([128, 128], BF16, "tp")
    mm_ps = p.psum([128, 512], F32, "mm")

    ntile = T // 128
    xnqT = p.sbuf([128, 8 * T], BF16, "xnqT")
    xnT = p.sbuf([128, 8 * T], BF16, "xnT")
    TC = lambda buf, c: buf[:, c * T:(c + 1) * T]

    xf = p.sbuf([128, D], F32, "xf")
    xn = p.sbuf([128, D], F32, "xn")
    xnb = p.sbuf([128, D], BF16, "xnb")
    xnqb = p.sbuf([128, D], BF16, "xnqb")
    for t in range(ntile):
        p.dma("sync", xf[:], x_d[128 * t:128 * (t + 1), :])
        _ln_tile(p, xn[:], xf[:], ln_cols, eps_col, q.W[0][:, 0:D])
        p.step("vector", lambda e: e.tensor_copy(xnb[:], xn[:]))
        q.emit(xnqb[:], xn[:], D)
        _transpose_tile(p, ident,
                        lambda kc, t=t: TC(xnT, kc)[:, 128 * t:128 * (t + 1)],
                        lambda kc: xnb[:, 128 * kc:128 * (kc + 1)], tp_ps, 8)
        _transpose_tile(p, ident,
                        lambda kc, t=t: TC(xnqT, kc)[:, 128 * t:128 * (t + 1)],
                        lambda kc: xnqb[:, 128 * kc:128 * (kc + 1)], tp_ps, 8)
    p.free(xnqb, xnb, xn, xf)

    wrow = p.sbuf([128, D], F32, "wrow")
    wrb = p.sbuf([128, D], BF16, "wrb")
    wT = p.sbuf([128, 8 * D], BF16, "wT")
    WC = lambda c: wT[:, c * D:(c + 1) * D]
    outb = p.sbuf([128, T], BF16, "outb")
    for wsrc, wdst, scalef in ((wq_d, qt_d, 0.125), (wk_d, kt_d, 1.0)):
        for r in range(8):
            p.dma("sync", wrow[:], wsrc[128 * r:128 * (r + 1), :])
            p.step("vector", lambda e, s=scalef: e.tensor_scalar(
                wrb[:], wrow[:], s, None, ALU.mult))
            _transpose_tile(p, ident,
                            lambda kc, r=r: WC(kc)[:, 128 * r:128 * (r + 1)],
                            lambda kc: wrb[:, 128 * kc:128 * (kc + 1)], tp_ps, 8)
        for ec in range(8):
            for tg in range(T // 512):
                def mms(e, ec=ec, tg=tg):
                    for dc in range(8):
                        inst = e.matmul(mm_ps[:],
                                        WC(dc)[:, 128 * ec:128 * (ec + 1)],
                                        TC(xnT, dc)[:, 512 * tg:512 * (tg + 1)],
                                        start=(dc == 0), stop=(dc == 7))
                    return inst
                p.step("tensor", mms)
                p.step("vector", lambda e, tg=tg: e.tensor_copy(
                    outb[:, 512 * tg:512 * (tg + 1)], mm_ps[:]))
            p.dma("gpsimd", wdst[128 * ec:128 * (ec + 1), :], outb[:])
    for r in range(8):
        p.dma("sync", wrow[:], wv_d[128 * r:128 * (r + 1), :])
        p.step("vector", lambda e: e.tensor_copy(wrb[:], wrow[:]))
        _transpose_tile(p, ident,
                        lambda kc, r=r: WC(kc)[:, 128 * r:128 * (r + 1)],
                        lambda kc: wrb[:, 128 * kc:128 * (kc + 1)], tp_ps, 8)
    for t in range(ntile):
        for eg in range(2):
            def mmv(e, t=t, eg=eg):
                for dc in range(8):
                    inst = e.matmul(mm_ps[:],
                                    TC(xnT, dc)[:, 128 * t:128 * (t + 1)],
                                    WC(dc)[:, 512 * eg:512 * (eg + 1)],
                                    start=(dc == 0), stop=(dc == 7))
                return inst
            p.step("tensor", mmv)
            p.step("vector", lambda e, eg=eg: e.tensor_copy(
                outb[:, 512 * eg:512 * (eg + 1)], mm_ps[:]))
        p.dma("gpsimd", v_d[128 * t:128 * (t + 1), :], outb[:, 0:D])
    p.free(outb, wT, wrb, wrow, xnT)

    wgqT = p.sbuf([128, 8 * I0], BF16, "wgqT")
    wuqT = p.sbuf([128, 8 * I0], BF16, "wuqT")
    WGC = lambda buf, c: buf[:, c * I0:(c + 1) * I0]
    wrow2 = p.sbuf([128, D], F32, "wrow2")
    wqb = p.sbuf([128, D], BF16, "wqb")
    for wsrc, wbuf in ((wg_d, wgqT), (wu_d, wuqT)):
        for r in range(I0 // 128):
            p.dma("sync", wrow2[:], wsrc[128 * r:128 * (r + 1), :])
            q.emit(wqb[:], wrow2[:], D)
            _transpose_tile(p, ident,
                            lambda kc, r=r, wbuf=wbuf:
                            WGC(wbuf, kc)[:, 128 * r:128 * (r + 1)],
                            lambda kc: wqb[:, 128 * kc:128 * (kc + 1)], tp_ps, 8)
    gate_ps = p.psum([128, 512], F32, "gate")
    up_ps = p.psum([128, 512], F32, "up")
    sgate = q.W[1]
    hbuf = p.sbuf([128, I0], F32, "hbuf")
    hqb = p.sbuf([128, I0], BF16, "hqb")
    hpc = p.sbuf([128, 128], BF16, "hpc")
    for t in range(ntile):
        for off, w in _nsplit(I0, 512):
            def mmg(e, t=t, off=off, w=w):
                for dc in range(8):
                    e.matmul(gate_ps[:, 0:w],
                             TC(xnqT, dc)[:, 128 * t:128 * (t + 1)],
                             WGC(wgqT, dc)[:, off:off + w],
                             start=(dc == 0), stop=(dc == 7))
                for dc in range(8):
                    inst = e.matmul(up_ps[:, 0:w],
                                    TC(xnqT, dc)[:, 128 * t:128 * (t + 1)],
                                    WGC(wuqT, dc)[:, off:off + w],
                                    start=(dc == 0), stop=(dc == 7))
                return inst
            p.step("tensor", mmg)
            p.step("scalar", lambda e, off=off, w=w: e.activation(
                sgate[:, off:off + w], gate_ps[:, 0:w], AF.Silu, bias=0.0, scale=1.0))
            p.step("vector", lambda e, off=off, w=w: e.tensor_tensor(
                hbuf[:, off:off + w], sgate[:, off:off + w], up_ps[:, 0:w], ALU.mult))
        q.emit(hqb[:], hbuf[:], I0)
        for kc in range(I0 // 128):
            p.step("tensor", lambda e, kc=kc: e.transpose(
                tp_ps[:], hqb[:, 128 * kc:128 * (kc + 1)], ident[:]))
            p.step("vector", lambda e: e.tensor_copy(hpc[:], tp_ps[:]))
            p.dma("gpsimd",
                  hq_d[128 * kc:128 * (kc + 1), 128 * t:128 * (t + 1)], hpc[:])
    p.free(hpc, hqb, hbuf, wqb, wrow2, wuqT, wgqT, xnqT)

    wdqT = p.sbuf([128, 22 * D], BF16, "wdqT")
    wdrow = p.sbuf([128, I0], F32, "wdrow")
    wdqb = p.sbuf([128, I0], BF16, "wdqb")
    for r in range(8):
        p.dma("sync", wdrow[:], wd_d[128 * r:128 * (r + 1), :])
        q.emit(wdqb[:], wdrow[:], I0)
        for kc in range(I0 // 128):
            p.step("tensor", lambda e, kc=kc: e.transpose(
                tp_ps[:], wdqb[:, 128 * kc:128 * (kc + 1)], ident[:]))
            p.step("vector", lambda e, kc=kc, r=r: e.tensor_copy(
                wdqT[:, kc * D + 128 * r: kc * D + 128 * (r + 1)], tp_ps[:]))
    p.free(wdqb, wdrow)
    hqT_s = p.sbuf([128, 22 * T], BF16, "hqTs")
    for kc in range(I0 // 128):
        p.dma("sync", hqT_s[:, kc * T:(kc + 1) * T], hq_d[128 * kc:128 * (kc + 1), :])
    gout = p.sbuf([128, D], F32, "gout")
    for t in range(ntile):
        for eg in range(2):
            def mmd(e, t=t, eg=eg):
                for kc in range(22):
                    inst = e.matmul(mm_ps[:],
                                    hqT_s[:, kc * T + 128 * t: kc * T + 128 * (t + 1)],
                                    wdqT[:, kc * D + 512 * eg: kc * D + 512 * (eg + 1)],
                                    start=(kc == 0), stop=(kc == 21))
                return inst
            p.step("tensor", mmd)
            p.step("scalar", lambda e, eg=eg: e.activation(
                gout[:, 512 * eg:512 * (eg + 1)], mm_ps[:], AF.Silu,
                bias=0.0, scale=1.0))
        p.dma("gpsimd", g_d[128 * t:128 * (t + 1), :], gout[:])
    return p.finalize(), p.out_names


def build_l0_attn():
    p = _P()
    qt_d = p.dram_in("QT", (D, T), BF16)
    kt_d = p.dram_in("KTb", (D, S), BF16)
    v_d = p.dram_in("Vb", (S, D), BF16)
    ao_d = p.dram_out("AOT", (D, T), BF16)

    KSUP = 1024
    QG = 512
    nks = S // KSUP
    nqg = T // QG

    qt_s = p.sbuf([128, 8 * T], BF16, "qts")
    TC = lambda buf, c, w=T: buf[:, c * w:(c + 1) * w]
    for c in range(8):
        p.dma("sync", TC(qt_s, c), qt_d[128 * c:128 * (c + 1), :])
    oacc = p.sbuf([128, 8 * T], F32, "oacc")
    p.step("vector", lambda e: e.memset(oacc[:], 0.0))
    lacc = p.sbuf([1, T], F32, "lacc")
    p.step("vector", lambda e: e.memset(lacc[:], 0.0))
    ones_col = p.sbuf([128, 1], BF16, "ones")
    p.step("vector", lambda e: e.memset(ones_col[:], 1.0))
    nc0 = p.sbuf([128, 1], F32, "nc0")
    p.step("vector", lambda e: e.memset(nc0[:], -C0))

    kt_s = p.sbuf([128, 8 * KSUP], BF16, "kts")
    v_s = p.sbuf([128, 8 * D], BF16, "vs")
    pt_s = p.sbuf([128, 8 * QG], BF16, "pts")
    s_ps = p.psum([128, QG], F32, "sps")
    o_ps = p.psum([128, QG], F32, "ops")
    l_ps = p.psum([1, QG], F32, "lps")

    for ks in range(nks):
        for c in range(8):
            p.dma("sync", TC(kt_s, c, KSUP),
                  kt_d[128 * c:128 * (c + 1), KSUP * ks:KSUP * (ks + 1)])
        for kb in range(8):
            p.dma("sync", TC(v_s, kb, D),
                  v_d[KSUP * ks + 128 * kb:KSUP * ks + 128 * (kb + 1), :])
        for qg in range(nqg):
            for kb in range(8):
                def mmS(e, kb=kb, qg=qg):
                    for dc in range(8):
                        inst = e.matmul(
                            s_ps[:],
                            TC(kt_s, dc, KSUP)[:, 128 * kb:128 * (kb + 1)],
                            TC(qt_s, dc)[:, QG * qg:QG * (qg + 1)],
                            start=(dc == 0), stop=(dc == 7))
                    return inst
                p.step("tensor", mmS)
                p.step("scalar", lambda e, kb=kb: e.activation(
                    TC(pt_s, kb, QG), s_ps[:], AF.Exp, bias=nc0[:], scale=1.0))
            def mmL(e):
                for kb in range(8):
                    inst = e.matmul(l_ps[:], ones_col[:],
                                    TC(pt_s, kb, QG),
                                    start=(kb == 0), stop=(kb == 7))
                return inst
            p.step("tensor", mmL)
            p.step("vector", lambda e, qg=qg: e.tensor_tensor(
                lacc[:, QG * qg:QG * (qg + 1)], lacc[:, QG * qg:QG * (qg + 1)],
                l_ps[:], ALU.add))
            for ec in range(8):
                def mmPV(e, ec=ec):
                    for kb in range(8):
                        inst = e.matmul(o_ps[:],
                                        TC(v_s, kb, D)[:, 128 * ec:128 * (ec + 1)],
                                        TC(pt_s, kb, QG),
                                        start=(kb == 0), stop=(kb == 7))
                    return inst
                p.step("tensor", mmPV)
                p.step("vector", lambda e, ec=ec, qg=qg: e.tensor_tensor(
                    TC(oacc, ec)[:, QG * qg:QG * (qg + 1)],
                    TC(oacc, ec)[:, QG * qg:QG * (qg + 1)], o_ps[:], ALU.add))

    rl = p.sbuf([1, T], F32, "rl")
    p64c = p.sbuf([1, 1], F32, "p64c")
    p.step("vector", lambda e: e.memset(p64c[:], 44.3614195558365))
    def srl(e):
        e.activation(rl[:], lacc[:], AF.Ln, bias=0.0,
                     scale=18446744073709551616.0)
        return e.activation(rl[:], rl[:], AF.Exp, bias=p64c[0:1, :], scale=-1.0)
    p.step("scalar", srl)
    rlb_ps = p.psum([128, QG], F32, "rlb")
    rlb = p.sbuf([128, T], F32, "rlbs")
    ones1 = p.sbuf([1, 128], F32, "ones1")
    p.step("vector", lambda e: e.memset(ones1[:], 1.0))
    for qg in range(nqg):
        p.step("tensor", lambda e, qg=qg: e.matmul(
            rlb_ps[:], ones1[:], rl[:, QG * qg:QG * (qg + 1)],
            start=True, stop=True))
        p.step("vector", lambda e, qg=qg: e.tensor_copy(
            rlb[:, QG * qg:QG * (qg + 1)], rlb_ps[:]))
    aob = p.sbuf([128, T], BF16, "aob")
    for ec in range(8):
        p.step("vector", lambda e, ec=ec: e.tensor_tensor(
            aob[:], TC(oacc, ec), rlb[:], ALU.mult))
        p.dma("gpsimd", ao_d[128 * ec:128 * (ec + 1), :], aob[:])
    return p.finalize(), p.out_names


def build_l0_tail_l1_head():
    p = _P()
    spt_d = p.dram_in("SPT", (D, T), BF16)
    xt_d = p.dram_in("xT", (D, T), F32)
    gt_d = p.dram_in("gT", (D, T), F32)
    wo_d = p.dram_in("wo", (D, D), F32)
    wqkv_d = p.dram_in("wqkv", (D, 3 * D), F32)
    bqkv_d = p.dram_in("bqkv", (3 * D,), F32)
    wg_d = p.dram_in("Wg", (D, 8), F32)
    bg_d = p.dram_in("bg", (1, 8), F32)
    w1_d = p.dram_in("W1", (8, D, D), F32)
    b1_d = p.dram_in("b1", (8, D), F32)
    w2_d = p.dram_in("W2", (8, D, D), F32)
    b2_d = p.dram_in("b2", (8, D), F32)
    x1_d = p.dram_out("x1", (T, D), F32)
    g1_d = p.dram_out("g1", (T, D), F32)
    q1_d = p.dram_out("Q1T", (D, T), BF16)
    k1_d = p.dram_out("K1T", (D, T), BF16)
    v1_d = p.dram_out("V1tok", (T, D), BF16)

    ident = _ident(p)
    identf = _ident(p, F32)
    ntile = T // 128
    TC = lambda buf, c, w=T: buf[:, c * w:(c + 1) * w]
    tp_ps = p.psum([128, 128], BF16, "tp")
    tpf_ps = p.psum([128, 128], F32, "tpf")
    mm_ps = p.psum([128, 512], F32, "mm")

    eps_col = p.sbuf([128, 1], F32, "lneps")
    p.step("vector", lambda e: e.memset(eps_col[:], LN_EPS))
    ln_cols = _mk_ln_cols(p)
    ntile_ = T // 128
    selT = p.sbuf([128, 8 * ntile_], F32, "selT")
    xn1T = p.sbuf([128, 8 * T], BF16, "xn1T")
    xn1loT = p.sbuf([128, 8 * T], BF16, "xn1loT")

    x1T = p.sbuf([128, 8 * T], F32, "x1T")
    spt_s = p.sbuf([128, 8 * T], BF16, "spts")
    for c in range(8):
        p.dma("sync", TC(spt_s, c), spt_d[128 * c:128 * (c + 1), :])
    woT = p.sbuf([128, 8 * D], BF16, "woT")
    wrow = p.sbuf([128, D], F32, "wrow_a")
    wrb = p.sbuf([128, D], BF16, "wrb_a")
    for r in range(8):
        p.dma("sync", wrow[:], wo_d[128 * r:128 * (r + 1), :])
        p.step("vector", lambda e: e.tensor_copy(wrb[:], wrow[:]))
        _transpose_tile(p, ident,
                        lambda kc, r=r: TC(woT, kc, D)[:, 128 * r:128 * (r + 1)],
                        lambda kc: wrb[:, 128 * kc:128 * (kc + 1)], tp_ps, 8)
    xg_x = p.sbuf([128, T], F32, "xgx")
    xg_g = p.sbuf([128, T], F32, "xgg")
    stmp = p.sbuf([128, 512], F32, "stmp")
    for ec in range(8):
        p.dma("sync", xg_x[:], xt_d[128 * ec:128 * (ec + 1), :])
        p.dma("sync", xg_g[:], gt_d[128 * ec:128 * (ec + 1), :])
        for tg in range(T // 512):
            def mmS(e, ec=ec, tg=tg):
                for jc in range(8):
                    inst = e.matmul(mm_ps[:],
                                    TC(woT, jc, D)[:, 128 * ec:128 * (ec + 1)],
                                    TC(spt_s, jc)[:, 512 * tg:512 * (tg + 1)],
                                    start=(jc == 0), stop=(jc == 7))
                return inst
            p.step("tensor", mmS)
            p.step("vector", lambda e, tg=tg: e.scalar_tensor_tensor(
                stmp[:], mm_ps[:], 1.0, xg_g[:, 512 * tg:512 * (tg + 1)],
                ALU.mult, ALU.mult))
            p.step("vector", lambda e, ec=ec, tg=tg: e.tensor_tensor(
                TC(x1T, ec)[:, 512 * tg:512 * (tg + 1)], stmp[:],
                xg_x[:, 512 * tg:512 * (tg + 1)], ALU.add))
    p.free(stmp, xg_g, xg_x, wrb, wrow, woT, spt_s)

    x1t_f = p.sbuf([128, D], F32, "x1tf")
    xn1_f = p.sbuf([128, D], F32, "xn1f")
    scrf = p.sbuf([128, D], F32, "scrf")
    xn1b = p.sbuf([128, D], BF16, "xn1b")
    xn1lo = p.sbuf([128, D], BF16, "xn1lo")
    for t in range(ntile):
        for kc in range(8):
            p.step("tensor", lambda e, kc=kc, t=t: e.transpose(
                tpf_ps[:], TC(x1T, kc)[:, 128 * t:128 * (t + 1)], identf[:]))
            p.step("vector", lambda e, kc=kc: e.tensor_copy(
                x1t_f[:, 128 * kc:128 * (kc + 1)], tpf_ps[:]))
        p.dma("gpsimd", x1_d[128 * t:128 * (t + 1), :], x1t_f[:])
        _ln_tile(p, xn1_f[:], x1t_f[:], ln_cols, eps_col, scrf[:])
        p.step("vector", lambda e: e.tensor_copy(xn1b[:], xn1_f[:]))
        p.step("vector", lambda e: e.tensor_sub(xn1lo[:], xn1_f[:], xn1b[:]))
        _transpose_tile(p, ident,
                        lambda kc, t=t: TC(xn1T, kc)[:, 128 * t:128 * (t + 1)],
                        lambda kc: xn1b[:, 128 * kc:128 * (kc + 1)], tp_ps, 8)
        _transpose_tile(p, ident,
                        lambda kc, t=t: TC(xn1loT, kc)[:, 128 * t:128 * (t + 1)],
                        lambda kc: xn1lo[:, 128 * kc:128 * (kc + 1)], tp_ps, 8)
    p.free(xn1lo, xn1b, scrf, xn1_f, x1t_f)
    p.free(x1T)

    wqkv_s = p.sbuf([128, 8 * 3 * D], BF16, "wqkvs")
    W3C = lambda c: wqkv_s[:, c * 3 * D:(c + 1) * 3 * D]
    wrow3 = p.sbuf([128, 3 * D], F32, "wrow3")
    for r in range(8):
        p.dma("sync", wrow3[:], wqkv_d[128 * r:128 * (r + 1), :])
        def cv(e, r=r):
            e.tensor_copy(W3C(r)[:, 0:2 * D], wrow3[:, 0:2 * D])
            return e.tensor_scalar(W3C(r)[:, D:2 * D], wrow3[:, D:2 * D],
                                   -2.0, None, ALU.mult)
        def cv2(e, r=r):
            return e.tensor_copy(W3C(r)[:, 2 * D:3 * D], wrow3[:, 2 * D:3 * D])
        p.step("vector", cv)
        p.step("vector", cv2)
    bq_s = p.sbuf([128, 16], F32, "bqs")
    for ec in range(8):
        p.dma("sync", bq_s[:, ec:ec + 1], bqkv_d[128 * ec:128 * (ec + 1)].unsqueeze(1))
    for ec in range(8):
        p.dma("sync", bq_s[:, 8 + ec:9 + ec],
              bqkv_d[D + 128 * ec:D + 128 * (ec + 1)].unsqueeze(1))
    p.step("vector", lambda e: e.tensor_scalar(bq_s[:, 8:16], bq_s[:, 8:16],
                                               -2.0, None, ALU.mult))
    outqb = p.sbuf([128, T], BF16, "outqb")
    for (dst, coff, bcol_off) in ((q1_d, 0, 0), (k1_d, D, 8)):
        for ec in range(8):
            for tg in range(T // 512):
                def mmq(e, ec=ec, tg=tg, coff=coff):
                    for dc in range(8):
                        inst = e.matmul(
                            mm_ps[:],
                            W3C(dc)[:, coff + 128 * ec:coff + 128 * (ec + 1)],
                            TC(xn1T, dc)[:, 512 * tg:512 * (tg + 1)],
                            start=(dc == 0), stop=(dc == 7))
                    return inst
                p.step("tensor", mmq)
                p.step("scalar", lambda e, tg=tg, ec=ec, bo=bcol_off: e.activation(
                    outqb[:, 512 * tg:512 * (tg + 1)], mm_ps[:], AF.Identity,
                    bias=bq_s[:, bo + ec:bo + ec + 1], scale=1.0))
            p.dma("gpsimd", dst[128 * ec:128 * (ec + 1), :], outqb[:])
    onesrow = p.sbuf([1, 128], BF16, "onesr")
    p.step("vector", lambda e: e.memset(onesrow[:], 1.0))
    bvrow = p.sbuf([1, D], F32, "bvrow")
    p.dma("sync", bvrow[:], bqkv_d[2 * D:3 * D].unsqueeze(0))
    bvrow_b = p.sbuf([1, D], BF16, "bvrowb")
    p.step("vector", lambda e: e.tensor_copy(bvrow_b[:], bvrow[:]))
    for t in range(ntile):
        for eg in range(2):
            def mmv(e, t=t, eg=eg):
                for dc in range(8):
                    e.matmul(mm_ps[:],
                             TC(xn1T, dc)[:, 128 * t:128 * (t + 1)],
                             W3C(dc)[:, 2 * D + 512 * eg:2 * D + 512 * (eg + 1)],
                             start=(dc == 0), stop=False)
                return e.matmul(mm_ps[:], onesrow[:],
                                bvrow_b[:, 512 * eg:512 * (eg + 1)],
                                start=False, stop=True)
            p.step("tensor", mmv)
            p.step("vector", lambda e, eg=eg: e.tensor_copy(
                outqb[:, 512 * eg:512 * (eg + 1)], mm_ps[:]))
        p.dma("gpsimd", v1_d[128 * t:128 * (t + 1), :], outqb[:, 0:D])
    p.free(bvrow_b, bvrow, onesrow, outqb, bq_s, wrow3, wqkv_s)

    wg_s = p.sbuf([128, 8 * 8], F32, "wgs")
    for dc in range(8):
        p.dma("sync", wg_s[:, 8 * dc:8 * (dc + 1)], wg_d[128 * dc:128 * (dc + 1), :])
    wg_hi = p.sbuf([128, 8 * 8], BF16, "wghi")
    wg_lo = p.sbuf([128, 8 * 8], BF16, "wglo")
    p.step("vector", lambda e: e.tensor_copy(wg_hi[:], wg_s[:]))
    p.step("vector", lambda e: e.tensor_sub(wg_lo[:], wg_s[:], wg_hi[:]))
    lg_ps = p.psum([8, 512], F32, "lgps")
    lgT = p.sbuf([8, T], F32, "lgT")
    for tg in range(T // 512):
        def mmlg(e, tg=tg):
            combos = []
            for dc in range(8):
                combos.append((wg_hi[:, 8 * dc:8 * (dc + 1)], TC(xn1T, dc)))
                combos.append((wg_lo[:, 8 * dc:8 * (dc + 1)], TC(xn1T, dc)))
                combos.append((wg_hi[:, 8 * dc:8 * (dc + 1)], TC(xn1loT, dc)))
            for i, (wb, xb) in enumerate(combos):
                inst = e.matmul(lg_ps[:], wb, xb[:, 512 * tg:512 * (tg + 1)],
                                start=(i == 0), stop=(i == len(combos) - 1))
            return inst
        p.step("tensor", mmlg)
        p.step("vector", lambda e, tg=tg: e.tensor_copy(
            lgT[:, 512 * tg:512 * (tg + 1)], lg_ps[:]))
    bg_s = p.sbuf([1, 8], F32, "bgs")
    p.dma("sync", bg_s[:], bg_d[:])
    ones1f = p.sbuf([1, 128], F32, "ones1f")
    p.step("vector", lambda e: e.memset(ones1f[:], 1.0))
    bgb_ps = p.psum([128, 8], F32, "bgb")
    bgb = p.sbuf([128, 8], F32, "bgbs")
    p.step("tensor", lambda e: e.matmul(bgb_ps[:], ones1f[:], bg_s[:],
                                        start=True, stop=True))
    p.step("vector", lambda e: e.tensor_copy(bgb[:], bgb_ps[:]))
    lgt_f = p.sbuf([128, 8], F32, "lgtf")
    mcol = p.sbuf([128, 1], F32, "mcol")
    for t in range(ntile):
        p.step("tensor", lambda e, t=t: e.transpose(
            tpf_ps[:, 0:8], lgT[:, 128 * t:128 * (t + 1)], identf[0:8, 0:8]))
        p.step("vector", lambda e: e.tensor_copy(lgt_f[:], tpf_ps[:, 0:8]))
        p.step("vector", lambda e: e.tensor_tensor(lgt_f[:], lgt_f[:], bgb[:], ALU.add))
        p.step("vector", lambda e: e.tensor_reduce(mcol[:], lgt_f[:], AX.X, ALU.max))
        p.step("vector", lambda e, t=t: e.tensor_scalar(
            selT[:, 8 * t:8 * (t + 1)], lgt_f[:], mcol[:], None, ALU.is_ge))
    p.free(mcol, lgt_f, bgb, ones1f, bg_s, lgT, wg_lo, wg_hi, wg_s)
    p.free(xn1loT)

    yacc = p.sbuf([128, ntile * D], F32, "yacc")
    p.step("vector", lambda e: e.memset(yacc[:], 0.0))
    heT = p.sbuf([128, 8 * T], BF16, "heT")
    w1b = p.sbuf([128, 8 * D], BF16, "w1b")
    w2b = p.sbuf([128, 8 * D], BF16, "w2b")
    wrowm = p.sbuf([128, D], F32, "wrowm")
    b1c = p.sbuf([128, 8], F32, "b1c")
    b2row = p.sbuf([1, D], F32, "b2row")
    b2row_b = p.sbuf([1, D], BF16, "b2rowb")
    onesrow2 = p.sbuf([1, 128], BF16, "onesr2")
    p.step("vector", lambda e: e.memset(onesrow2[:], 1.0))
    ytmp = p.sbuf([128, 512], F32, "ytmp")
    for ex in range(8):
        for r in range(8):
            p.dma("sync", wrowm[:], w1_d[ex, 128 * r:128 * (r + 1), :])
            p.step("vector", lambda e, r=r: e.tensor_copy(
                w1b[:, D * r:D * (r + 1)], wrowm[:]))
        for r in range(8):
            p.dma("sync", wrowm[:], w2_d[ex, 128 * r:128 * (r + 1), :])
            p.step("vector", lambda e, r=r: e.tensor_copy(
                w2b[:, D * r:D * (r + 1)], wrowm[:]))
        for ic in range(8):
            p.dma("sync", b1c[:, ic:ic + 1],
                  b1_d[ex, 128 * ic:128 * (ic + 1)].unsqueeze(1))
        p.dma("sync", b2row[:], b2_d[ex:ex + 1, :])
        p.step("vector", lambda e: e.tensor_copy(b2row_b[:], b2row[:]))
        for ic in range(8):
            for tg in range(T // 512):
                def mmh(e, ic=ic, tg=tg):
                    for dc in range(8):
                        inst = e.matmul(
                            mm_ps[:],
                            w1b[:, D * dc + 128 * ic:D * dc + 128 * (ic + 1)],
                            TC(xn1T, dc)[:, 512 * tg:512 * (tg + 1)],
                            start=(dc == 0), stop=(dc == 7))
                    return inst
                p.step("tensor", mmh)
                p.step("scalar", lambda e, ic=ic, tg=tg: e.activation(
                    TC(heT, ic)[:, 512 * tg:512 * (tg + 1)], mm_ps[:],
                    AF.Gelu, bias=b1c[:, ic:ic + 1], scale=1.0))
        for t in range(ntile):
            for eg in range(2):
                def mmy(e, t=t, eg=eg):
                    for ic in range(8):
                        e.matmul(mm_ps[:],
                                 TC(heT, ic)[:, 128 * t:128 * (t + 1)],
                                 w2b[:, D * ic + 512 * eg:D * ic + 512 * (eg + 1)],
                                 start=(ic == 0), stop=False)
                    return e.matmul(mm_ps[:], onesrow2[:],
                                    b2row_b[:, 512 * eg:512 * (eg + 1)],
                                    start=False, stop=True)
                p.step("tensor", mmy)
                p.step("vector", lambda e, t=t, eg=eg, ex=ex: e.tensor_scalar(
                    ytmp[:], mm_ps[:], selT[:, 8 * t + ex:8 * t + ex + 1],
                    None, ALU.mult))
                p.step("vector", lambda e, t=t, eg=eg: e.tensor_tensor(
                    yacc[:, D * t + 512 * eg:D * t + 512 * (eg + 1)],
                    yacc[:, D * t + 512 * eg:D * t + 512 * (eg + 1)],
                    ytmp[:], ALU.add))
    g1b = p.sbuf([128, D], F32, "g1b")
    for t in range(ntile):
        p.step("scalar", lambda e, t=t: e.activation(
            g1b[:], yacc[:, D * t:D * (t + 1)], AF.Silu, bias=0.0, scale=1.0))
        p.dma("gpsimd", g1_d[128 * t:128 * (t + 1), :], g1b[:])
    return p.finalize(), p.out_names


_DBG = False


def build_l1_attn(nheads=4):
    p = _P()
    HB = nheads * 64
    qh_d = p.dram_in("QHT", (HB, T), BF16)
    kh_d = p.dram_in("KHT", (HB, S), BF16)
    v_d = p.dram_in("Vb", (S, HB), BF16)
    qt_d = p.dram_in("qtrueb", (128, T), F32)
    selc_d = p.dram_in("selc", (128, 4), F32)
    eyeb_d = p.dram_in("eyeb", (128, 128), BF16)
    eyef_d = p.dram_in("eyef", (128, 128), F32)
    taus_d = p.dram_in("tausin", (128, S // 128), F32)
    ao_d = p.dram_out("AOTp", (HB, T), BF16)
    khscr = p.dram_tmp("khscr", (nheads, 66, S), BF16)
    dbg = p.dram_out("DBG", (66, 512), F32) if _DBG else None
    dbg2 = p.dram_out("DBG2", (128, 1024), F32) if _DBG else None

    QG = 512
    nqg = T // QG
    nkb = S // 128

    identf = _ident(p, F32)
    ones64f = p.sbuf([64, 1], F32, "o64")
    p.step("vector", lambda e: e.memset(ones64f[:], 1.0))
    ones1f = p.sbuf([1, 64], F32, "o1f")
    p.step("vector", lambda e: e.memset(ones1f[:], 1.0))
    nc1 = p.sbuf([128, 1], F32, "nc1")
    p.step("vector", lambda e: e.memset(nc1[:], -C1))
    qtrue = p.sbuf([128, T], F32, "qtrue")
    p.dma("sync", qtrue[:], qt_d[:])
    taus = p.sbuf([128, nkb], F32, "taus")
    p.dma("sync", taus[:], taus_d[:])

    QHs = [p.sbuf([66, T], BF16, f"QH{h}") for h in range(nheads)]
    onesT = p.sbuf([1, T], BF16, "onesT")
    p.step("vector", lambda e: e.memset(onesT[:], 1.0))
    qsq = p.sbuf([64, T], F32, "qsq")
    qn_ps = p.psum([1, QG], F32, "qnps")
    for h in range(nheads):
        p.dma("sync", QHs[h][0:64, :], qh_d[64 * h:64 * (h + 1), :])
        p.step("scalar", lambda e, h=h: e.activation(
            qsq[:], QHs[h][0:64, :], AF.Square, bias=0.0, scale=1.0))
        p.dma("gpsimd", QHs[h][65:66, :], onesT[:, 0:T])
        for tg in range(nqg):
            p.step("tensor", lambda e, tg=tg: e.matmul(
                qn_ps[:], ones64f[:], qsq[:, QG * tg:QG * (tg + 1)],
                start=True, stop=True))
            p.step("vector", lambda e, h=h, tg=tg: e.tensor_copy(
                QHs[h][64:65, QG * tg:QG * (tg + 1)], qn_ps[:]))
    p.free(qsq)

    KH = p.sbuf([66, S], BF16, "KH")
    ksq = p.sbuf([64, 512], F32, "ksq")
    kn_ps = qn_ps
    knrow = p.sbuf([1, 512], F32, "knrow")
    knrow_b = p.sbuf([1, S], BF16, "knrowb")
    maskb = p.sbuf([128, nkb * 512], BF16, "maskb")
    mtmp = p.sbuf([128, 512], F32, "mtmp")
    vbuf = p.sbuf([128, 65], BF16, "vbuf")
    p.step("vector", lambda e: e.memset(vbuf[:, 64:65], 1.0))
    den_ps = p.psum([128, 1024], F32, "denps")
    dot_ps = p.psum([128, 1024], F32, "dotps")
    o_ps = p.psum([65, QG], F32, "ops")
    rlb_ps = p.psum([64, QG], F32, "rlbps")
    bln = p.sbuf([128, 1024], F32, "bln")
    rln = p.sbuf([128, 1024], F32, "rln")
    m1 = p.sbuf([128, 1024], F32, "m1")
    uu = p.sbuf([128, 1024], F32, "uu")
    pt = p.sbuf([128, 1024], BF16, "pt")
    ohl = p.sbuf([65, QG], F32, "ohl")
    dbg2s = p.sbuf([128, 1024], F32, "dbg2s") if _DBG else None
    rl = p.sbuf([1, QG], F32, "rl")
    p64c = p.sbuf([1, 1], F32, "p64c")
    p.step("vector", lambda e: e.memset(p64c[:], 44.3614195558365))
    aob = p.sbuf([64, QG], BF16, "aob")

    for qg in range(nqg):
        for kb in range(nkb):
            p.step("vector", lambda e, kb=kb, qg=qg: e.tensor_scalar(
                mtmp[:], qtrue[:, QG * qg:QG * (qg + 1)], taus[:, kb:kb + 1],
                None, ALU.is_ge))
            p.step("vector", lambda e, kb=kb: e.tensor_copy(
                maskb[:, 512 * kb:512 * (kb + 1)], mtmp[:]))
        for h in range(nheads):
            if qg == 0:
                p.dma("sync", KH[0:64, :], kh_d[64 * h:64 * (h + 1), :])
                p.step("vector", lambda e: e.memset(KH[64:65, :], 1.0))
                for sg in range(S // 512):
                    p.step("scalar", lambda e, sg=sg: e.activation(
                        ksq[:], KH[0:64, 512 * sg:512 * (sg + 1)], AF.Square,
                        bias=0.0, scale=0.5))
                    p.step("tensor", lambda e: e.matmul(
                        kn_ps[:], ones64f[:], ksq[:], start=True, stop=True))
                    p.step("vector", lambda e: e.tensor_scalar(
                        knrow[:], kn_ps[:], ATT_EPS, None, ALU.add))
                    p.step("vector", lambda e, sg=sg: e.tensor_copy(
                        knrow_b[:, 512 * sg:512 * (sg + 1)], knrow[:]))
                p.dma("gpsimd", KH[65:66, :], knrow_b[:])
                p.dma("gpsimd", khscr[h], KH[:])
            else:
                p.dma("sync", KH[:], khscr[h])
            for kbp in range(nkb // 2):
                def mms(e, kbp=kbp, qg=qg, h=h):
                    for half in range(2):
                        kb = 2 * kbp + half
                        e.matmul(den_ps[:, 512 * half:512 * (half + 1)],
                                 KH[0:66, 128 * kb:128 * (kb + 1)],
                                 QHs[h][0:66, QG * qg:QG * (qg + 1)],
                                 start=True, stop=True)
                        inst = e.matmul(dot_ps[:, 512 * half:512 * (half + 1)],
                                        KH[0:64, 128 * kb:128 * (kb + 1)],
                                        QHs[h][0:64, QG * qg:QG * (qg + 1)],
                                        start=True, stop=True)
                    return inst
                p.step("tensor", mms)

                def sln(e):
                    e.activation(bln[:], den_ps[:], AF.Ln, bias=0.0, scale=1.0)
                    return e.activation(rln[:], bln[:], AF.Exp, bias=0.0, scale=-1.0)
                p.step("scalar", sln)

                def vm(e):
                    e.tensor_tensor(m1[:], dot_ps[:], rln[:], ALU.mult)
                    return e.tensor_tensor(uu[:], m1[:], dot_ps[:], ALU.mult)
                p.step("vector", vm)
                p.step("scalar", lambda e: e.activation(
                    pt[:], uu[:], AF.Exp, bias=nc1[:], scale=0.25))
                p.step("vector", lambda e, kbp=kbp: e.tensor_tensor(
                    pt[:], pt[:], maskb[:, 1024 * kbp:1024 * (kbp + 1)], ALU.mult))
                if _DBG and h == 0 and qg == 0 and kbp == 0:
                    p.step("vector", lambda e: e.tensor_copy(dbg2s[:], pt[:]))
                    p.dma("gpsimd", dbg2[:], dbg2s[:])
                for half in range(2):
                    kb = 2 * kbp + half
                    p.dma("sync", vbuf[:, 0:64],
                          v_d[128 * kb:128 * (kb + 1), 64 * h:64 * (h + 1)])
                    p.step("tensor", lambda e, kb=kb, half=half: e.matmul(
                        o_ps[:], vbuf[:], pt[:, 512 * half:512 * (half + 1)],
                        start=(kb == 0), stop=(kb == nkb - 1)))
            p.step("vector", lambda e: e.tensor_copy(ohl[:], o_ps[:]))
            if _DBG and h == 0 and qg == 0:
                p.dma("gpsimd", dbg[0:65, :], ohl[:])
            def srl(e):
                e.activation(rl[:], ohl[64:65, :], AF.Ln, bias=0.0,
                             scale=18446744073709551616.0)
                return e.activation(rl[:], rl[:], AF.Exp, bias=p64c[0:1, :],
                                    scale=-1.0)
            p.step("scalar", srl)
            p.step("tensor", lambda e: e.matmul(rlb_ps[:], ones1f[:], rl[:],
                                                start=True, stop=True))
            p.step("vector", lambda e: e.tensor_tensor(
                aob[:], ohl[0:64, :], rlb_ps[:], ALU.mult))
            p.dma("gpsimd", ao_d[64 * h:64 * (h + 1), QG * qg:QG * (qg + 1)],
                  aob[:])
    return p.finalize(), p.out_names


def build_l1_tail():
    p = _P()
    ao_d = p.dram_in("AOT", (D, T), BF16)
    wo_d = p.dram_in("w_out", (D, D), F32)
    bo_d = p.dram_in("b_out", (D,), F32)
    x1_d = p.dram_in("x1", (T, D), F32)
    g1_d = p.dram_in("g1", (T, D), F32)
    out_d = p.dram_out("out", (T, D), F32)

    identf = _ident(p, F32)
    ntile = T // 128
    TC = lambda buf, c, w=T: buf[:, c * w:(c + 1) * w]
    mm_ps = p.psum([128, 512], F32, "mm")
    tpf_ps = p.psum([128, 128], F32, "tpf")

    ao_s = p.sbuf([128, 8 * T], BF16, "aos")
    for c in range(8):
        p.dma("sync", TC(ao_s, c), ao_d[128 * c:128 * (c + 1), :])
    w_s = p.sbuf([128, 8 * D], BF16, "ws")
    wrow = p.sbuf([128, D], F32, "wr")
    for r in range(8):
        p.dma("sync", wrow[:], wo_d[128 * r:128 * (r + 1), :])
        p.step("vector", lambda e, r=r: e.tensor_copy(w_s[:, D * r:D * (r + 1)],
                                                      wrow[:]))
    bo_s = p.sbuf([128, 8], F32, "bos")
    for ec in range(8):
        p.dma("sync", bo_s[:, ec:ec + 1], bo_d[128 * ec:128 * (ec + 1)].unsqueeze(1))
    s1T = p.sbuf([128, 8 * T], F32, "s1T")
    for ec in range(8):
        for tg in range(T // 512):
            def mmo(e, ec=ec, tg=tg):
                for dc in range(8):
                    inst = e.matmul(mm_ps[:],
                                    w_s[:, D * dc + 128 * ec:D * dc + 128 * (ec + 1)],
                                    TC(ao_s, dc)[:, 512 * tg:512 * (tg + 1)],
                                    start=(dc == 0), stop=(dc == 7))
                return inst
            p.step("tensor", mmo)
            p.step("scalar", lambda e, ec=ec, tg=tg: e.activation(
                TC(s1T, ec)[:, 512 * tg:512 * (tg + 1)], mm_ps[:], AF.Identity,
                bias=bo_s[:, ec:ec + 1], scale=1.0))
    x1f = p.sbuf([128, D], F32, "x1f")
    g1f = p.sbuf([128, D], F32, "g1f")
    s1f = p.sbuf([128, D], F32, "s1f")
    of = p.sbuf([128, D], F32, "of")
    for t in range(ntile):
        p.dma("sync", x1f[:], x1_d[128 * t:128 * (t + 1), :])
        p.dma("sync", g1f[:], g1_d[128 * t:128 * (t + 1), :])
        for kc in range(8):
            p.step("tensor", lambda e, kc=kc, t=t: e.transpose(
                tpf_ps[:], TC(s1T, kc)[:, 128 * t:128 * (t + 1)], identf[:]))
            p.step("vector", lambda e, kc=kc: e.tensor_copy(
                s1f[:, 128 * kc:128 * (kc + 1)], tpf_ps[:]))
        p.step("vector", lambda e: e.tensor_tensor(of[:], s1f[:], g1f[:], ALU.mult))
        p.step("vector", lambda e: e.tensor_tensor(of[:], of[:], x1f[:], ALU.add))
        p.dma("gpsimd", out_d[128 * t:128 * (t + 1), :], of[:])
    return p.finalize(), p.out_names


MEGA_UPTO = 99
SKIP_COLL = False


def _mega_stub_out(p, out_d):
    z = p.sbuf([128, D], F32, "stubz")
    p.step("vector", lambda e: e.memset(z[:], 0.0))
    for t in range(T // 128):
        p.dma("gpsimd", out_d[128 * t:128 * (t + 1), :], z[:])
    return p.finalize()


def build_mega():
    p = _P()
    nc = p.nc
    x_d = p.dram_in("x", (T, D), F32)
    wq_d = p.dram_in("wq", (D, D), F32)
    wk_d = p.dram_in("wk", (D, D), F32)
    wv_d = p.dram_in("wv", (D, D), F32)
    wg_d = p.dram_in("wg", (I0, D), F32)
    wu_d = p.dram_in("wu", (I0, D), F32)
    wd_d = p.dram_in("wd", (D, I0), F32)
    wo_d = p.dram_in("wo", (D, D), F32)
    wqkv_d = p.dram_in("wqkv", (D, 3 * D), F32)
    bqkv_d = p.dram_in("bqkv", (3 * D,), F32)
    wgate_d = p.dram_in("Wg", (D, 8), F32)
    bg_d = p.dram_in("bg", (1, 8), F32)
    w1_d = p.dram_in("W1", (8, D, D), F32)
    b1_d = p.dram_in("b1", (8, D), F32)
    w2_d = p.dram_in("W2", (8, D, D), F32)
    b2_d = p.dram_in("b2", (8, D), F32)
    wout_d = p.dram_in("w_out", (D, D), F32)
    bout_d = p.dram_in("b_out", (D,), F32)
    qt_d = p.dram_in("qtrueb", (128, T), F32)
    selc_d = p.dram_in("selc", (128, 4), F32)
    eyeb_d = p.dram_in("eyeb", (128, 128), BF16)
    eyef_d = p.dram_in("eyef", (128, 128), F32)
    taus_d = p.dram_in("tausin", (128, S // 128), F32)
    out_d = p.dram_out("out", (T, D), F32)

    qt0_d = p.dram_tmp("sQT", (D, T), BF16)
    kt0_d = p.dram_tmp("sKT", (D, T), BF16)
    v0_d = p.dram_tmp("sV", (T, D), BF16)
    g_d = p.dram_tmp("sg", (T, D), F32)
    hq_d = p.dram_tmp("hqT", (I0, T), BF16)
    gk0_d = p.dram_tmp("gK0", (4 * D, T), BF16)
    gv0_d = p.dram_tmp("gV0", (4 * T, D), BF16)
    aot_d = p.dram_tmp("sAOT", (D, T), BF16)
    gat_d = p.dram_tmp("gAT", (4 * D, T), BF16)
    x1_d = p.dram_tmp("sx1", (T, D), F32)
    xn1_d = p.dram_tmp("sxn1", (T, D), F32)
    g1_d = p.dram_tmp("sg1", (T, D), F32)
    q1_d = p.dram_tmp("sQ1T", (D, T), BF16)
    k1_d = p.dram_tmp("sK1T", (D, T), BF16)
    v1_d = p.dram_tmp("sV1", (T, D), BF16)
    gk1_d = p.dram_tmp("gK1", (4 * D, T), BF16)
    gv1_d = p.dram_tmp("gV1", (4 * T, D), BF16)
    ao1_d = p.dram_tmp("sAO1", (D, T), BF16)
    khscr = p.dram_tmp("khscr", (NH, 66, S), BF16)

    GRP = [[0, 1, 2, 3], [4, 5, 6, 7]]

    def collective(kind, in_ap, out_ap):
        if SKIP_COLL:
            return
        p.step("gpsimd", lambda e: e.collective_compute(
            kind, ALU.bypass, GRP, [in_ap.opt()], [out_ap.opt()]), inc=1,
            nodrain=True)

    ident = p.sbuf([128, 128], BF16, "identb")
    p.dma("sync", ident[:], eyeb_d[:])
    identf = p.sbuf([128, 128], F32, "identfs")
    p.dma("sync", identf[:], eyef_d[:])
    tpf_ps = p.psum([128, 128], F32, "tpf")
    tp_ps = p.psum([128, 128], BF16, "tp")
    mm_ps = p.psum([128, 512], F32, "mm")
    ntile = T // 128
    TC = lambda buf, c, w=T: buf[:, c * w:(c + 1) * w]

    eps_col = p.sbuf([128, 1], F32, "lneps")
    p.step("vector", lambda e: e.memset(eps_col[:], LN_EPS))
    ln_cols = _mk_ln_cols(p)
    q = _Quant(p, I0)

    xnqT = p.sbuf([128, 8 * T], BF16, "xnqT")
    xnT = p.sbuf([128, 8 * T], BF16, "xnT")
    xf = p.sbuf([128, D], F32, "xf")
    xn = p.sbuf([128, D], F32, "xn")
    xnb = p.sbuf([128, D], BF16, "xnb")
    xnqb = p.sbuf([128, D], BF16, "xnqb")
    for t in range(ntile):
        p.dma("sync", xf[:], x_d[128 * t:128 * (t + 1), :])
        _ln_tile(p, xn[:], xf[:], ln_cols, eps_col, q.W[0][:, 0:D])
        p.step("vector", lambda e: e.tensor_copy(xnb[:], xn[:]))
        q.emit(xnqb[:], xn[:], D)
        _transpose_tile(p, ident,
                        lambda kc, t=t: TC(xnT, kc)[:, 128 * t:128 * (t + 1)],
                        lambda kc: xnb[:, 128 * kc:128 * (kc + 1)], tp_ps, 8)
        _transpose_tile(p, ident,
                        lambda kc, t=t: TC(xnqT, kc)[:, 128 * t:128 * (t + 1)],
                        lambda kc: xnqb[:, 128 * kc:128 * (kc + 1)], tp_ps, 8)
    if MEGA_UPTO < 1.1:
        return _mega_stub_out(p, out_d), p.out_names
    wrow = p.sbuf([128, D], F32, "wrow")
    wrb = p.sbuf([128, D], BF16, "wrb")
    wT = p.sbuf([128, 8 * D], BF16, "wT")
    WC = lambda c: wT[:, c * D:(c + 1) * D]
    outb = p.sbuf([128, T], BF16, "outb")
    for wsrc, wdst, scalef in ((wq_d, qt0_d, 0.125), (wk_d, kt0_d, 1.0)):
        for r in range(8):
            p.dma("sync", wrow[:], wsrc[128 * r:128 * (r + 1), :])
            p.step("vector", lambda e, sf=scalef: e.tensor_scalar(
                wrb[:], wrow[:], sf, None, ALU.mult))
            _transpose_tile(p, ident,
                            lambda kc, r=r: WC(kc)[:, 128 * r:128 * (r + 1)],
                            lambda kc: wrb[:, 128 * kc:128 * (kc + 1)], tp_ps, 8)
        for ec in range(8):
            for tg in range(T // 512):
                def mms(e, ec=ec, tg=tg):
                    for dc in range(8):
                        inst = e.matmul(mm_ps[:],
                                        WC(dc)[:, 128 * ec:128 * (ec + 1)],
                                        TC(xnT, dc)[:, 512 * tg:512 * (tg + 1)],
                                        start=(dc == 0), stop=(dc == 7))
                    return inst
                p.step("tensor", mms)
                p.step("vector", lambda e, tg=tg: e.tensor_copy(
                    outb[:, 512 * tg:512 * (tg + 1)], mm_ps[:]))
            p.dma("gpsimd", wdst[128 * ec:128 * (ec + 1), :], outb[:])
    for r in range(8):
        p.dma("sync", wrow[:], wv_d[128 * r:128 * (r + 1), :])
        p.step("vector", lambda e: e.tensor_copy(wrb[:], wrow[:]))
        _transpose_tile(p, ident,
                        lambda kc, r=r: WC(kc)[:, 128 * r:128 * (r + 1)],
                        lambda kc: wrb[:, 128 * kc:128 * (kc + 1)], tp_ps, 8)
    for t in range(ntile):
        for eg in range(2):
            def mmv(e, t=t, eg=eg):
                for dc in range(8):
                    inst = e.matmul(mm_ps[:],
                                    TC(xnT, dc)[:, 128 * t:128 * (t + 1)],
                                    WC(dc)[:, 512 * eg:512 * (eg + 1)],
                                    start=(dc == 0), stop=(dc == 7))
                return inst
            p.step("tensor", mmv)
            p.step("vector", lambda e, eg=eg: e.tensor_copy(
                outb[:, 512 * eg:512 * (eg + 1)], mm_ps[:]))
        p.dma("gpsimd", v0_d[128 * t:128 * (t + 1), :], outb[:, 0:D])
    p.free(outb, wT, wrb, wrow, xnqb, xnb, xn, xf, xnT)
    if MEGA_UPTO < 1.3:
        return _mega_stub_out(p, out_d), p.out_names
    collective("AllGather", kt0_d, gk0_d)
    collective("AllGather", v0_d, gv0_d)
    if MEGA_UPTO < 1.35:
        return _mega_stub_out(p, out_d), p.out_names
    wgqT = p.sbuf([128, 8 * I0], BF16, "wgqT")
    wuqT = p.sbuf([128, 8 * I0], BF16, "wuqT")
    WGC = lambda buf, c: buf[:, c * I0:(c + 1) * I0]
    wrow2 = p.sbuf([128, D], F32, "wrow2")
    wqb = p.sbuf([128, D], BF16, "wqb")
    for wsrc, wbuf in ((wg_d, wgqT), (wu_d, wuqT)):
        for r in range(I0 // 128):
            p.dma("sync", wrow2[:], wsrc[128 * r:128 * (r + 1), :])
            q.emit(wqb[:], wrow2[:], D)
            _transpose_tile(p, ident,
                            lambda kc, r=r, wbuf=wbuf:
                            WGC(wbuf, kc)[:, 128 * r:128 * (r + 1)],
                            lambda kc: wqb[:, 128 * kc:128 * (kc + 1)], tp_ps, 8)
    gate_ps = p.psum([128, 512], F32, "gate")
    up_ps = p.psum([128, 512], F32, "up")
    sgate = q.W[1]
    hbuf = p.sbuf([128, I0], F32, "hbuf")
    hqb = p.sbuf([128, I0], BF16, "hqb")
    hpc = p.sbuf([128, 128], BF16, "hpc")
    for t in range(ntile):
        for off, w in _nsplit(I0, 512):
            def mmg(e, t=t, off=off, w=w):
                for dc in range(8):
                    e.matmul(gate_ps[:, 0:w],
                             TC(xnqT, dc)[:, 128 * t:128 * (t + 1)],
                             WGC(wgqT, dc)[:, off:off + w],
                             start=(dc == 0), stop=(dc == 7))
                for dc in range(8):
                    inst = e.matmul(up_ps[:, 0:w],
                                    TC(xnqT, dc)[:, 128 * t:128 * (t + 1)],
                                    WGC(wuqT, dc)[:, off:off + w],
                                    start=(dc == 0), stop=(dc == 7))
                return inst
            p.step("tensor", mmg)
            p.step("scalar", lambda e, off=off, w=w: e.activation(
                sgate[:, off:off + w], gate_ps[:, 0:w], AF.Silu, bias=0.0,
                scale=1.0))
            p.step("vector", lambda e, off=off, w=w: e.tensor_tensor(
                hbuf[:, off:off + w], sgate[:, off:off + w], up_ps[:, 0:w],
                ALU.mult))
        q.emit(hqb[:], hbuf[:], I0)
        for kc in range(I0 // 128):
            p.step("tensor", lambda e, kc=kc: e.transpose(
                tp_ps[:], hqb[:, 128 * kc:128 * (kc + 1)], ident[:]))
            p.step("vector", lambda e: e.tensor_copy(hpc[:], tp_ps[:]))
            p.dma("gpsimd",
                  hq_d[128 * kc:128 * (kc + 1), 128 * t:128 * (t + 1)], hpc[:])
    p.free(hpc, hqb, hbuf, wqb, wrow2, wuqT, wgqT, xnqT)
    p.free(up_ps, gate_ps)
    if MEGA_UPTO < 1.5:
        return _mega_stub_out(p, out_d), p.out_names
    wdqT = p.sbuf([128, 22 * D], BF16, "wdqT")
    wdrow = p.sbuf([128, I0], F32, "wdrow")
    wdqb = p.sbuf([128, I0], BF16, "wdqb")
    for r in range(8):
        p.dma("sync", wdrow[:], wd_d[128 * r:128 * (r + 1), :])
        q.emit(wdqb[:], wdrow[:], I0)
        for kc in range(I0 // 128):
            p.step("tensor", lambda e, kc=kc: e.transpose(
                tp_ps[:], wdqb[:, 128 * kc:128 * (kc + 1)], ident[:]))
            p.step("vector", lambda e, kc=kc, r=r: e.tensor_copy(
                wdqT[:, kc * D + 128 * r: kc * D + 128 * (r + 1)], tp_ps[:]))
    p.free(wdqb, wdrow)
    hqT_s = p.sbuf([128, 22 * T], BF16, "hqTs")
    for kc in range(I0 // 128):
        p.dma("sync", hqT_s[:, kc * T:(kc + 1) * T],
              hq_d[128 * kc:128 * (kc + 1), :])
    gout = p.sbuf([128, D], F32, "gout")
    for t in range(ntile):
        for eg in range(2):
            def mmd(e, t=t, eg=eg):
                for kc in range(22):
                    inst = e.matmul(
                        mm_ps[:],
                        hqT_s[:, kc * T + 128 * t: kc * T + 128 * (t + 1)],
                        wdqT[:, kc * D + 512 * eg: kc * D + 512 * (eg + 1)],
                        start=(kc == 0), stop=(kc == 21))
                return inst
            p.step("tensor", mmd)
            p.step("scalar", lambda e, eg=eg: e.activation(
                gout[:, 512 * eg:512 * (eg + 1)], mm_ps[:], AF.Silu,
                bias=0.0, scale=1.0))
        p.dma("gpsimd", g_d[128 * t:128 * (t + 1), :], gout[:])
    p.free(gout, hqT_s, wdqT)
    q.free()

    lnscr = p.sbuf([128, D], F32, "lnscr")

    if MEGA_UPTO < 2:
        return _mega_stub_out(p, out_d), p.out_names
    KSUP = 1024
    QG = 512
    nks = S // KSUP
    nqg = T // QG
    qt_s = p.sbuf([128, 8 * T], BF16, "qts")
    for c in range(8):
        p.dma("sync", TC(qt_s, c), qt0_d[128 * c:128 * (c + 1), :])
    oacc = p.sbuf([128, 8 * T], F32, "oacc")
    p.step("vector", lambda e: e.memset(oacc[:], 0.0))
    lacc = p.sbuf([1, T], F32, "lacc")
    p.step("vector", lambda e: e.memset(lacc[:], 0.0))
    ones_col = p.sbuf([128, 1], BF16, "ones")
    p.step("vector", lambda e: e.memset(ones_col[:], 1.0))
    nc0 = p.sbuf([128, 1], F32, "nc0")
    p.step("vector", lambda e: e.memset(nc0[:], -C0))
    p64c = p.sbuf([1, 1], F32, "p64c")
    p.step("vector", lambda e: e.memset(p64c[:], 44.3614195558365))
    kt_s = p.sbuf([128, 8 * KSUP], BF16, "kts")
    v_s = p.sbuf([128, 8 * D], BF16, "vs")
    pt_s = p.sbuf([128, 8 * QG], BF16, "pts")
    s_ps = p.psum([128, QG], F32, "sps")
    o_ps = p.psum([128, QG], F32, "ops")
    l_ps = p.psum([1, QG], F32, "lps")
    for ks in range(nks):
        rk = ks // 2
        lo = KSUP * (ks % 2)
        for c in range(8):
            p.dma("sync", TC(kt_s, c, KSUP),
                  gk0_d[D * rk + 128 * c:D * rk + 128 * (c + 1), lo:lo + KSUP])
        for kb in range(8):
            p.dma("sync", TC(v_s, kb, D),
                  gv0_d[KSUP * ks + 128 * kb:KSUP * ks + 128 * (kb + 1), :])
        for qg in range(nqg):
            for kb in range(8):
                def mmS(e, kb=kb, qg=qg):
                    for dc in range(8):
                        inst = e.matmul(
                            s_ps[:],
                            TC(kt_s, dc, KSUP)[:, 128 * kb:128 * (kb + 1)],
                            TC(qt_s, dc)[:, QG * qg:QG * (qg + 1)],
                            start=(dc == 0), stop=(dc == 7))
                    return inst
                p.step("tensor", mmS)
                p.step("scalar", lambda e, kb=kb: e.activation(
                    TC(pt_s, kb, QG), s_ps[:], AF.Exp, bias=nc0[:], scale=1.0))
            def mmL(e):
                for kb in range(8):
                    inst = e.matmul(l_ps[:], ones_col[:], TC(pt_s, kb, QG),
                                    start=(kb == 0), stop=(kb == 7))
                return inst
            p.step("tensor", mmL)
            p.step("vector", lambda e, qg=qg: e.tensor_tensor(
                lacc[:, QG * qg:QG * (qg + 1)], lacc[:, QG * qg:QG * (qg + 1)],
                l_ps[:], ALU.add))
            for ec in range(8):
                def mmPV(e, ec=ec):
                    for kb in range(8):
                        inst = e.matmul(o_ps[:],
                                        TC(v_s, kb, D)[:, 128 * ec:128 * (ec + 1)],
                                        TC(pt_s, kb, QG),
                                        start=(kb == 0), stop=(kb == 7))
                    return inst
                p.step("tensor", mmPV)
                p.step("vector", lambda e, ec=ec, qg=qg: e.tensor_tensor(
                    TC(oacc, ec)[:, QG * qg:QG * (qg + 1)],
                    TC(oacc, ec)[:, QG * qg:QG * (qg + 1)], o_ps[:], ALU.add))
    rl0 = p.sbuf([1, T], F32, "rl0")
    def srl0(e):
        e.activation(rl0[:], lacc[:], AF.Ln, bias=0.0,
                     scale=18446744073709551616.0)
        return e.activation(rl0[:], rl0[:], AF.Exp, bias=p64c[0:1, :], scale=-1.0)
    p.step("scalar", srl0)
    rlb_ps2 = p.psum([128, QG], F32, "rlb2")
    rlb0 = p.sbuf([128, T], F32, "rlbs0")
    ones1 = p.sbuf([1, 128], F32, "ones1")
    p.step("vector", lambda e: e.memset(ones1[:], 1.0))
    for qg in range(nqg):
        p.step("tensor", lambda e, qg=qg: e.matmul(
            rlb_ps2[:], ones1[:], rl0[:, QG * qg:QG * (qg + 1)],
            start=True, stop=True))
        p.step("vector", lambda e, qg=qg: e.tensor_copy(
            rlb0[:, QG * qg:QG * (qg + 1)], rlb_ps2[:]))
    aob0 = p.sbuf([128, T], BF16, "aob0")
    for ec in range(8):
        p.step("vector", lambda e, ec=ec: e.tensor_tensor(
            aob0[:], TC(oacc, ec), rlb0[:], ALU.mult))
        p.dma("gpsimd", aot_d[128 * ec:128 * (ec + 1), :], aob0[:])
    p.free(aob0, ones1, rlb0, rl0, pt_s, v_s, kt_s, p64c, nc0, ones_col,
           lacc, oacc, qt_s)
    p.free(rlb_ps2, l_ps, o_ps, s_ps)
    collective("AllGather", aot_d, gat_d)

    if MEGA_UPTO < 3:
        return _mega_stub_out(p, out_d), p.out_names
    spt = p.sbuf([128, 8 * T], BF16, "spt")
    ag_s = p.sbuf([128, 8 * T], BF16, "ags")
    selc_s = p.sbuf([128, 4], F32, "selcs")
    p.dma("sync", selc_s[:], selc_d[:])
    ag4 = p.sbuf([128, 4 * T], BF16, "ag4")
    agmix = p.sbuf([128, T], F32, "agmix")
    for pc in range(8):
        i_snd, half = pc // 2, pc % 2
        for cc in range(4):
            p.dma("sync", ag4[:, T * cc:T * (cc + 1)],
                  gat_d[D * i_snd + 256 * cc + 128 * half:
                        D * i_snd + 256 * cc + 128 * half + 128, :])
        def selmix(e, pc=pc):
            e.tensor_scalar(agmix[:], ag4[:, 0:T], selc_s[:, 0:1], None,
                            ALU.mult)
            for cc in range(1, 4):
                e.scalar_tensor_tensor(agmix[:], ag4[:, T * cc:T * (cc + 1)],
                                       selc_s[:, cc:cc + 1], agmix[:],
                                       ALU.mult, ALU.add)
            return e.tensor_copy(TC(ag_s, pc), agmix[:])
        p.step("vector", selmix)
    p.free(agmix, ag4, selc_s)
    for rr in range(8):
        for jc in range(8):
            sp0 = 1024 * rr + 128 * jc
            rp = sp0 // 2048
            tl0 = sp0 % 2048
            for half in range(2):
                pc = 2 * rp + half
                p.step("tensor", lambda e, pc=pc, tl0=tl0: e.transpose(
                    tp_ps[:], TC(ag_s, pc)[:, tl0:tl0 + 128], ident[:]))
                p.step("vector", lambda e, jc=jc, rr=rr, half=half: e.tensor_copy(
                    TC(spt, jc)[:, 256 * rr + 128 * half:256 * rr + 128 * (half + 1)],
                    tp_ps[:]))
    p.free(ag_s)
    woT = p.sbuf([128, 8 * D], BF16, "woT")
    wrowo = p.sbuf([128, D], F32, "wrowo")
    wrbo = p.sbuf([128, D], BF16, "wrbo")
    for r in range(8):
        p.dma("sync", wrowo[:], wo_d[128 * r:128 * (r + 1), :])
        p.step("vector", lambda e: e.tensor_copy(wrbo[:], wrowo[:]))
        _transpose_tile(p, ident,
                        lambda kc, r=r: TC(woT, kc, D)[:, 128 * r:128 * (r + 1)],
                        lambda kc: wrbo[:, 128 * kc:128 * (kc + 1)], tp_ps, 8)
    sTe = p.sbuf([128, T], F32, "sTe")
    xpi = p.sbuf([128, D], F32, "xpi")
    gpi = p.sbuf([128, D], F32, "gpi")
    x1pi = p.sbuf([128, D], F32, "x1pi")
    xn1pi = p.sbuf([128, D], F32, "xn1pi")
    s1t = p.sbuf([128, D], F32, "s1t")
    sT_all = p.sbuf([128, 8 * T], F32, "sTall")
    for ec in range(8):
        for tg in range(T // 512):
            def mmW(e, ec=ec, tg=tg):
                for jc in range(8):
                    inst = e.matmul(mm_ps[:],
                                    TC(woT, jc, D)[:, 128 * ec:128 * (ec + 1)],
                                    TC(spt, jc)[:, 512 * tg:512 * (tg + 1)],
                                    start=(jc == 0), stop=(jc == 7))
                return inst
            p.step("tensor", mmW)
            p.step("vector", lambda e, ec=ec, tg=tg: e.tensor_copy(
                TC(sT_all, ec)[:, 512 * tg:512 * (tg + 1)], mm_ps[:]))
    xv = x_d.rearrange("(a b) n -> b a n", b=8)
    x1v = x1_d.rearrange("(a b) n -> b a n", b=8)
    xn1v = xn1_d.rearrange("(a b) n -> b a n", b=8)
    gv = g_d.rearrange("(a b) n -> b a n", b=8)
    for rr in range(8):
        for half in range(2):
            al0 = 128 * half
            for ec in range(8):
                p.step("tensor", lambda e, ec=ec, rr=rr, half=half: e.transpose(
                    tpf_ps[:],
                    TC(sT_all, ec)[:, 256 * rr + 128 * half:256 * rr + 128 * (half + 1)],
                    identf[:]))
                p.step("vector", lambda e, ec=ec: e.tensor_copy(
                    s1t[:, 128 * ec:128 * (ec + 1)], tpf_ps[:]))
            p.dma("sync", xpi[:], xv[rr, al0:al0 + 128, :])
            p.dma("sync", gpi[:], gv[rr, al0:al0 + 128, :])
            p.step("vector", lambda e: e.tensor_tensor(x1pi[:], s1t[:], gpi[:],
                                                       ALU.mult))
            p.step("vector", lambda e: e.tensor_tensor(x1pi[:], x1pi[:], xpi[:],
                                                       ALU.add))
            p.dma("gpsimd", x1v[rr, al0:al0 + 128, :], x1pi[:])
            _ln_tile(p, xn1pi[:], x1pi[:], ln_cols, eps_col, lnscr[:])
            p.dma("gpsimd", xn1v[rr, al0:al0 + 128, :], xn1pi[:])
    p.free(sT_all, s1t, xn1pi, x1pi, gpi, xpi, sTe, wrbo, wrowo, woT, spt,
           lnscr)

    if MEGA_UPTO < 4:
        return _mega_stub_out(p, out_d), p.out_names
    xn1T = p.sbuf([128, 8 * T], BF16, "xn1T")
    selT = p.sbuf([128, 8 * ntile], F32, "selT")
    xn1loT = p.sbuf([128, 8 * T], BF16, "xn1loT")
    xn1f = p.sbuf([128, D], F32, "xn1f")
    xn1b = p.sbuf([128, D], BF16, "xn1b")
    xn1lo = p.sbuf([128, D], BF16, "xn1lo")
    for t in range(ntile):
        p.dma("sync", xn1f[:], xn1_d[128 * t:128 * (t + 1), :])
        p.step("vector", lambda e: e.tensor_copy(xn1b[:], xn1f[:]))
        p.step("vector", lambda e: e.tensor_sub(xn1lo[:], xn1f[:], xn1b[:]))
        _transpose_tile(p, ident,
                        lambda kc, t=t: TC(xn1T, kc)[:, 128 * t:128 * (t + 1)],
                        lambda kc: xn1b[:, 128 * kc:128 * (kc + 1)], tp_ps, 8)
        _transpose_tile(p, ident,
                        lambda kc, t=t: TC(xn1loT, kc)[:, 128 * t:128 * (t + 1)],
                        lambda kc: xn1lo[:, 128 * kc:128 * (kc + 1)], tp_ps, 8)
    p.free(xn1lo, xn1b, xn1f)
    wqkv_s = p.sbuf([128, 8 * 3 * D], BF16, "wqkvs")
    W3C = lambda c: wqkv_s[:, c * 3 * D:(c + 1) * 3 * D]
    wrow3 = p.sbuf([128, 3 * D], F32, "wrow3")
    for r in range(8):
        p.dma("sync", wrow3[:], wqkv_d[128 * r:128 * (r + 1), :])
        def cv(e, r=r):
            e.tensor_copy(W3C(r)[:, 0:D], wrow3[:, 0:D])
            e.tensor_scalar(W3C(r)[:, D:2 * D], wrow3[:, D:2 * D], -2.0, None,
                            ALU.mult)
            return e.tensor_copy(W3C(r)[:, 2 * D:3 * D], wrow3[:, 2 * D:3 * D])
        p.step("vector", cv)
    bq_s = p.sbuf([128, 16], F32, "bqs")
    for ec in range(8):
        p.dma("sync", bq_s[:, ec:ec + 1],
              bqkv_d[128 * ec:128 * (ec + 1)].unsqueeze(1))
    for ec in range(8):
        p.dma("sync", bq_s[:, 8 + ec:9 + ec],
              bqkv_d[D + 128 * ec:D + 128 * (ec + 1)].unsqueeze(1))
    p.step("vector", lambda e: e.tensor_scalar(bq_s[:, 8:16], bq_s[:, 8:16],
                                               -2.0, None, ALU.mult))
    outqb = p.sbuf([128, T], BF16, "outqb")
    for (dst, coff, bo) in ((q1_d, 0, 0), (k1_d, D, 8)):
        for ec in range(8):
            for tg in range(T // 512):
                def mmq(e, ec=ec, tg=tg, coff=coff):
                    for dc in range(8):
                        inst = e.matmul(
                            mm_ps[:],
                            W3C(dc)[:, coff + 128 * ec:coff + 128 * (ec + 1)],
                            TC(xn1T, dc)[:, 512 * tg:512 * (tg + 1)],
                            start=(dc == 0), stop=(dc == 7))
                    return inst
                p.step("tensor", mmq)
                p.step("scalar", lambda e, tg=tg, ec=ec, bo=bo: e.activation(
                    outqb[:, 512 * tg:512 * (tg + 1)], mm_ps[:], AF.Identity,
                    bias=bq_s[:, bo + ec:bo + ec + 1], scale=1.0))
            p.dma("gpsimd", dst[128 * ec:128 * (ec + 1), :], outqb[:])
    onesrow = p.sbuf([1, 128], BF16, "onesr")
    p.step("vector", lambda e: e.memset(onesrow[:], 1.0))
    bvrow = p.sbuf([1, D], F32, "bvrow")
    p.dma("sync", bvrow[:], bqkv_d[2 * D:3 * D].unsqueeze(0))
    bvrow_b = p.sbuf([1, D], BF16, "bvrowb")
    p.step("vector", lambda e: e.tensor_copy(bvrow_b[:], bvrow[:]))
    for t in range(ntile):
        for eg in range(2):
            def mmv1(e, t=t, eg=eg):
                for dc in range(8):
                    e.matmul(mm_ps[:],
                             TC(xn1T, dc)[:, 128 * t:128 * (t + 1)],
                             W3C(dc)[:, 2 * D + 512 * eg:2 * D + 512 * (eg + 1)],
                             start=(dc == 0), stop=False)
                return e.matmul(mm_ps[:], onesrow[:],
                                bvrow_b[:, 512 * eg:512 * (eg + 1)],
                                start=False, stop=True)
            p.step("tensor", mmv1)
            p.step("vector", lambda e, eg=eg: e.tensor_copy(
                outqb[:, 512 * eg:512 * (eg + 1)], mm_ps[:]))
        p.dma("gpsimd", v1_d[128 * t:128 * (t + 1), :], outqb[:, 0:D])
    p.free(bvrow_b, bvrow, onesrow, outqb, bq_s, wrow3, wqkv_s)
    collective("AllGather", k1_d, gk1_d)
    collective("AllGather", v1_d, gv1_d)
    wg_s = p.sbuf([128, 8 * 8], F32, "wgs")
    for dc in range(8):
        p.dma("sync", wg_s[:, 8 * dc:8 * (dc + 1)],
              wgate_d[128 * dc:128 * (dc + 1), :])
    wg_hi = p.sbuf([128, 8 * 8], BF16, "wghi")
    wg_lo = p.sbuf([128, 8 * 8], BF16, "wglo")
    p.step("vector", lambda e: e.tensor_copy(wg_hi[:], wg_s[:]))
    p.step("vector", lambda e: e.tensor_sub(wg_lo[:], wg_s[:], wg_hi[:]))
    lg_ps = p.psum([8, 512], F32, "lgps")
    lgT = p.sbuf([8, T], F32, "lgT")
    for tg in range(T // 512):
        def mmlg(e, tg=tg):
            combos = []
            for dc in range(8):
                combos.append((wg_hi[:, 8 * dc:8 * (dc + 1)], TC(xn1T, dc)))
                combos.append((wg_lo[:, 8 * dc:8 * (dc + 1)], TC(xn1T, dc)))
                combos.append((wg_hi[:, 8 * dc:8 * (dc + 1)], TC(xn1loT, dc)))
            for i, (wb, xb) in enumerate(combos):
                inst = e.matmul(lg_ps[:], wb, xb[:, 512 * tg:512 * (tg + 1)],
                                start=(i == 0), stop=(i == len(combos) - 1))
            return inst
        p.step("tensor", mmlg)
        p.step("vector", lambda e, tg=tg: e.tensor_copy(
            lgT[:, 512 * tg:512 * (tg + 1)], lg_ps[:]))
    bg_s = p.sbuf([1, 8], F32, "bgs")
    p.dma("sync", bg_s[:], bg_d[:])
    ones1f = p.sbuf([1, 128], F32, "ones1f")
    p.step("vector", lambda e: e.memset(ones1f[:], 1.0))
    bgb_ps = p.psum([128, 8], F32, "bgb")
    bgb = p.sbuf([128, 8], F32, "bgbs")
    p.step("tensor", lambda e: e.matmul(bgb_ps[:], ones1f[:], bg_s[:],
                                        start=True, stop=True))
    p.step("vector", lambda e: e.tensor_copy(bgb[:], bgb_ps[:]))
    lgt_f = p.sbuf([128, 8], F32, "lgtf")
    mcol = p.sbuf([128, 1], F32, "mcol")
    for t in range(ntile):
        p.step("tensor", lambda e, t=t: e.transpose(
            tpf_ps[:, 0:8], lgT[:, 128 * t:128 * (t + 1)], identf[0:8, 0:8]))
        p.step("vector", lambda e: e.tensor_copy(lgt_f[:], tpf_ps[:, 0:8]))
        p.step("vector", lambda e: e.tensor_tensor(lgt_f[:], lgt_f[:], bgb[:],
                                                   ALU.add))
        p.step("vector", lambda e: e.tensor_reduce(mcol[:], lgt_f[:], AX.X,
                                                   ALU.max))
        p.step("vector", lambda e, t=t: e.tensor_scalar(
            selT[:, 8 * t:8 * (t + 1)], lgt_f[:], mcol[:], None, ALU.is_ge))
    p.free(mcol, lgt_f, bgb, ones1f, bg_s, lgT, wg_lo, wg_hi, wg_s, xn1loT)
    p.free(bgb_ps, lg_ps)
    yacc = p.sbuf([128, ntile * D], F32, "yacc")
    p.step("vector", lambda e: e.memset(yacc[:], 0.0))
    heT = p.sbuf([128, 8 * T], BF16, "heT")
    w1b = p.sbuf([128, 8 * D], BF16, "w1b")
    w2b = p.sbuf([128, 8 * D], BF16, "w2b")
    wrowm = p.sbuf([128, D], F32, "wrowm")
    b1c = p.sbuf([128, 8], F32, "b1c")
    b2row = p.sbuf([1, D], F32, "b2row")
    b2row_b = p.sbuf([1, D], BF16, "b2rowb")
    onesrow2 = p.sbuf([1, 128], BF16, "onesr2")
    p.step("vector", lambda e: e.memset(onesrow2[:], 1.0))
    ytmp = p.sbuf([128, 512], F32, "ytmp")
    for ex in range(8):
        for r in range(8):
            p.dma("sync", wrowm[:], w1_d[ex, 128 * r:128 * (r + 1), :])
            p.step("vector", lambda e, r=r: e.tensor_copy(
                w1b[:, D * r:D * (r + 1)], wrowm[:]))
        for r in range(8):
            p.dma("sync", wrowm[:], w2_d[ex, 128 * r:128 * (r + 1), :])
            p.step("vector", lambda e, r=r: e.tensor_copy(
                w2b[:, D * r:D * (r + 1)], wrowm[:]))
        for ic in range(8):
            p.dma("sync", b1c[:, ic:ic + 1],
                  b1_d[ex, 128 * ic:128 * (ic + 1)].unsqueeze(1))
        p.dma("sync", b2row[:], b2_d[ex:ex + 1, :])
        p.step("vector", lambda e: e.tensor_copy(b2row_b[:], b2row[:]))
        for ic in range(8):
            for tg in range(T // 512):
                def mmh(e, ic=ic, tg=tg):
                    for dc in range(8):
                        inst = e.matmul(
                            mm_ps[:],
                            w1b[:, D * dc + 128 * ic:D * dc + 128 * (ic + 1)],
                            TC(xn1T, dc)[:, 512 * tg:512 * (tg + 1)],
                            start=(dc == 0), stop=(dc == 7))
                    return inst
                p.step("tensor", mmh)
                p.step("scalar", lambda e, ic=ic, tg=tg: e.activation(
                    TC(heT, ic)[:, 512 * tg:512 * (tg + 1)], mm_ps[:],
                    AF.Gelu, bias=b1c[:, ic:ic + 1], scale=1.0))
        for t in range(ntile):
            for eg in range(2):
                def mmy(e, t=t, eg=eg):
                    for ic in range(8):
                        e.matmul(mm_ps[:],
                                 TC(heT, ic)[:, 128 * t:128 * (t + 1)],
                                 w2b[:, D * ic + 512 * eg:D * ic + 512 * (eg + 1)],
                                 start=(ic == 0), stop=False)
                    return e.matmul(mm_ps[:], onesrow2[:],
                                    b2row_b[:, 512 * eg:512 * (eg + 1)],
                                    start=False, stop=True)
                p.step("tensor", mmy)
                p.step("vector", lambda e, t=t, ex=ex: e.tensor_scalar(
                    ytmp[:], mm_ps[:], selT[:, 8 * t + ex:8 * t + ex + 1],
                    None, ALU.mult))
                p.step("vector", lambda e, t=t, eg=eg: e.tensor_tensor(
                    yacc[:, D * t + 512 * eg:D * t + 512 * (eg + 1)],
                    yacc[:, D * t + 512 * eg:D * t + 512 * (eg + 1)],
                    ytmp[:], ALU.add))
    g1b = p.sbuf([128, D], F32, "g1b")
    for t in range(ntile):
        p.step("scalar", lambda e, t=t: e.activation(
            g1b[:], yacc[:, D * t:D * (t + 1)], AF.Silu, bias=0.0, scale=1.0))
        p.dma("gpsimd", g1_d[128 * t:128 * (t + 1), :], g1b[:])
    p.free(g1b, ytmp, onesrow2, b2row_b, b2row, b1c, wrowm, w2b, w1b, heT,
           yacc, selT, xn1T)
    p.free(mm_ps, tp_ps)

    if MEGA_UPTO < 5:
        return _mega_stub_out(p, out_d), p.out_names
    emit_l1_attn_fused(p, q1_d, gk1_d, gv1_d, qt_d, ao1_d, khscr, identf,
                       taus_d)

    if MEGA_UPTO < 6:
        return _mega_stub_out(p, out_d), p.out_names
    mm_ps = p.psum([128, 512], F32, "mm6")
    ao_s = p.sbuf([128, 8 * T], BF16, "aos")
    for c in range(8):
        p.dma("sync", TC(ao_s, c), ao1_d[128 * c:128 * (c + 1), :])
    w_s = p.sbuf([128, 8 * D], BF16, "ws")
    wrow5 = p.sbuf([128, D], F32, "wr5")
    for r in range(8):
        p.dma("sync", wrow5[:], wout_d[128 * r:128 * (r + 1), :])
        p.step("vector", lambda e, r=r: e.tensor_copy(
            w_s[:, D * r:D * (r + 1)], wrow5[:]))
    bo_s = p.sbuf([128, 8], F32, "bos")
    for ec in range(8):
        p.dma("sync", bo_s[:, ec:ec + 1],
              bout_d[128 * ec:128 * (ec + 1)].unsqueeze(1))
    s1T = p.sbuf([128, 8 * T], F32, "s1T5")
    for ec in range(8):
        for tg in range(T // 512):
            def mmo(e, ec=ec, tg=tg):
                for dc in range(8):
                    inst = e.matmul(
                        mm_ps[:],
                        w_s[:, D * dc + 128 * ec:D * dc + 128 * (ec + 1)],
                        TC(ao_s, dc)[:, 512 * tg:512 * (tg + 1)],
                        start=(dc == 0), stop=(dc == 7))
                return inst
            p.step("tensor", mmo)
            p.step("scalar", lambda e, ec=ec, tg=tg: e.activation(
                TC(s1T, ec)[:, 512 * tg:512 * (tg + 1)], mm_ps[:], AF.Identity,
                bias=bo_s[:, ec:ec + 1], scale=1.0))
    x1f5 = p.sbuf([128, D], F32, "x1f5")
    g1f5 = p.sbuf([128, D], F32, "g1f5")
    s1f5 = p.sbuf([128, D], F32, "s1f5")
    of5 = p.sbuf([128, D], F32, "of5")
    for t in range(ntile):
        p.dma("sync", x1f5[:], x1_d[128 * t:128 * (t + 1), :])
        p.dma("sync", g1f5[:], g1_d[128 * t:128 * (t + 1), :])
        for kc in range(8):
            p.step("tensor", lambda e, kc=kc, t=t: e.transpose(
                tpf_ps[:], TC(s1T, kc)[:, 128 * t:128 * (t + 1)], identf[:]))
            p.step("vector", lambda e, kc=kc: e.tensor_copy(
                s1f5[:, 128 * kc:128 * (kc + 1)], tpf_ps[:]))
        p.step("vector", lambda e: e.tensor_tensor(of5[:], s1f5[:], g1f5[:],
                                                   ALU.mult))
        p.step("vector", lambda e: e.tensor_tensor(of5[:], of5[:], x1f5[:],
                                                   ALU.add))
        p.dma("gpsimd", out_d[128 * t:128 * (t + 1), :], of5[:])
    return p.finalize(), p.out_names


def emit_l1_attn_fused(p, q1_d, gk1_d, gv1_d, qt_d, ao_d, khscr, identf,
                       taus_d):
    QG = 512
    nqg = T // QG
    nkb = S // 128
    ones64f = p.sbuf([64, 1], F32, "o64")
    p.step("vector", lambda e: e.memset(ones64f[:], 1.0))
    ones1f64 = p.sbuf([1, 64], F32, "o1f")
    p.step("vector", lambda e: e.memset(ones1f64[:], 1.0))
    nc1c = p.sbuf([128, 1], F32, "nc1")
    p.step("vector", lambda e: e.memset(nc1c[:], -C1))
    p64c5 = p.sbuf([1, 1], F32, "p64c5")
    p.step("vector", lambda e: e.memset(p64c5[:], 44.3614195558365))
    qtrue = p.sbuf([128, T], F32, "qtrue")
    p.dma("sync", qtrue[:], qt_d[:])
    taus = p.sbuf([128, nkb], F32, "taus")
    p.dma("sync", taus[:], taus_d[:])
    QHs = [p.sbuf([66, T], BF16, f"QH{h}") for h in range(NH)]
    onesT = p.sbuf([1, T], BF16, "onesT")
    p.step("vector", lambda e: e.memset(onesT[:], 1.0))
    qsq = p.sbuf([64, T], F32, "qsq")
    qn_ps = p.psum([1, 512], F32, "qnps")
    for h in range(NH):
        p.dma("sync", QHs[h][0:64, :], q1_d[64 * h:64 * (h + 1), :])
        p.step("scalar", lambda e, h=h: e.activation(
            qsq[:], QHs[h][0:64, :], AF.Square, bias=0.0, scale=1.0))
        p.dma("gpsimd", QHs[h][65:66, :], onesT[:, 0:T])
        for tg in range(nqg):
            p.step("tensor", lambda e, tg=tg: e.matmul(
                qn_ps[:], ones64f[:], qsq[:, QG * tg:QG * (tg + 1)],
                start=True, stop=True))
            p.step("vector", lambda e, h=h, tg=tg: e.tensor_copy(
                QHs[h][64:65, QG * tg:QG * (tg + 1)], qn_ps[:]))
    p.free(qsq)
    KH = p.sbuf([66, S], BF16, "KH")
    ksq = p.sbuf([64, 512], F32, "ksq")
    kn_ps = qn_ps
    knrow = p.sbuf([1, 512], F32, "knrow")
    knrow_b = p.sbuf([1, S], BF16, "knrowb")
    maskb = p.sbuf([128, nkb * 512], BF16, "maskb")
    mtmp = p.sbuf([128, 512], F32, "mtmp")
    vbuf = p.sbuf([128, 65], BF16, "vbuf")
    p.step("vector", lambda e: e.memset(vbuf[:, 64:65], 1.0))
    den_ps = p.psum([128, 1024], F32, "denps")
    dot_ps = p.psum([128, 1024], F32, "dotps")
    o_ps = p.psum([65, QG], F32, "ops5")
    rlb_ps = p.psum([64, QG], F32, "rlbps")
    bln = p.sbuf([128, 1024], F32, "bln")
    rln = p.sbuf([128, 1024], F32, "rln")
    m1 = p.sbuf([128, 1024], F32, "m1")
    uu = p.sbuf([128, 1024], F32, "uu")
    pt = p.sbuf([128, 1024], BF16, "pt")
    ohl = p.sbuf([65, QG], F32, "ohl")
    rl = p.sbuf([1, QG], F32, "rl")
    aob = p.sbuf([64, QG], BF16, "aob")
    for qg in range(nqg):
        for kb in range(nkb):
            p.step("vector", lambda e, kb=kb, qg=qg: e.tensor_scalar(
                mtmp[:], qtrue[:, QG * qg:QG * (qg + 1)], taus[:, kb:kb + 1],
                None, ALU.is_ge))
            p.step("vector", lambda e, kb=kb: e.tensor_copy(
                maskb[:, 512 * kb:512 * (kb + 1)], mtmp[:]))
        for h in range(NH):
            if qg == 0:
                for r in range(4):
                    p.dma("sync", KH[0:64, T * r:T * (r + 1)],
                          gk1_d[D * r + 64 * h:D * r + 64 * (h + 1), :])
                p.step("vector", lambda e: e.memset(KH[64:65, :], 1.0))
                for sg in range(S // 512):
                    p.step("scalar", lambda e, sg=sg: e.activation(
                        ksq[:], KH[0:64, 512 * sg:512 * (sg + 1)], AF.Square,
                        bias=0.0, scale=0.5))
                    p.step("tensor", lambda e: e.matmul(
                        kn_ps[:], ones64f[:], ksq[:], start=True, stop=True))
                    p.step("vector", lambda e: e.tensor_scalar(
                        knrow[:], kn_ps[:], ATT_EPS, None, ALU.add))
                    p.step("vector", lambda e, sg=sg: e.tensor_copy(
                        knrow_b[:, 512 * sg:512 * (sg + 1)], knrow[:]))
                p.dma("gpsimd", KH[65:66, :], knrow_b[:])
                p.dma("gpsimd", khscr[h], KH[:])
            else:
                p.dma("sync", KH[:], khscr[h])
            for kbp in range(nkb // 2):
                def mms(e, kbp=kbp, qg=qg, h=h):
                    for half in range(2):
                        kb = 2 * kbp + half
                        e.matmul(den_ps[:, 512 * half:512 * (half + 1)],
                                 KH[0:66, 128 * kb:128 * (kb + 1)],
                                 QHs[h][0:66, QG * qg:QG * (qg + 1)],
                                 start=True, stop=True)
                        inst = e.matmul(dot_ps[:, 512 * half:512 * (half + 1)],
                                        KH[0:64, 128 * kb:128 * (kb + 1)],
                                        QHs[h][0:64, QG * qg:QG * (qg + 1)],
                                        start=True, stop=True)
                    return inst
                p.step("tensor", mms)

                def sln(e):
                    e.activation(bln[:], den_ps[:], AF.Ln, bias=0.0, scale=1.0)
                    return e.activation(rln[:], bln[:], AF.Exp, bias=0.0,
                                        scale=-1.0)
                p.step("scalar", sln)

                def vm(e):
                    e.tensor_tensor(m1[:], dot_ps[:], rln[:], ALU.mult)
                    return e.tensor_tensor(uu[:], m1[:], dot_ps[:], ALU.mult)
                p.step("vector", vm)
                p.step("scalar", lambda e: e.activation(
                    pt[:], uu[:], AF.Exp, bias=nc1c[:], scale=0.25))
                p.step("vector", lambda e, kbp=kbp: e.tensor_tensor(
                    pt[:], pt[:], maskb[:, 1024 * kbp:1024 * (kbp + 1)],
                    ALU.mult))
                for half in range(2):
                    kb = 2 * kbp + half
                    p.dma("sync", vbuf[:, 0:64],
                          gv1_d[128 * kb:128 * (kb + 1), 64 * h:64 * (h + 1)])
                    p.step("tensor", lambda e, kb=kb, half=half: e.matmul(
                        o_ps[:], vbuf[:], pt[:, 512 * half:512 * (half + 1)],
                        start=(kb == 0), stop=(kb == nkb - 1)))
            p.step("vector", lambda e: e.tensor_copy(ohl[:], o_ps[:]))
            def srl(e):
                e.activation(rl[:], ohl[64:65, :], AF.Ln, bias=0.0,
                             scale=18446744073709551616.0)
                return e.activation(rl[:], rl[:], AF.Exp, bias=p64c5[0:1, :],
                                    scale=-1.0)
            p.step("scalar", srl)
            p.step("tensor", lambda e: e.matmul(rlb_ps[:], ones1f64[:], rl[:],
                                                start=True, stop=True))
            p.step("vector", lambda e: e.tensor_tensor(
                aob[:], ohl[0:64, :], rlb_ps[:], ALU.mult))
            p.dma("gpsimd", ao_d[64 * h:64 * (h + 1), QG * qg:QG * (qg + 1)],
                  aob[:])
    p.free(aob, rl, ohl, pt, uu, m1, rln, bln, vbuf, mtmp, maskb, knrow_b,
           knrow, ksq, KH, onesT, *reversed(QHs), taus, qtrue, p64c5, nc1c,
           ones1f64, ones64f)
    p.free(rlb_ps, o_ps, dot_ps, den_ps, qn_ps)


_BUILD_CACHE = {}


def _get(name, builder):
    if name not in _BUILD_CACHE:
        _BUILD_CACHE[name] = builder()
    return _BUILD_CACHE[name]


import ml_dtypes as _mld
_EYEB = np.eye(128, dtype=_mld.bfloat16)
_EYEF = np.eye(128, dtype=np.float32)
_TAUS = (np.arange(128, dtype=np.float32)[:, None]
         + 128.0 * np.arange(64, dtype=np.float32)[None, :])


def kernel(**inputs):
    from concourse.aot_env import allow_ici_all_to_all
    f32 = np.float32
    C = np.ascontiguousarray
    x = np.asarray(inputs["L0_attn_q"]).reshape(16384, D)
    cores = list(range(NCORE))
    nc, _ = _get("mega", build_mega)
    maps = []
    for c in range(NCORE):
        t0 = T * (c % 4)
        qtrueb = np.broadcast_to(
            (t0 + np.arange(T, dtype=f32))[None, :], (128, T))
        selc = np.zeros((128, 4), np.float32)
        selc[:, c % 4] = 1.0
        maps.append({
            "x": C(x[T * c:T * (c + 1)]),
            "selc": selc,
            "wq": inputs["L0_attn_wq"], "wk": inputs["L0_attn_wk"],
            "wv": inputs["L0_attn_wv"],
            "wg": inputs["L0_ffn_w_gate"], "wu": inputs["L0_ffn_w_up"],
            "wd": inputs["L0_ffn_w_down"],
            "wo": inputs["L0_attn_wo"],
            "wqkv": inputs["L1_attn_w_qkv"], "bqkv": inputs["L1_attn_b_qkv"],
            "Wg": inputs["L1_ffn_Wg"],
            "bg": np.asarray(inputs["L1_ffn_bg"]).reshape(1, 8),
            "W1": inputs["L1_ffn_W1"], "b1": inputs["L1_ffn_b1"],
            "W2": inputs["L1_ffn_W2"], "b2": inputs["L1_ffn_b2"],
            "w_out": inputs["L1_attn_w_out"], "b_out": inputs["L1_attn_b_out"],
            "qtrueb": C(qtrueb),
            "eyeb": _EYEB, "eyef": _EYEF, "tausin": _TAUS,
        })
    with allow_ici_all_to_all():
        res = run_bass_kernel_spmd(nc, maps, core_ids=cores)
    out = np.concatenate([np.asarray(r["out"]) for r in res.results], axis=0)
    return out.reshape(2, S, D).astype(np.float32)
